# version 02
# baseline (speedup 1.0000x reference)
import math
import numpy as np
import ml_dtypes
import concourse.bass as bass
import concourse.mybir as mybir
from concourse.bass_utils import run_bass_kernel_spmd

F32 = mybir.dt.float32
BF16 = mybir.dt.bfloat16
I32 = mybir.dt.int32
AF = mybir.ActivationFunctionType
ALU = mybir.AluOpType

ENGINES = ("sync", "tensor", "vector", "scalar", "gpsimd")


DMA_POOL = 16


class Prog:

    def __init__(self, nc):
        self.nc = nc
        self.count = {e: 0 for e in ENGINES}
        self.dma_idx = {e: 0 for e in ENGINES}
        self.ops = {e: [] for e in ENGINES}
        self.writers = {}
        self.readers = {}
        self.hw = {e: {} for e in ENGINES}
        self.sems = {}
        self.sem_ids = set(ENGINES)
        self.pe_base = None

    def mm(self, t, out, lhsT, rhs, **kw):
        key = (lhsT.base_partition(), rhs.base_partition())
        if self.pe_base != key:
            t.drain()
        self.pe_base = key
        return t.matmul(out, lhsT, rhs, **kw)

    def emit(self, eng, fn, reads=(), writes=(), dma=False):
        deps = []
        for k in reads:
            w = self.writers.get(k)
            if w:
                deps.append(w)
        for k in writes:
            w = self.writers.get(k)
            if w:
                deps.append(w)
            for de, dv in self.readers.get(k, {}).items():
                deps.append((de, dv))
        if dma:
            idx = self.dma_idx[eng]
            self.dma_idx[eng] = idx + 1
            slot = idx % DMA_POOL
            cycle = idx // DMA_POOL
            sem_id = f"{eng}dma{slot}"
            self.sem_ids.add(sem_id)
            if cycle > 0:
                deps.append((sem_id, 16 * cycle))
            val = 16 * (cycle + 1)
            my_id = sem_id
        else:
            self.count[eng] += 1
            val = self.count[eng]
            my_id = eng
        waits = {}
        for de, dv in deps:
            if de == "tensor" and eng == "tensor":
                continue
            if dv > self.hw[eng].get(de, 0):
                waits[de] = max(waits.get(de, 0), dv)
        for de, dv in waits.items():
            self.hw[eng][de] = dv
        self.ops[eng].append((fn, waits, my_id, 16 if dma else 1))
        for k in reads:
            cur = self.readers.setdefault(k, {})
            cur[my_id] = max(cur.get(my_id, 0), val)
        for k in writes:
            self.writers[k] = (my_id, val)
            self.readers[k] = {}
        return my_id, val

    def barrier_key(self, key, eng="gpsimd"):
        w = self.writers.get(key)
        if w:
            de, dv = w
            if dv > self.hw[eng].get(de, 0):
                self.hw[eng][de] = dv
                self.ops[eng].append((None, {de: dv}, eng, 0))

    def finalize(self, join_eng="sync"):
        nc = self.nc
        final = dict(self.count)
        for e in ENGINES:
            n = self.dma_idx[e]
            for s in range(min(n, DMA_POOL)):
                ncyc = (n - s + DMA_POOL - 1) // DMA_POOL
                final[f"{e}dma{s}"] = 16 * ncyc
        ctxs = []
        opened = {}
        for sid in sorted(self.sem_ids):
            c = nc.semaphore(f"sem_{sid}")
            ctxs.append(c)
            opened[sid] = c.__enter__()
        self.sems = opened
        blk = nc.Block()
        block = blk.__enter__()
        for e in ENGINES:
            ops = self.ops[e]
            need_join = (e == join_eng)
            if not ops and not need_join:
                continue

            def body(engobj, e=e, ops=ops, need_join=need_join):
                for fn, waits, my_id, per in ops:
                    for de, dv in waits.items():
                        engobj.wait_ge(self.sems[de], dv)
                    if fn is not None:
                        instr = fn(engobj)
                        instr.then_inc(self.sems[my_id], per)
                if need_join:
                    for sid, fv in final.items():
                        if sid != e and fv > self.hw[e].get(sid, 0):
                            engobj.wait_ge(self.sems[sid], fv)

            getattr(block, e)(body)
        blk.__exit__(None, None, None)
        for c in ctxs:
            c.__exit__(None, None, None)




def ceil_div(a, b):
    return (a + b - 1) // b


def mm_acc(p, psum_ap, pairs, writes, reads=(), accum=False):
    def fn(t):
        last = None
        npairs = len(pairs)
        for i, (l, r) in enumerate(pairs):
            last = p.mm(t, psum_ap, l, r,
                        start=(i == 0 and not accum),
                        stop=(i == npairs - 1))
        return last
    return p.emit("tensor", fn, reads=reads, writes=writes)


def bcast_row(p, psum_ap, ones_col, row_ap, reads, writes, accum=False):
    return mm_acc(p, psum_ap, [(ones_col, row_ap)], writes=writes, reads=reads,
                  accum=accum)


def np_bf16(x):
    return np.asarray(x).astype(ml_dtypes.bfloat16)


"""Per-launch builders for the mk_seq_postln_d2 module (v0 multi-launch)."""

T = 2048
D = 1024
NH = 16
HD = 64
CH = 128
NCH = T // CH



def l0_consts():
    g = 1.0 - 2.0 ** (-5.0 - np.arange(NH, dtype=np.float64))
    tp = np.arange(CH)
    M = np.zeros((CH, NH * CH), np.float32)
    for h in range(NH):
        diff = tp[None, :] - tp[:, None]
        blk = np.where(diff >= 0, g[h] ** np.maximum(diff, 0), 0.0)
        M[:, h * CH:(h + 1) * CH] = blk.astype(np.float32)
    E = np.zeros((CH, D), np.float32)
    for h in range(NH):
        E[:, h * HD:(h + 1) * HD] = (g[h] ** (CH - 1 - tp))[:, None]
    G = np.zeros((128, 512), np.float32)
    for h in range(NH):
        par, hp = h % 2, h // 2
        G[64 * par:64 * (par + 1), HD * hp:HD * (hp + 1)] = g[h] ** CH
    Dg = np.zeros((CH, D), np.float32)
    for h in range(NH):
        Dg[:, h * HD:(h + 1) * HD] = (g[h] ** (tp + 1))[:, None]
    return M, E, G, Dg



def build_l0_qkv(nc):
    p = Prog(nc)
    xT = nc.dram_tensor("xT", [D, T], F32, kind="ExternalInput").ap()
    w = nc.dram_tensor("w", [D, 3 * D], F32, kind="ExternalInput").ap()
    outs = [nc.dram_tensor(nm, [D, T], BF16, kind="ExternalOutput").ap()
            for nm in ("q_fm", "k_fm", "v_fm")]

    xfm = nc.alloc_sbuf_tensor("xfm", [128, 8 * T], BF16)
    ws = nc.alloc_sbuf_tensor("ws", [128, 8 * 3 * D], BF16)
    osb = [nc.alloc_sbuf_tensor(f"osb{j}", [128, T], BF16) for j in range(2)]
    ps = [nc.alloc_psum_tensor(f"ps{j}", [128, T], F32) for j in range(2)]

    for kc in range(8):
        p.emit("gpsimd", lambda g, kc=kc: g.dma_start(
            out=xfm[:, T * kc:T * (kc + 1)], in_=xT[128 * kc:128 * (kc + 1), :]),
            writes=[f"xfm{kc}"], dma=True)
        p.emit("gpsimd", lambda g, kc=kc: g.dma_start(
            out=ws[:, 3 * D * kc:3 * D * (kc + 1)], in_=w[128 * kc:128 * (kc + 1), :]),
            writes=[f"ws{kc}"], dma=True)

    NT = T // 512
    for ec in range(24):
        j = ec % 2
        for tt in range(NT):
            pairs = [(ws[:, 3 * D * kc + 128 * ec: 3 * D * kc + 128 * (ec + 1)],
                      xfm[:, T * kc + 512 * tt: T * kc + 512 * (tt + 1)])
                     for kc in range(8)]
            mm_acc(p, ps[j][:, 512 * tt:512 * (tt + 1)], pairs,
                   reads=[f"ws{kc}" for kc in range(8)] + [f"xfm{kc}" for kc in range(8)],
                   writes=[f"ps{j}"])
        eng = "vector" if j == 0 else "scalar"
        if eng == "vector":
            p.emit(eng, lambda v, j=j: v.tensor_copy(osb[j][:, :], ps[j][:, :]),
                   reads=[f"ps{j}"], writes=[f"osb{j}"])
        else:
            p.emit(eng, lambda s, j=j: s.copy(osb[j][:, :], ps[j][:, :]),
                   reads=[f"ps{j}"], writes=[f"osb{j}"])
        o = outs[ec // 8]
        r0 = 128 * (ec % 8)
        p.emit("gpsimd", lambda g, j=j, o=o, r0=r0: g.dma_start(
            out=o[r0:r0 + 128, :], in_=osb[j][:, :]),
            reads=[f"osb{j}"], writes=[f"out{ec}"], dma=True)
    p.finalize()
    return nc



def build_l0_ret(nc):
    p = Prog(nc)
    qf = nc.dram_tensor("q_fm", [D, T], BF16, kind="ExternalInput").ap()
    kf = nc.dram_tensor("k_fm", [D, T], BF16, kind="ExternalInput").ap()
    vt = nc.dram_tensor("v_tm", [T, D], BF16, kind="ExternalInput").ap()
    kt = nc.dram_tensor("k_tm", [T, D], BF16, kind="ExternalInput").ap()
    pkv = nc.dram_tensor("pkv", [NH, HD, HD], F32, kind="ExternalInput").ap()
    cM = nc.dram_tensor("cM", [CH, NH * CH], F32, kind="ExternalInput").ap()
    cE = nc.dram_tensor("cE", [CH, D], F32, kind="ExternalInput").ap()
    cG = nc.dram_tensor("cG", [128, 512], F32, kind="ExternalInput").ap()
    cDg = nc.dram_tensor("cDg", [CH, D], F32, kind="ExternalInput").ap()
    ar = nc.dram_tensor("attnraw", [T, D], F32, kind="ExternalOutput").ap()

    qs = nc.alloc_sbuf_tensor("qs", [128, 8 * T], BF16)
    ks = nc.alloc_sbuf_tensor("ks", [128, 8 * T], BF16)
    vts = nc.alloc_sbuf_tensor("vts", [128, NCH * D], BF16)
    kts = nc.alloc_sbuf_tensor("kts", [128, NCH * D], BF16)
    pkvs = nc.alloc_sbuf_tensor("pkvs", [128, 512], F32)
    pkvb = nc.alloc_sbuf_tensor("pkvb", [128, 512], BF16)
    Ms = nc.alloc_sbuf_tensor("Ms", [CH, NH * CH], F32)
    Es = nc.alloc_sbuf_tensor("Es", [CH, D], F32)
    Gs = nc.alloc_sbuf_tensor("Gs", [128, 512], F32)
    Dgs = nc.alloc_sbuf_tensor("Dgs", [CH, D], F32)
    state = nc.alloc_sbuf_tensor("state", [128, 512], F32)
    stateb = nc.alloc_sbuf_tensor("stateb", [128, 512], BF16)
    kdecb = nc.alloc_sbuf_tensor("kdecb", [128, D], BF16)
    Sm = nc.alloc_sbuf_tensor("Sm", [128, NH * CH], BF16)
    scr1 = nc.alloc_sbuf_tensor("scr1", [128, D], F32)
    scr2 = nc.alloc_sbuf_tensor("scr2", [128, 512], F32)
    artile = nc.alloc_sbuf_tensor("artile", [128, D], F32)

    ps_S = nc.alloc_psum_tensor("ps_S", [128, 1024], F32)
    ps_in = nc.alloc_psum_tensor("ps_in", [128, 1024], F32)
    ps_cr = nc.alloc_psum_tensor("ps_cr", [128, 1024], F32)
    ps_su = nc.alloc_psum_tensor("ps_su", [128, 512], F32)

    for kc in range(8):
        p.emit("sync", lambda s, kc=kc: s.dma_start(
            out=qs[:, T * kc:T * (kc + 1)], in_=qf[128 * kc:128 * (kc + 1), :]),
            writes=[f"qs{kc}"], dma=True)
        p.emit("sync", lambda s, kc=kc: s.dma_start(
            out=ks[:, T * kc:T * (kc + 1)], in_=kf[128 * kc:128 * (kc + 1), :]),
            writes=[f"ks{kc}"], dma=True)
    for c in range(NCH):
        p.emit("sync", lambda s, c=c: s.dma_start(
            out=vts[:, D * c:D * (c + 1)], in_=vt[CH * c:CH * (c + 1), :]),
            writes=[f"vts{c}"], dma=True)
        p.emit("sync", lambda s, c=c: s.dma_start(
            out=kts[:, D * c:D * (c + 1)], in_=kt[CH * c:CH * (c + 1), :]),
            writes=[f"kts{c}"], dma=True)
    for par in range(2):
        p.emit("sync", lambda s, par=par: s.dma_start(
            out=pkvs[64 * par:64 * (par + 1), :].rearrange("p (h e) -> p h e", h=NH // 2),
            in_=pkv.rearrange("(hp par) d e -> par d hp e", par=2)[par]),
            writes=["pkvs"], dma=True)
    p.emit("sync", lambda s: s.dma_start(out=Ms[:, :], in_=cM[:, :]), writes=["Ms"], dma=True)
    p.emit("sync", lambda s: s.dma_start(out=Es[:, :], in_=cE[:, :]), writes=["Es"], dma=True)
    p.emit("sync", lambda s: s.dma_start(out=Gs[:, :], in_=cG[:, :]), writes=["Gs"], dma=True)
    p.emit("sync", lambda s: s.dma_start(out=Dgs[:, :], in_=cDg[:, :]), writes=["Dgs"], dma=True)
    p.emit("vector", lambda v: v.tensor_scalar_mul(pkvb[:, :], pkvs[:, :], float(NH * HD)),
           reads=["pkvs"], writes=["pkvb"])
    p.emit("vector", lambda v: v.memset(state[:, :], 0.0), writes=["state"])

    def hslice(buf, h, c):
        ec = h // 2
        r0 = 64 * (h % 2)
        return buf[r0:r0 + 64, T * ec + CH * c: T * ec + CH * (c + 1)]

    for c in range(NCH):
        p.emit("vector", lambda v, c=c: v.tensor_tensor(
            kdecb[:, :], kts[:, D * c:D * (c + 1)], Es[:, :], ALU.mult),
            reads=[f"kts{c}", "Es"], writes=["kdecb"])
        p.emit("scalar", lambda s: s.copy(stateb[:, :], state[:, :]),
            reads=["state"], writes=["stateb"])

        for half in range(2):
            def smm(t, c=c, half=half):
                last = None
                for hh in range(8):
                    h = 8 * half + hh
                    last = p.mm(t, ps_S[:, CH * hh:CH * (hh + 1)],
                                    hslice(ks, h, c), hslice(qs, h, c),
                                    start=True, stop=True)
                return last
            p.emit("tensor", smm,
                   reads=[f"ks{kc}" for kc in range(8)] + [f"qs{kc}" for kc in range(8)],
                   writes=["ps_S"])
            p.emit("vector", lambda v, c=c, half=half: v.tensor_tensor(
                Sm[:, 1024 * half:1024 * (half + 1)], ps_S[:, :],
                Ms[:, 1024 * half:1024 * (half + 1)], ALU.mult),
                reads=["ps_S", "Ms"], writes=["Sm"])

        def imm(t, c=c):
            last = None
            for h in range(NH):
                last = p.mm(t, ps_in[:, HD * h:HD * (h + 1)],
                                Sm[:, CH * h:CH * (h + 1)],
                                vts[:, D * c + HD * h: D * c + HD * (h + 1)],
                                start=True, stop=False)
                last = p.mm(t, ps_in[:, HD * h:HD * (h + 1)],
                                hslice(qs, h, c),
                                pkvb[64 * (h % 2):64 * (h % 2) + 64, HD * (h // 2):HD * (h // 2) + 64],
                                start=False, stop=True)
            return last
        p.emit("tensor", imm,
               reads=["Sm", f"vts{c}", "pkvb"] + [f"qs{kc}" for kc in range(8)],
               writes=["ps_in"])

        def cmm(t, c=c):
            last = None
            for h in range(NH):
                last = p.mm(t, ps_cr[:, HD * h:HD * (h + 1)],
                                hslice(qs, h, c),
                                stateb[64 * (h % 2):64 * (h % 2) + 64, HD * (h // 2):HD * (h // 2) + 64],
                                start=True, stop=True)
            return last
        p.emit("tensor", cmm,
               reads=["stateb"] + [f"qs{kc}" for kc in range(8)], writes=["ps_cr"])

        def umm(t, c=c):
            last = None
            for h in range(NH):
                last = p.mm(t, ps_su[64 * (h % 2):64 * (h % 2) + 64,
                                      HD * (h // 2):HD * (h // 2) + 64],
                                kdecb[:, HD * h:HD * (h + 1)],
                                vts[:, D * c + HD * h: D * c + HD * (h + 1)],
                                start=True, stop=True,
                                tile_position=(0, 64 * (h % 2)))
            return last
        p.emit("tensor", umm, reads=["kdecb", f"vts{c}"], writes=["ps_su"])

        p.emit("vector", lambda v: v.tensor_tensor(scr1[:, :], ps_cr[:, :], Dgs[:, :], ALU.mult),
               reads=["ps_cr", "Dgs"], writes=["scr1"])
        p.emit("vector", lambda v: v.tensor_tensor(artile[:, :], scr1[:, :], ps_in[:, :], ALU.add),
               reads=["scr1", "ps_in"], writes=["artile"])
        p.emit("scalar", lambda s, c=c: s.dma_start(
            out=ar[CH * c:CH * (c + 1), :], in_=artile[:, :]),
            reads=["artile"], writes=[f"ar{c}"], dma=True)

        p.emit("vector", lambda v: v.tensor_tensor(scr2[:, :], state[:, :], Gs[:, :], ALU.mult),
               reads=["state", "Gs"], writes=["scr2"])
        p.emit("vector", lambda v: v.tensor_tensor(state[:, :], scr2[:, :], ps_su[:, :], ALU.add),
               reads=["scr2", "ps_su"], writes=["state"])
    p.finalize()
    return nc



def build_l0_gn(nc):
    p = Prog(nc)
    x = nc.dram_tensor("x", [T, D], F32, kind="ExternalInput").ap()
    ar = nc.dram_tensor("attnraw", [T, D], F32, kind="ExternalInput").ap()
    gnw = nc.dram_tensor("gnw", [NH], F32, kind="ExternalInput").ap()
    gnb = nc.dram_tensor("gnb", [NH], F32, kind="ExternalInput").ap()
    xo = nc.dram_tensor("x1", [T, D], F32, kind="ExternalOutput").ap()

    xs = nc.alloc_sbuf_tensor("xs", [128, D], F32)
    ars = nc.alloc_sbuf_tensor("ars", [128, NCH * D], F32)
    gnws = nc.alloc_sbuf_tensor("gnws", [1, NH], F32)
    gnbs = nc.alloc_sbuf_tensor("gnbs", [1, NH], F32)
    onescol = nc.alloc_sbuf_tensor("onescol", [128, 1], F32)
    onesrow = nc.alloc_sbuf_tensor("onesrow", [1, 128], BF16)
    colsum = nc.alloc_sbuf_tensor("colsum", [128, NCH], F32)
    colsq = nc.alloc_sbuf_tensor("colsq", [128, NCH], F32)
    sqscr = nc.alloc_sbuf_tensor("sqscr", [128, D], F32)
    redr = nc.alloc_sbuf_tensor("redr", [1, 2 * NCH], F32)
    stat = nc.alloc_sbuf_tensor("stat", [1, 8], F32)
    arow = nc.alloc_sbuf_tensor("arow", [1, D], F32)
    brow = nc.alloc_sbuf_tensor("brow", [1, D], F32)
    rowsb = nc.alloc_sbuf_tensor("rowsb", [1, 2 * D], BF16)
    zt = nc.alloc_sbuf_tensor("zt", [128, D], F32)
    z2 = nc.alloc_sbuf_tensor("z2", [128, D], F32)
    mu = nc.alloc_sbuf_tensor("mu", [128, 1], F32)
    var = nc.alloc_sbuf_tensor("var", [128, 1], F32)
    sq2 = nc.alloc_sbuf_tensor("sq2", [128, 1], F32)
    rstd = nc.alloc_sbuf_tensor("rstd", [128, 1], F32)
    ot = nc.alloc_sbuf_tensor("ot", [128, D], F32)

    epst = nc.alloc_sbuf_tensor("epst", [128, 1], F32)
    nhalf = nc.alloc_sbuf_tensor("nhalf", [128, 1], F32)
    ps_red = nc.alloc_psum_tensor("ps_red", [128, 512], F32)
    ps_ab = nc.alloc_psum_tensor("ps_ab", [128, 2 * D], F32)

    p.emit("sync", lambda s: s.dma_start(out=gnws[:, :], in_=gnw[None, :]), writes=["gnws"], dma=True)
    p.emit("sync", lambda s: s.dma_start(out=gnbs[:, :], in_=gnb[None, :]), writes=["gnbs"], dma=True)
    p.emit("vector", lambda v: v.memset(onescol[:, :], 1.0), writes=["onescol"])
    p.emit("vector", lambda v: v.memset(epst[:, :], 1e-5), writes=["epst"])
    p.emit("vector", lambda v: v.memset(nhalf[:, :], -0.5), writes=["nhalf"])
    p.emit("vector", lambda v: v.memset(onesrow[:, :], 1.0), writes=["onesrow"])
    for c in range(NCH):
        p.emit("sync", lambda s, c=c: s.dma_start(
            out=ars[:, D * c:D * (c + 1)], in_=ar[CH * c:CH * (c + 1), :]),
            writes=[f"ars{c}"], dma=True)
    for c in range(NCH):
        p.emit("vector", lambda v, c=c: v.reduce_sum(
            colsum[:, c:c + 1], ars[:, D * c:D * (c + 1)], axis=mybir.AxisListType.X),
            reads=[f"ars{c}"], writes=["colsum"])
        p.emit("scalar", lambda s, c=c: s.activation(
            out=sqscr[:, :], in_=ars[:, D * c:D * (c + 1)], func=AF.Square,
            accum_out=colsq[:, c:c + 1]),
            reads=[f"ars{c}"], writes=["colsq", "sqscr"])
    def rmm(t):
        p.mm(t, ps_red[0:1, 0:NCH], onescol[:, :], colsum[:, :], start=True, stop=True)
        return p.mm(t, ps_red[0:1, NCH:2 * NCH], onescol[:, :], colsq[:, :], start=True, stop=True)
    p.emit("tensor", rmm, reads=["colsum", "colsq", "onescol"], writes=["ps_red"])
    p.emit("vector", lambda v: v.tensor_copy(redr[:, :], ps_red[0:1, 0:2 * NCH]),
           reads=["ps_red"], writes=["redr"])
    p.emit("vector", lambda v: v.reduce_sum(stat[:, 0:1], redr[:, 0:NCH], axis=mybir.AxisListType.X),
           reads=["redr"], writes=["stat_s"])
    p.emit("vector", lambda v: v.reduce_sum(stat[:, 1:2], redr[:, NCH:2 * NCH], axis=mybir.AxisListType.X),
           reads=["redr"], writes=["stat_q"])
    NTOT = float(T * D)
    p.emit("vector", lambda v: v.tensor_scalar_mul(stat[:, 2:3], stat[:, 0:1], 1.0 / NTOT),
           reads=["stat_s"], writes=["stat_mean"])
    p.emit("vector", lambda v: v.tensor_scalar_mul(stat[:, 3:4], stat[:, 1:2], 1.0 / NTOT),
           reads=["stat_q"], writes=["stat_e2"])
    p.emit("vector", lambda v: v.tensor_tensor(stat[:, 4:5], stat[:, 2:3], stat[:, 2:3], ALU.mult),
           reads=["stat_mean"], writes=["stat_m2"])
    p.emit("vector", lambda v: v.tensor_tensor(stat[:, 5:6], stat[:, 3:4], stat[:, 4:5], ALU.subtract),
           reads=["stat_e2", "stat_m2"], writes=["stat_var"])
    p.emit("scalar", lambda s: s.activation(out=stat[:, 6:7], in_=stat[:, 5:6], func=AF.Ln,
                                            bias=epst[0:1, 0:1]),
           reads=["stat_var", "epst"], writes=["stat_sd"])
    p.emit("scalar", lambda s: s.activation(out=stat[:, 7:8], in_=stat[:, 6:7], func=AF.Exp,
                                            scale=nhalf[0:1, 0:1]),
           reads=["stat_sd", "nhalf"], writes=["stat_rstd"])
    def gn_bcast(buf):
        return buf[:, :, None].to_broadcast([1, NH, HD])
    p.emit("vector", lambda v: v.tensor_single_scalar(
        arow[:, :].rearrange("p (h u) -> p h u", h=NH), gn_bcast(gnws), stat[0:1, 7:8], ALU.mult),
        reads=["gnws", "stat_rstd"], writes=["arow"])
    p.emit("vector", lambda v: v.tensor_scalar_mul(stat[:, 2:3], stat[:, 2:3], -1.0),
           reads=["stat_mean"], writes=["stat_negmu"])
    p.emit("vector", lambda v: v.scalar_tensor_tensor(
        brow[:, :].rearrange("p (h u) -> p h u", h=NH), arow[:, :].rearrange("p (h u) -> p h u", h=NH),
        stat[0:1, 2:3], gn_bcast(gnbs), ALU.mult, ALU.add),
        reads=["arow", "stat_negmu", "gnbs"], writes=["brow"])
    p.emit("vector", lambda v: v.tensor_copy(rowsb[:, 0:D], arow[:, :]),
           reads=["arow"], writes=["rowsb"])
    p.emit("vector", lambda v: v.tensor_copy(rowsb[:, D:2 * D], brow[:, :]),
           reads=["brow"], writes=["rowsb2"])
    def abmm(t):
        last = None
        for i in range(4):
            last = p.mm(t, ps_ab[:, 512 * i:512 * (i + 1)], onesrow[:, :],
                            rowsb[0:1, 512 * i:512 * (i + 1)], start=True, stop=True)
        return last
    p.emit("tensor", abmm, reads=["rowsb", "rowsb2", "onesrow"], writes=["ps_ab"])

    for c in range(NCH):
        p.emit("sync", lambda s, c=c: s.dma_start(
            out=xs[:, :], in_=x[CH * c:CH * (c + 1), :]),
            writes=["xs"], dma=True)
        p.emit("vector", lambda v, c=c: v.tensor_tensor(
            zt[:, :], ars[:, D * c:D * (c + 1)], ps_ab[:, 0:D], ALU.mult),
            reads=[f"ars{c}", "ps_ab"], writes=["zt"])
        p.emit("vector", lambda v: v.tensor_tensor(zt[:, :], zt[:, :], ps_ab[:, D:2 * D], ALU.add),
               reads=["ps_ab"], writes=["zt"])
        p.emit("vector", lambda v: v.tensor_tensor(z2[:, :], zt[:, :], xs[:, :], ALU.add),
               reads=["zt", "xs"], writes=["z2"])
        p.emit("vector", lambda v: v.reduce_sum(mu[:, :], z2[:, :], axis=mybir.AxisListType.X),
               reads=["z2"], writes=["mu"])
        p.emit("vector", lambda v: v.tensor_scalar_mul(mu[:, :], mu[:, :], 1.0 / D),
               writes=["mu"])
        p.emit("scalar", lambda s: s.activation(out=sqscr[:, :], in_=z2[:, :], func=AF.Square,
                                                accum_out=sq2[:, :]),
               reads=["z2"], writes=["sq2", "sqscr"])
        p.emit("vector", lambda v: v.tensor_scalar_mul(sq2[:, :], sq2[:, :], 1.0 / D),
               reads=["sq2"], writes=["sq2b"])
        p.emit("vector", lambda v: v.tensor_tensor(var[:, :], mu[:, :], mu[:, :], ALU.mult),
               reads=["mu"], writes=["var"])
        p.emit("vector", lambda v: v.tensor_tensor(var[:, :], sq2[:, :], var[:, :], ALU.subtract),
               reads=["sq2b"], writes=["var"])
        p.emit("scalar", lambda s: s.activation(out=rstd[:, :], in_=var[:, :], func=AF.Ln,
                                                bias=epst[:, 0:1]),
               reads=["var", "epst"], writes=["rstd"])
        p.emit("scalar", lambda s: s.activation(out=rstd[:, :], in_=rstd[:, :], func=AF.Exp,
                                                scale=nhalf[:, 0:1]),
               reads=["rstd", "nhalf"], writes=["rstd"])
        p.emit("vector", lambda v: v.tensor_scalar(
            ot[:, :], z2[:, :], mu[:, 0:1], rstd[:, 0:1], ALU.subtract, ALU.mult),
            reads=["z2", "mu", "rstd"], writes=["ot"])
        p.emit("gpsimd", lambda g, c=c: g.dma_start(
            out=xo[CH * c:CH * (c + 1), :], in_=ot[:, :]),
            reads=["ot"], writes=[f"xo{c}"], dma=True)
    p.finalize()
    return nc



def build_quant(nc, name, R, C):
    p = Prog(nc)
    wt = nc.dram_tensor(name, [R, C], F32, kind="ExternalInput").ap()
    wq = nc.dram_tensor(name + "_q", [R, C], BF16, kind="ExternalOutput").ap()
    NR = R // 128
    tiles = [nc.alloc_sbuf_tensor(f"wtile{j}", [128, C], F32) for j in range(4)]
    qt = [nc.alloc_sbuf_tensor(f"qtile{j}", [128, C], BF16) for j in range(2)]
    at = [nc.alloc_sbuf_tensor(f"at{j}", [128, C], F32) for j in range(2)]
    bt = [nc.alloc_sbuf_tensor(f"bt{j}", [128, C], F32) for j in range(2)]
    cols = nc.alloc_sbuf_tensor("cols", [128, NR], F32)
    onescol = nc.alloc_sbuf_tensor("onescol", [128, 1], F32)
    onesrow = nc.alloc_sbuf_tensor("onesrow", [1, 128], F32)
    red = nc.alloc_sbuf_tensor("red", [1, NR], F32)
    gam = nc.alloc_sbuf_tensor("gam", [1, 4], F32)
    thr = nc.alloc_sbuf_tensor("thr", [128, 2], F32)
    ps_r = nc.alloc_psum_tensor("ps_r", [128, 512], F32)

    p.emit("vector", lambda v: v.memset(onescol[:, :], 1.0), writes=["onescol"])
    p.emit("vector", lambda v: v.memset(onesrow[:, :], 1.0), writes=["onesrow"])
    for r in range(NR):
        j = r % 4
        p.emit("sync", lambda s, r=r, j=j: s.dma_start(
            out=tiles[j][:, :], in_=wt[128 * r:128 * (r + 1), :]),
            writes=[f"wt{j}"], dma=True)
        p.emit("vector", lambda v, r=r, j=j: v.reduce_sum(
            cols[:, r:r + 1], tiles[j][:, :], axis=mybir.AxisListType.X,
            apply_absolute_value=True),
            reads=[f"wt{j}"], writes=[f"cols{r}"])
    p.emit("tensor", lambda t: p.mm(t, ps_r[0:1, 0:NR], onescol[:, :], cols[:, :],
                                    start=True, stop=True),
           reads=[f"cols{r}" for r in range(NR)] + ["onescol"], writes=["ps_r"])
    p.emit("vector", lambda v: v.tensor_copy(red[:, :], ps_r[0:1, 0:NR]),
           reads=["ps_r"], writes=["red"])
    p.emit("vector", lambda v: v.reduce_sum(gam[:, 0:1], red[:, :], axis=mybir.AxisListType.X),
           reads=["red"], writes=["gam0"])
    p.emit("vector", lambda v: v.tensor_scalar(
        gam[:, 1:2], gam[:, 0:1], 0.5 / float(R * C), 5e-7, ALU.mult, ALU.add),
        reads=["gam0"], writes=["gam1"])
    def thmm(t):
        return p.mm(t, ps_r[:, 0:1], onesrow[:, :], gam[0:1, 1:2], start=True, stop=True)
    p.emit("tensor", thmm, reads=["gam1", "onesrow"], writes=["ps_r"])
    p.emit("vector", lambda v: v.tensor_copy(thr[:, 0:1], ps_r[:, 0:1]),
           reads=["ps_r"], writes=["thr"])
    p.emit("vector", lambda v: v.tensor_scalar_mul(thr[:, 1:2], thr[:, 0:1], -1.0),
           reads=["thr"], writes=["thrn"])
    for r in range(NR):
        j = r % 4
        j2 = r % 2
        p.emit("sync", lambda s, r=r, j=j: s.dma_start(
            out=tiles[j][:, :], in_=wt[128 * r:128 * (r + 1), :]),
            writes=[f"wt{j}"], dma=True)
        p.emit("vector", lambda v, j=j, j2=j2: v.tensor_single_scalar(
            at[j2][:, :], tiles[j][:, :], thr[:, 0:1], ALU.is_gt),
            reads=[f"wt{j}", "thr"], writes=[f"at{j2}"])
        p.emit("gpsimd", lambda g, j=j, j2=j2: g.tensor_single_scalar(
            bt[j2][:, :], tiles[j][:, :], thr[:, 1:2], ALU.is_lt),
            reads=[f"wt{j}", "thrn"], writes=[f"bt{j2}"])
        p.emit("vector", lambda v, j2=j2: v.tensor_tensor(
            qt[j2][:, :], at[j2][:, :], bt[j2][:, :], ALU.subtract),
            reads=[f"at{j2}", f"bt{j2}"], writes=[f"qt{j2}"])
        p.emit("scalar", lambda s, r=r, j2=j2: s.dma_start(
            out=wq[128 * r:128 * (r + 1), :], in_=qt[j2][:, :]),
            reads=[f"qt{j2}"], writes=[f"wq{r}"], dma=True)
    p.finalize()
    return nc



def build_l0_ffn(nc, stage=9):
    p = Prog(nc)
    x1T = nc.dram_tensor("x1T", [D, T], F32, kind="ExternalInput").ap()
    wfc = nc.dram_tensor("wfcT_q", [D, 4 * D], BF16, kind="ExternalInput").ap()
    wpr = nc.dram_tensor("wprT_q", [4 * D, D], BF16, kind="ExternalInput").ap()
    lnw = nc.dram_tensor("lnw", [4 * D], F32, kind="ExternalInput").ap()
    lnb = nc.dram_tensor("lnb", [4 * D], F32, kind="ExternalInput").ap()
    xo = nc.dram_tensor("x2fm", [D, T], F32, kind="ExternalOutput").ap()

    TT_ = 256
    NTT = T // TT_
    wfcs = nc.alloc_sbuf_tensor("wfcs", [128, 8 * 4 * D], BF16)
    wprs = nc.alloc_sbuf_tensor("wprs", [128, 32 * D], BF16)
    x1f = nc.alloc_sbuf_tensor("x1f", [128, 8 * TT_], F32)
    x1b = nc.alloc_sbuf_tensor("x1b", [128, 8 * TT_], BF16)
    hb = nc.alloc_sbuf_tensor("hb", [128, 32 * TT_], BF16)
    sqs = [nc.alloc_sbuf_tensor(f"sqs{j}", [128, TT_], F32) for j in range(2)]
    lnwc = nc.alloc_sbuf_tensor("lnwc", [128, 64], F32)
    row = nc.alloc_sbuf_tensor("row", [1, 6 * TT_], F32)
    zfm = nc.alloc_sbuf_tensor("zfm", [128, 8 * TT_], F32)
    ofm = [nc.alloc_sbuf_tensor(f"ofm_{j}", [128, TT_], F32) for j in range(2)]
    onescol = nc.alloc_sbuf_tensor("onescol", [128, 1], F32)
    onescolb = nc.alloc_sbuf_tensor("onescolb", [128, 1], BF16)
    onesrow = nc.alloc_sbuf_tensor("onesrow", [1, 128], F32)
    ident = nc.alloc_sbuf_tensor("ident", [128, 128], F32)
    lnrows = nc.alloc_sbuf_tensor("lnrows", [32, 256], F32)
    epst = nc.alloc_sbuf_tensor("epst", [128, 1], F32)
    nhalf = nc.alloc_sbuf_tensor("nhalf", [128, 1], F32)
    scr = [nc.alloc_sbuf_tensor(f"scr_{j}", [128, TT_], F32) for j in range(2)]
    scr2 = [nc.alloc_sbuf_tensor(f"scr2_{j}", [128, TT_], F32) for j in range(2)]

    ps_h = [nc.alloc_psum_tensor(f"ps_h{j}", [128, TT_], F32) for j in range(2)]
    ps_o = [nc.alloc_psum_tensor(f"ps_o{j}", [128, TT_], F32) for j in range(2)]
    ps_st = nc.alloc_psum_tensor("ps_st", [128, 2 * TT_], F32)
    ps_tr = nc.alloc_psum_tensor("ps_tr", [128, 64], F32)
    ps_bc = nc.alloc_psum_tensor("ps_bc", [128, 2 * TT_], F32)

    p.emit("vector", lambda v: v.memset(onescol[:, :], 1.0), writes=["onescol"])
    p.emit("vector", lambda v: v.memset(onescolb[:, :], 1.0), writes=["onescolb"])
    p.emit("vector", lambda v: v.memset(onesrow[:, :], 1.0), writes=["onesrow"])
    p.emit("vector", lambda v: v.memset(epst[:, :], 1e-5), writes=["epst"])
    p.emit("vector", lambda v: v.memset(nhalf[:, :], -0.5), writes=["nhalf"])
    p.emit("gpsimd", lambda g: g.iota(ident[:, :], pattern=[[1, 128]], base=0,
                                      channel_multiplier=-1,
                                      allow_small_or_imprecise_dtypes=True),
           writes=["ident"])
    p.emit("vector", lambda v: v.tensor_single_scalar(ident[:, :], ident[:, :], 0.0, ALU.is_equal),
           reads=["ident"], writes=["ident"])
    for kc in range(8):
        p.emit("gpsimd", lambda g, kc=kc: g.dma_start(
            out=wfcs[:, 4 * D * kc:4 * D * (kc + 1)], in_=wfc[128 * kc:128 * (kc + 1), :]),
            writes=[f"wfc{kc}"], dma=True)
    for fc in range(32):
        p.emit("gpsimd", lambda g, fc=fc: g.dma_start(
            out=wprs[:, D * fc:D * (fc + 1)], in_=wpr[128 * fc:128 * (fc + 1), :]),
            writes=[f"wpr{fc}"], dma=True)
    p.emit("sync", lambda s: s.dma_start(
        out=lnrows[:, 0:128], in_=lnw.rearrange("(c p) -> c p", p=128)),
        writes=["lnrow_w"], dma=True)
    p.emit("sync", lambda s: s.dma_start(
        out=lnrows[:, 128:256], in_=lnb.rearrange("(c p) -> c p", p=128)),
        writes=["lnrow_b"], dma=True)
    def lntr(t):
        t.drain()
        p.pe_base = None
        t.transpose(ps_tr[:, 0:32], lnrows[0:32, 0:128], ident[0:32, 0:32])
        return t.transpose(ps_tr[:, 32:64], lnrows[0:32, 128:256], ident[0:32, 0:32])
    p.emit("tensor", lntr, reads=["lnrow_w", "lnrow_b", "ident"], writes=["ps_stt"])
    p.emit("vector", lambda v: v.tensor_copy(lnwc[:, :], ps_tr[:, 0:64]),
           reads=["ps_stt"], writes=["lnwc"])

    for tt in range(NTT):
        for kc in range(8):
            p.emit("sync", lambda s, tt=tt, kc=kc: s.dma_start(
                out=x1f[:, TT_ * kc:TT_ * (kc + 1)],
                in_=x1T[128 * kc:128 * (kc + 1), TT_ * tt:TT_ * (tt + 1)]),
                writes=[f"x1f{kc}"], dma=True)
            p.emit("vector", lambda v, kc=kc: v.tensor_copy(
                x1b[:, TT_ * kc:TT_ * (kc + 1)], x1f[:, TT_ * kc:TT_ * (kc + 1)]),
                reads=[f"x1f{kc}"], writes=[f"x1b{kc}"])
        for fc in range(32):
            j = fc % 2
            pairs = [(wfcs[:, 4 * D * kc + 128 * fc: 4 * D * kc + 128 * (fc + 1)],
                      x1b[:, TT_ * kc:TT_ * (kc + 1)]) for kc in range(8)]
            mm_acc(p, ps_h[j][:, :], pairs,
                   reads=[f"wfc{kc}" for kc in range(8)] + [f"x1b{kc}" for kc in range(8)],
                   writes=[f"ps_h{j}"])
            p.emit("scalar", lambda s, fc=fc, j=j: s.activation(
                out=hb[:, TT_ * fc:TT_ * (fc + 1)], in_=ps_h[j][:, :], func=AF.Gelu),
                reads=[f"ps_h{j}"], writes=[f"hb{fc}"])
        if stage < 2:
            for ec in range(8):
                p.emit("vector", lambda v, ec=ec: v.tensor_copy(
                    ofm[:, :], hb[:, TT_ * ec:TT_ * (ec + 1)]),
                    reads=[f"hb{ec}"], writes=["ofm"])
                p.emit("gpsimd", lambda g, tt=tt, ec=ec: g.dma_start(
                    out=xo[128 * ec:128 * (ec + 1), TT_ * tt:TT_ * (tt + 1)], in_=ofm[:, :]),
                    reads=["ofm"], writes=[f"xo{tt}_{ec}"], dma=True)
            continue
        def susum(t):
            last = None
            for fc in range(32):
                last = p.mm(t, ps_st[0:1, 0:TT_], onescolb[:, :],
                            hb[:, TT_ * fc:TT_ * (fc + 1)],
                            start=(fc == 0), stop=(fc == 31))
            return last
        p.emit("tensor", susum, reads=[f"hb{fc}" for fc in range(32)] + ["onescolb"],
               writes=["ps_st"])
        p.emit("vector", lambda v: v.tensor_copy(row[:, 0:TT_], ps_st[0:1, 0:TT_]),
               reads=["ps_st"], writes=["row_sum"])
        if stage < 2.2:
            p.emit("vector", lambda v: v.memset(row[:, TT_:2 * TT_], 1.0), writes=["row_sq"])
        else:
            for fc in range(32):
                j = fc % 2
                p.emit("scalar", lambda s, fc=fc, j=j: s.activation(
                    out=sqs[j][:, :], in_=hb[:, TT_ * fc:TT_ * (fc + 1)], func=AF.Square),
                    reads=[f"hb{fc}"], writes=[f"sqs{j}"])
                mm_acc(p, ps_st[0:1, TT_:2 * TT_], [(onescol[:, :], sqs[j][:, :])],
                       reads=[f"sqs{j}", "onescol"], writes=["ps_st"], accum=(fc > 0))
            p.emit("vector", lambda v: v.tensor_copy(row[:, TT_:2 * TT_], ps_st[0:1, TT_:2 * TT_]),
                   reads=["ps_st"], writes=["row_sq"])
        p.emit("vector", lambda v: v.tensor_scalar_mul(
            row[:, 2 * TT_:3 * TT_], row[:, 0:TT_], 1.0 / (4 * D)),
            reads=["row_sum"], writes=["row_mu"])
        p.emit("vector", lambda v: v.tensor_scalar_mul(
            row[:, 3 * TT_:4 * TT_], row[:, TT_:2 * TT_], 1.0 / (4 * D)),
            reads=["row_sq"], writes=["row_e2"])
        p.emit("vector", lambda v: v.tensor_tensor(
            row[:, 4 * TT_:5 * TT_], row[:, 2 * TT_:3 * TT_], row[:, 2 * TT_:3 * TT_], ALU.mult),
            reads=["row_mu"], writes=["row_m2"])
        p.emit("vector", lambda v: v.tensor_tensor(
            row[:, 4 * TT_:5 * TT_], row[:, 3 * TT_:4 * TT_], row[:, 4 * TT_:5 * TT_], ALU.subtract),
            reads=["row_e2", "row_m2"], writes=["row_var"])
        p.emit("scalar", lambda s: s.activation(
            out=row[:, 4 * TT_:5 * TT_], in_=row[:, 4 * TT_:5 * TT_], func=AF.Ln,
            bias=epst[0:1, 0:1]),
            reads=["row_var", "epst"], writes=["row_sd"])
        p.emit("scalar", lambda s: s.activation(
            out=row[:, 5 * TT_:6 * TT_], in_=row[:, 4 * TT_:5 * TT_], func=AF.Exp,
            scale=nhalf[0:1, 0:1]),
            reads=["row_sd", "nhalf"], writes=["row_rstd"])
        def bcmm(t):
            p.mm(t, ps_bc[:, 0:TT_], onesrow[:, :], row[0:1, 2 * TT_:3 * TT_],
                 start=True, stop=True)
            return p.mm(t, ps_bc[:, TT_:2 * TT_], onesrow[:, :], row[0:1, 5 * TT_:6 * TT_],
                        start=True, stop=True)
        p.emit("tensor", bcmm, reads=["row_mu", "row_rstd", "onesrow"], writes=["ps_bc"])
        if stage < 3:
            for ec in range(8):
                p.emit("vector", lambda v, ec=ec: v.tensor_copy(
                    ofm[:, :], hb[:, TT_ * ec:TT_ * (ec + 1)]),
                    reads=[f"hb{ec}", "row_rstd"], writes=["ofm"])
                p.emit("gpsimd", lambda g, tt=tt, ec=ec: g.dma_start(
                    out=xo[128 * ec:128 * (ec + 1), TT_ * tt:TT_ * (tt + 1)], in_=ofm[:, :]),
                    reads=["ofm"], writes=[f"xo{tt}_{ec}"], dma=True)
            continue
        for fc in range(32):
            j = fc % 2
            p.emit("vector", lambda v, fc=fc, j=j: v.tensor_tensor(
                scr[j][:, :], hb[:, TT_ * fc:TT_ * (fc + 1)], ps_bc[:, 0:TT_], ALU.subtract),
                reads=[f"hb{fc}", "ps_bc"], writes=[f"scr_{j}"])
            p.emit("vector", lambda v, j=j: v.tensor_tensor(
                scr2[j][:, :], scr[j][:, :], ps_bc[:, TT_:2 * TT_], ALU.mult),
                reads=[f"scr_{j}", "ps_bc"], writes=[f"scr2_{j}"])
            p.emit("vector", lambda v, fc=fc, j=j: v.tensor_scalar(
                hb[:, TT_ * fc:TT_ * (fc + 1)], scr2[j][:, :],
                lnwc[:, fc:fc + 1],
                lnwc[:, 32 + fc:33 + fc], ALU.mult, ALU.add),
                reads=[f"scr2_{j}", "lnwc"], writes=[f"hb{fc}"])
        if stage < 4:
            for ec in range(8):
                p.emit("vector", lambda v, ec=ec: v.tensor_copy(
                    ofm[:, :], hb[:, TT_ * ec:TT_ * (ec + 1)]),
                    reads=[f"hb{ec}"], writes=["ofm"])
                p.emit("gpsimd", lambda g, tt=tt, ec=ec: g.dma_start(
                    out=xo[128 * ec:128 * (ec + 1), TT_ * tt:TT_ * (tt + 1)], in_=ofm[:, :]),
                    reads=["ofm"], writes=[f"xo{tt}_{ec}"], dma=True)
            continue
        for ec in range(8):
            j = ec % 2
            pairs = [(wprs[:, D * fc + 128 * ec: D * fc + 128 * (ec + 1)],
                      hb[:, TT_ * fc:TT_ * (fc + 1)]) for fc in range(32)]
            mm_acc(p, ps_o[j][:, :], pairs,
                   reads=[f"wpr{fc}" for fc in range(32)] + [f"hb{fc}" for fc in range(32)],
                   writes=[f"ps_o{j}"])
            p.emit("vector", lambda v, ec=ec, j=j: v.tensor_tensor(
                zfm[:, TT_ * ec:TT_ * (ec + 1)], ps_o[j][:, :],
                x1f[:, TT_ * ec:TT_ * (ec + 1)], ALU.add),
                reads=[f"ps_o{j}", f"x1f{ec}"], writes=[f"zfm{ec}"])
        if stage < 5:
            for ec in range(8):
                p.emit("vector", lambda v, ec=ec: v.tensor_copy(
                    ofm[:, :], zfm[:, TT_ * ec:TT_ * (ec + 1)]),
                    reads=[f"zfm{ec}"], writes=["ofm"])
                p.emit("gpsimd", lambda g, tt=tt, ec=ec: g.dma_start(
                    out=xo[128 * ec:128 * (ec + 1), TT_ * tt:TT_ * (tt + 1)], in_=ofm[:, :]),
                    reads=["ofm"], writes=[f"xo{tt}_{ec}"], dma=True)
            continue
        def zsum(t):
            last = None
            for ec in range(8):
                last = p.mm(t, ps_st[0:1, 0:TT_], onescol[:, :],
                            zfm[:, TT_ * ec:TT_ * (ec + 1)],
                            start=(ec == 0), stop=(ec == 7))
            return last
        p.emit("tensor", zsum, reads=[f"zfm{ec}" for ec in range(8)] + ["onescol"],
               writes=["ps_st"])
        p.emit("vector", lambda v: v.tensor_copy(row[:, 0:TT_], ps_st[0:1, 0:TT_]),
               reads=["ps_st"], writes=["row_sum"])
        for ec in range(8):
            j = ec % 2
            p.emit("scalar", lambda s, ec=ec, j=j: s.activation(
                out=sqs[j][:, :], in_=zfm[:, TT_ * ec:TT_ * (ec + 1)], func=AF.Square),
                reads=[f"zfm{ec}"], writes=[f"sqs{j}"])
            mm_acc(p, ps_st[0:1, TT_:2 * TT_], [(onescol[:, :], sqs[j][:, :])],
                   reads=[f"sqs{j}", "onescol"], writes=["ps_st"], accum=(ec > 0))
        p.emit("vector", lambda v: v.tensor_copy(row[:, TT_:2 * TT_], ps_st[0:1, TT_:2 * TT_]),
               reads=["ps_st"], writes=["row_sq"])
        p.emit("vector", lambda v: v.tensor_scalar_mul(
            row[:, 2 * TT_:3 * TT_], row[:, 0:TT_], 1.0 / D),
            reads=["row_sum"], writes=["row_mu"])
        p.emit("vector", lambda v: v.tensor_scalar_mul(
            row[:, 3 * TT_:4 * TT_], row[:, TT_:2 * TT_], 1.0 / D),
            reads=["row_sq"], writes=["row_e2"])
        p.emit("vector", lambda v: v.tensor_tensor(
            row[:, 4 * TT_:5 * TT_], row[:, 2 * TT_:3 * TT_], row[:, 2 * TT_:3 * TT_], ALU.mult),
            reads=["row_mu"], writes=["row_m2"])
        p.emit("vector", lambda v: v.tensor_tensor(
            row[:, 4 * TT_:5 * TT_], row[:, 3 * TT_:4 * TT_], row[:, 4 * TT_:5 * TT_], ALU.subtract),
            reads=["row_e2", "row_m2"], writes=["row_var"])
        p.emit("scalar", lambda s: s.activation(
            out=row[:, 4 * TT_:5 * TT_], in_=row[:, 4 * TT_:5 * TT_], func=AF.Ln,
            bias=epst[0:1, 0:1]),
            reads=["row_var", "epst"], writes=["row_sd"])
        p.emit("scalar", lambda s: s.activation(
            out=row[:, 5 * TT_:6 * TT_], in_=row[:, 4 * TT_:5 * TT_], func=AF.Exp,
            scale=nhalf[0:1, 0:1]),
            reads=["row_sd", "nhalf"], writes=["row_rstd"])
        def bcmm2(t):
            p.mm(t, ps_bc[:, 0:TT_], onesrow[:, :], row[0:1, 2 * TT_:3 * TT_],
                 start=True, stop=True)
            return p.mm(t, ps_bc[:, TT_:2 * TT_], onesrow[:, :], row[0:1, 5 * TT_:6 * TT_],
                        start=True, stop=True)
        p.emit("tensor", bcmm2, reads=["row_mu", "row_rstd", "onesrow"], writes=["ps_bc"])
        for ec in range(8):
            j = ec % 2
            p.emit("vector", lambda v, ec=ec, j=j: v.tensor_tensor(
                scr[j][:, :], zfm[:, TT_ * ec:TT_ * (ec + 1)], ps_bc[:, 0:TT_], ALU.subtract),
                reads=[f"zfm{ec}", "ps_bc"], writes=[f"scr_{j}"])
            p.emit("vector", lambda v, j=j: v.tensor_tensor(
                ofm[j][:, :], scr[j][:, :], ps_bc[:, TT_:2 * TT_], ALU.mult),
                reads=[f"scr_{j}", "ps_bc"], writes=[f"ofm{j}"])
            p.emit("gpsimd", lambda g, tt=tt, ec=ec, j=j: g.dma_start(
                out=xo[128 * ec:128 * (ec + 1), TT_ * tt:TT_ * (tt + 1)], in_=ofm[j][:, :]),
                reads=[f"ofm{j}"], writes=[f"xo{tt}_{ec}"], dma=True)
    p.finalize()
    return nc



SCALE1 = 1.0 / float(np.sqrt(512.0))
EPSF = float(np.finfo(np.float32).eps)


def l1_consts():
    m = np.zeros((128, 128), np.float32)
    for p in range(128):
        for f in range(128):
            if p // 16 == f // 16:
                m[p, f] = 1.0
    return m


def build_l1_attn(nc, dbg=False):
    p = Prog(nc)
    x2T = nc.dram_tensor("x2T", [D, T], F32, kind="ExternalInput").ap()
    wqT = nc.dram_tensor("wqT", [D, D], F32, kind="ExternalInput").ap()
    wkT = nc.dram_tensor("wkT", [D, D], F32, kind="ExternalInput").ap()
    wvT = nc.dram_tensor("wvT", [D, D], F32, kind="ExternalInput").ap()
    lam = nc.dram_tensor("lam", [4], F32, kind="ExternalInput").ap()
    g1 = nc.dram_tensor("g1", [D], F32, kind="ExternalInput").ap()
    g2 = nc.dram_tensor("g2", [HD], F32, kind="ExternalInput").ap()
    g3 = nc.dram_tensor("g3", [D], F32, kind="ExternalInput").ap()
    cMA = nc.dram_tensor("cMA", [128, 128], F32, kind="ExternalInput").ap()
    a3o = nc.dram_tensor("attn3", [T, D], F32, kind="ExternalOutput").ap()
    vtmd = nc.dram_tensor("vtmd", [T, D], BF16, kind="Internal").ap()
    a2d = nc.dram_tensor("a2d", [T, D], BF16, kind="Internal").ap()
    if dbg:
        vtm_o = nc.dram_tensor("vtm_o", [T, D], BF16, kind="ExternalOutput").ap()
        a2_o = nc.dram_tensor("a2_o", [T, D], BF16, kind="ExternalOutput").ap()
        qil_o = nc.dram_tensor("qil_o", [64, 16 * T // 2], BF16, kind="ExternalOutput").ap()
        kil_o = nc.dram_tensor("kil_o", [64, 16 * T // 2], BF16, kind="ExternalOutput").ap()

    xf = [nc.alloc_sbuf_tensor("xf0", [128, T], F32)]
    xf = [xf[0], xf[0]]
    xnb = nc.alloc_sbuf_tensor("xnb", [128, 8 * T], BF16)
    wq = nc.alloc_sbuf_tensor("wq", [128, 8 * D], BF16)
    wk = nc.alloc_sbuf_tensor("wk", [128, 8 * D], BF16)
    wv = nc.alloc_sbuf_tensor("wv", [128, 8 * D], BF16)
    qil = nc.alloc_sbuf_tensor("qil", [64, 16 * (T // 2)], BF16)
    kil = nc.alloc_sbuf_tensor("kil", [64, 16 * (T // 2)], BF16)
    vfm = nc.alloc_sbuf_tensor("vfm", [128, D], BF16)
    vil = nc.alloc_sbuf_tensor("vil", [128, 16 * HD], BF16)
    rstdbc = nc.alloc_sbuf_tensor("rstdbc", [128, T], F32)
    sqscr = nc.alloc_sbuf_tensor("sqscr", [128, 512], F32)
    rowa = nc.alloc_sbuf_tensor("rowa", [1, T], F32)
    g3row = nc.alloc_sbuf_tensor("g3row", [1, D], F32)
    g1c = nc.alloc_sbuf_tensor("g1c", [128, 8], F32)
    g3c = nc.alloc_sbuf_tensor("g3c", [128, 8], F32)
    g2bc = nc.alloc_sbuf_tensor("g2bc", [128, HD], F32)
    g3bc = nc.alloc_sbuf_tensor("g3bc", [128, D], F32)
    grow = nc.alloc_sbuf_tensor("grow", [8, 256], F32)
    g2r = nc.alloc_sbuf_tensor("g2r", [1, HD], F32)
    lamt = nc.alloc_sbuf_tensor("lamt", [1, 8], F32)
    lvec = nc.alloc_sbuf_tensor("lvec", [64, 2], F32)
    maskA = nc.alloc_sbuf_tensor("maskA", [128, 128], F32)
    ident = nc.alloc_sbuf_tensor("ident", [128, 128], F32)
    onesrow = nc.alloc_sbuf_tensor("onesrow", [1, 128], F32)
    onescol = nc.alloc_sbuf_tensor("onescol", [128, 1], F32)
    epst = nc.alloc_sbuf_tensor("epst", [128, 1], F32)
    nhalf = nc.alloc_sbuf_tensor("nhalf", [128, 1], F32)
    escr = [nc.alloc_sbuf_tensor(f"escr{j}", [64, 512], F32) for j in range(6)]
    Am = nc.alloc_sbuf_tensor("Am", [128, 512], BF16)
    a2g = nc.alloc_sbuf_tensor("a2g", [128, 1024], BF16)
    r2s = nc.alloc_sbuf_tensor("r2s", [128, 16], F32)
    mu1 = nc.alloc_sbuf_tensor("mu1", [128, 4], F32)

    ps = [nc.alloc_psum_tensor(f"psg{j}", [128, 512], F32) for j in range(8)]

    p.emit("vector", lambda v: v.memset(onesrow[:, :], 1.0), writes=["onesrow"])
    p.emit("vector", lambda v: v.memset(onescol[:, :], 1.0), writes=["onescol"])
    p.emit("vector", lambda v: v.memset(epst[:, :], EPSF), writes=["epst"])
    p.emit("vector", lambda v: v.memset(nhalf[:, :], -0.5), writes=["nhalf"])
    p.emit("gpsimd", lambda g: g.iota(ident[:, :], pattern=[[1, 128]], base=0,
                                      channel_multiplier=-1,
                                      allow_small_or_imprecise_dtypes=True),
           writes=["ident"])
    p.emit("vector", lambda v: v.tensor_single_scalar(ident[:, :], ident[:, :], 0.0, ALU.is_equal),
           reads=["ident"], writes=["ident"])
    p.emit("sync", lambda s: s.dma_start(out=maskA[:, :], in_=cMA[:, :]), writes=["maskA"], dma=True)
    p.emit("sync", lambda s: s.dma_start(out=lamt[:, 0:4], in_=lam[None, :]), writes=["lamt"], dma=True)
    p.emit("vector", lambda v: v.tensor_tensor(lamt[:, 4:5], lamt[:, 0:1], lamt[:, 1:2], ALU.mult),
           reads=["lamt"], writes=["l01"])
    p.emit("vector", lambda v: v.tensor_tensor(lamt[:, 5:6], lamt[:, 2:3], lamt[:, 3:4], ALU.mult),
           reads=["lamt"], writes=["l23"])
    p.emit("scalar", lambda s: s.activation(out=lamt[:, 4:5], in_=lamt[:, 4:5], func=AF.Exp),
           reads=["l01"], writes=["l01"])
    p.emit("scalar", lambda s: s.activation(out=lamt[:, 5:6], in_=lamt[:, 5:6], func=AF.Exp),
           reads=["l23"], writes=["l23"])
    p.emit("vector", lambda v: v.tensor_tensor(lamt[:, 6:7], lamt[:, 4:5], lamt[:, 5:6], ALU.subtract),
           reads=["l01", "l23"], writes=["lmb"])
    p.emit("vector", lambda v: v.tensor_scalar(lamt[:, 6:7], lamt[:, 6:7], 0.2, -1.0,
                                               ALU.add, ALU.mult),
           reads=["lmb"], writes=["neglmb"])
    def lbc(t):
        return p.mm(t, ps[0][0:64, 0:1], onesrow[0:1, 0:64], lamt[0:1, 6:7],
                    start=True, stop=True)
    p.emit("tensor", lbc, reads=["neglmb", "onesrow"], writes=["psg0"])
    p.emit("gpsimd", lambda g: g.iota(lvec[:, 1:2], pattern=[[0, 1]], base=0,
                                      channel_multiplier=1,
                                      allow_small_or_imprecise_dtypes=True),
           writes=["lv_i"])
    p.emit("vector", lambda v: v.tensor_single_scalar(lvec[:, 1:2], lvec[:, 1:2], 31.5, ALU.is_gt),
           reads=["lv_i"], writes=["lv_m"])
    p.emit("vector", lambda v: v.tensor_scalar_add(lvec[:, 0:1], ps[0][0:64, 0:1], -1.0),
           reads=["psg0"], writes=["lv_t"])
    p.emit("vector", lambda v: v.tensor_tensor(lvec[:, 0:1], lvec[:, 1:2], lvec[:, 0:1], ALU.mult),
           reads=["lv_t", "lv_m"], writes=["lv_t2"])
    p.emit("vector", lambda v: v.tensor_scalar_add(lvec[:, 0:1], lvec[:, 0:1], 1.0),
           reads=["lv_t2"], writes=["lvec"])
    p.emit("sync", lambda s: s.dma_start(out=grow[0:8, 0:128], in_=g1.rearrange("(c p) -> c p", p=128)),
           writes=["g1r"], dma=True)
    p.emit("sync", lambda s: s.dma_start(out=grow[0:8, 128:256], in_=g3.rearrange("(c p) -> c p", p=128)),
           writes=["g3r"], dma=True)
    p.emit("sync", lambda s: s.dma_start(out=g2r[0:1, :], in_=g2[None, :]),
           writes=["g2r"], dma=True)
    def gtr2(t):
        t.drain(); p.pe_base = None
        t.transpose(ps[1][:, 0:8], grow[0:8, 0:128], ident[0:8, 0:8])
        return t.transpose(ps[1][:, 8:16], grow[0:8, 128:256], ident[0:8, 0:8])
    p.emit("tensor", gtr2, reads=["g1r", "g3r", "ident"], writes=["psg1"])
    p.emit("vector", lambda v: v.tensor_copy(g1c[:, :], ps[1][:, 0:8]),
           reads=["psg1"], writes=["g1c"])
    p.emit("vector", lambda v: v.tensor_copy(g3c[:, :], ps[1][:, 8:16]),
           reads=["psg1"], writes=["g3c"])
    def g2b(t):
        return p.mm(t, ps[2][:, 0:64], onesrow[:, :], g2r[0:1, :], start=True, stop=True)
    p.emit("tensor", g2b, reads=["g2r", "onesrow"], writes=["psg2"])
    p.emit("vector", lambda v: v.tensor_scalar_mul(g2bc[:, :], ps[2][:, 0:64], 0.8),
           reads=["psg2"], writes=["g2bc"])
    p.emit("sync", lambda s: s.dma_start(out=g3row[0:1, 0:D], in_=g3[None, :]),
           writes=["g3row"], dma=True)
    for i in range(2):
        def g3bi(t, i=i):
            return p.mm(t, ps[3][:, 0:512], onesrow[:, :], g3row[0:1, 512 * i:512 * (i + 1)],
                        start=True, stop=True)
        p.emit("tensor", g3bi, reads=["g3row", "onesrow"], writes=["psg3"])
        p.emit("vector", lambda v, i=i: v.tensor_copy(g3bc[:, 512 * i:512 * (i + 1)], ps[3][:, 0:512]),
               reads=["psg3"], writes=["g3bc"])

    for kc in range(8):
        p.emit("gpsimd", lambda g, kc=kc: g.dma_start(
            out=wq[:, D * kc:D * (kc + 1)], in_=wqT[128 * kc:128 * (kc + 1), :]),
            writes=[f"wq{kc}"], dma=True)
        p.emit("gpsimd", lambda g, kc=kc: g.dma_start(
            out=wk[:, D * kc:D * (kc + 1)], in_=wkT[128 * kc:128 * (kc + 1), :]),
            writes=[f"wk{kc}"], dma=True)
        p.emit("gpsimd", lambda g, kc=kc: g.dma_start(
            out=wv[:, D * kc:D * (kc + 1)], in_=wvT[128 * kc:128 * (kc + 1), :]),
            writes=[f"wv{kc}"], dma=True)

    for kc in range(8):
        j = kc % 2
        p.emit("sync", lambda s, kc=kc, j=j: s.dma_start(
            out=xf[j][:, :], in_=x2T[128 * kc:128 * (kc + 1), :]),
            writes=["xfA"], dma=True)
        for tt in range(4):
            p.emit("scalar", lambda s, j=j, tt=tt: s.activation(
                out=sqscr[:, :], in_=xf[j][:, 512 * tt:512 * (tt + 1)], func=AF.Square),
                reads=["xfA"], writes=["sqscr"])
            mm_acc(p, ps[tt][0:1, 0:512], [(onescol[:, :], sqscr[:, :])],
                   reads=["sqscr", "onescol"], writes=[f"psg{tt}"], accum=(kc > 0))
            if kc == 7:
                p.emit("vector", lambda v, tt=tt: v.tensor_copy(
                    rowa[:, 512 * tt:512 * (tt + 1)], ps[tt][0:1, 0:512]),
                    reads=[f"psg{tt}"], writes=[f"msrow{tt}"])
    p.emit("vector", lambda v: v.tensor_scalar_mul(rowa[:, 0:T], rowa[:, 0:T], 1.0 / D),
           reads=[f"msrow{tt}" for tt in range(4)], writes=["msn"])
    p.emit("scalar", lambda s: s.activation(out=rowa[:, 0:T], in_=rowa[:, 0:T],
                                            func=AF.Ln, bias=epst[0:1, 0:1]),
           reads=["msn"], writes=["msq"])
    p.emit("scalar", lambda s: s.activation(out=rowa[:, 0:T], in_=rowa[:, 0:T],
                                            func=AF.Exp, scale=nhalf[0:1, 0:1]),
           reads=["msq", "nhalf"], writes=["rstdrow"])
    for tt in range(4):
        def rbc(t, tt=tt):
            return p.mm(t, ps[6][:, 0:512], onesrow[:, :],
                        rowa[0:1, 512 * tt:512 * (tt + 1)], start=True, stop=True)
        p.emit("tensor", rbc, reads=["rstdrow", "onesrow"], writes=["psg6"])
        p.emit("vector", lambda v, tt=tt: v.tensor_copy(
            rstdbc[:, 512 * tt:512 * (tt + 1)], ps[6][:, 0:512]),
            reads=["psg6"], writes=["rstdbc"])
    for kc in range(8):
        j = kc % 2
        p.emit("sync", lambda s, kc=kc, j=j: s.dma_start(
            out=xf[j][:, :], in_=x2T[128 * kc:128 * (kc + 1), :]),
            writes=["xfA"], dma=True)
        p.emit("vector", lambda v, j=j: v.tensor_tensor(
            xf[j][:, :], xf[j][:, :], rstdbc[:, :], ALU.mult),
            reads=["xfA", "rstdbc"], writes=["xfA"])
        p.emit("vector", lambda v, kc=kc, j=j: v.tensor_single_scalar(
            xnb[:, T * kc:T * (kc + 1)], xf[j][:, :], g1c[:, kc:kc + 1], ALU.mult),
            reads=["xfA", "g1c"], writes=[f"xnb{kc}"])

    for gt in range(16):
        for half in range(2):
            pj = 6 + half
            pairs = [(xnb[:, T * kc + 128 * gt: T * kc + 128 * (gt + 1)],
                      wv[:, D * kc + 512 * half: D * kc + 512 * (half + 1)])
                     for kc in range(8)]
            mm_acc(p, ps[pj][:, 0:512], pairs,
                   reads=[f"wv{kc}" for kc in range(8)] + [f"xnb{kc}" for kc in range(8)],
                   writes=[f"psg{pj}"])
            p.emit("vector", lambda v, half=half, pj=pj: v.tensor_scalar_mul(
                vfm[:, 512 * half:512 * (half + 1)], ps[pj][:, 0:512], SCALE1),
                reads=[f"psg{pj}"], writes=[f"vfm{half}"])
        p.emit("gpsimd", lambda g, gt=gt: g.dma_start(
            out=vtmd[128 * gt:128 * (gt + 1), :], in_=vfm[:, 0:1024]),
            reads=["vfm0", "vfm1"], writes=[f"vtm{gt}"], dma=True)

    for hf in range(2):
        for tt in range(2):
            for h in range(NH):
                pj = h % 4
                wmat, wkeys = (wq, "wq") if True else None
                pairs_q = [(wq[:, D * kc + HD * h: D * kc + HD * (h + 1)],
                            xnb[:, T * kc + 1024 * hf + 512 * tt: T * kc + 1024 * hf + 512 * tt + 512])
                           for kc in range(8)]
                mm_acc(p, ps[pj][0:64, 0:512], pairs_q,
                       reads=[f"wq{kc}" for kc in range(8)] + [f"xnb{kc}" for kc in range(8)],
                       writes=[f"psg{pj}"])
                eb = 3 * (h % 2)
                p.emit("vector", lambda v, pj=pj, eb=eb: v.tensor_scalar_min(
                    escr[eb][:, :], ps[pj][0:64, 0:512], 0.0),
                    reads=[f"psg{pj}"], writes=[f"escr0_{eb}"])
                p.emit("scalar", lambda s, eb=eb: s.activation(
                    out=escr[eb + 1][:, :], in_=escr[eb][:, :], func=AF.Exp),
                    reads=[f"escr0_{eb}"], writes=[f"escr1_{eb}"])
                p.emit("vector", lambda v, pj=pj, eb=eb: v.tensor_scalar_max(
                    escr[eb + 2][:, :], ps[pj][0:64, 0:512], 0.0),
                    reads=[f"psg{pj}"], writes=[f"escr2_{eb}"])
                p.emit("vector", lambda v, eb=eb: v.scalar_tensor_tensor(
                    escr[eb][:, :], escr[eb + 1][:, :], -1.0, escr[eb + 2][:, :], ALU.add, ALU.add),
                    reads=[f"escr1_{eb}", f"escr2_{eb}"], writes=[f"escr0b_{eb}"])
                p.emit("vector", lambda v, h=h, tt=tt, eb=eb: v.tensor_single_scalar(
                    qil[:, :].rearrange("p (t h) -> p t h", h=NH)[:, 512 * tt:512 * tt + 512, h:h + 1],
                    escr[eb][:, :], lvec[:, 0:1], ALU.mult),
                    reads=[f"escr0b_{eb}", "lvec"], writes=[f"qil_{hf}_{tt}_{h}"])
                pairs_k = [(wk[:, D * kc + HD * h: D * kc + HD * (h + 1)],
                            xnb[:, T * kc + 1024 * hf + 512 * tt: T * kc + 1024 * hf + 512 * tt + 512])
                           for kc in range(8)]
                pj2 = 4 + h % 3
                mm_acc(p, ps[pj2][0:64, 0:512], pairs_k,
                       reads=[f"wk{kc}" for kc in range(8)] + [f"xnb{kc}" for kc in range(8)],
                       writes=[f"psg{pj2}"])
                eb2 = 3 * (h % 2)
                p.emit("vector", lambda v, pj2=pj2, eb2=eb2: v.tensor_scalar_min(
                    escr[eb2][:, :], ps[pj2][0:64, 0:512], 0.0),
                    reads=[f"psg{pj2}"], writes=[f"escr0_{eb2}"])
                p.emit("scalar", lambda s, eb2=eb2: s.activation(
                    out=escr[eb2 + 1][:, :], in_=escr[eb2][:, :], func=AF.Exp),
                    reads=[f"escr0_{eb2}"], writes=[f"escr1_{eb2}"])
                p.emit("vector", lambda v, pj2=pj2, eb2=eb2: v.tensor_scalar_max(
                    escr[eb2 + 2][:, :], ps[pj2][0:64, 0:512], 0.0),
                    reads=[f"psg{pj2}"], writes=[f"escr2_{eb2}"])
                p.emit("vector", lambda v, h=h, hf=hf, tt=tt, eb2=eb2: v.scalar_tensor_tensor(
                    kil[:, :].rearrange("p (t h) -> p t h", h=NH)[:, 512 * tt:512 * tt + 512, h:h + 1],
                    escr[eb2 + 1][:, :], -1.0, escr[eb2 + 2][:, :], ALU.add, ALU.add),
                    reads=[f"escr1_{eb2}", f"escr2_{eb2}"], writes=[f"kil_{hf}_{tt}_{h}"])

        for lt in range(8):
            gt = 8 * hf + lt
            p.emit("sync", lambda s, gt=gt: s.dma_start(
                out=vil[:, :].rearrange("p (c q) -> p c q", c=16),
                in_=vtmd[128 * gt:128 * (gt + 1), :].rearrange("(c l) (h q) -> (l h) c q", l=8, h=NH)),
                reads=[f"vtm{gt}"], writes=["vil"], dma=True)
            for grp in range(4):
                def amms(t, lt=lt, grp=grp):
                    last = None
                    for j in range(4):
                        cc = 16 * lt + 4 * grp + j
                        last = p.mm(t, ps[0][:, 128 * j:128 * (j + 1)],
                                    kil[:, 128 * cc:128 * (cc + 1)],
                                    qil[:, 128 * cc:128 * (cc + 1)],
                                    start=True, stop=True)
                    return last
                rd = []
                for tt in range(2):
                    for h in range(NH):
                        rd += [f"qil_{hf}_{tt}_{h}", f"kil_{hf}_{tt}_{h}"]
                p.emit("tensor", amms, reads=rd, writes=["psg0"])
                p.emit("vector", lambda v: v.tensor_tensor(
                    Am[:, :].rearrange("p (c f) -> p c f", c=4),
                    ps[0][:, 0:512].rearrange("p (c f) -> p c f", c=4),
                    maskA[:, None, :].to_broadcast([128, 4, 128]),
                    ALU.mult),
                    reads=["psg0", "maskA"], writes=["Am"])
                def avms(t, lt=lt, grp=grp):
                    last = None
                    for j in range(4):
                        cc4 = 4 * grp + j
                        last = p.mm(t, ps[1][:, 64 * j:64 * (j + 1)],
                                    Am[:, 128 * j:128 * (j + 1)],
                                    vil[:, 64 * cc4:64 * (cc4 + 1)],
                                    start=True, stop=True)
                    return last
                p.emit("tensor", avms, reads=["Am", "vil"], writes=["psg1"])
                p.emit("scalar", lambda s: s.activation(
                    out=sqscr[:, 0:256], in_=ps[1][:, 0:256], func=AF.Square),
                    reads=["psg1"], writes=["sqscr"])
                p.emit("vector", lambda v, grp=grp: v.tensor_reduce(
                    r2s[:, 4 * grp:4 * (grp + 1)],
                    sqscr[:, 0:256].rearrange("p (c q) -> p c q", c=4),
                    axis=mybir.AxisListType.X, op=ALU.add),
                    reads=["sqscr"], writes=["r2s"])
                p.emit("vector", lambda v, grp=grp: v.tensor_scalar_mul(
                    r2s[:, 4 * grp:4 * (grp + 1)], r2s[:, 4 * grp:4 * (grp + 1)], 1.0 / HD),
                    reads=["r2s"], writes=["r2s"])
                p.emit("scalar", lambda s, grp=grp: s.activation(
                    out=r2s[:, 4 * grp:4 * (grp + 1)], in_=r2s[:, 4 * grp:4 * (grp + 1)],
                    func=AF.Ln, bias=epst[:, 0:1]),
                    reads=["r2s", "epst"], writes=["r2s"])
                p.emit("scalar", lambda s, grp=grp: s.activation(
                    out=r2s[:, 4 * grp:4 * (grp + 1)], in_=r2s[:, 4 * grp:4 * (grp + 1)],
                    func=AF.Exp, scale=nhalf[:, 0:1]),
                    reads=["r2s", "nhalf"], writes=["r2s"])
                p.emit("vector", lambda v, grp=grp: v.tensor_tensor(
                    sqscr[:, 0:256].rearrange("p (c q) -> p c q", c=4),
                    ps[1][:, 0:256].rearrange("p (c q) -> p c q", c=4),
                    r2s[:, 4 * grp:4 * (grp + 1), None].to_broadcast([128, 4, HD]),
                    ALU.mult),
                    reads=["psg1", "r2s"], writes=["sqscr"])
                p.emit("vector", lambda v, grp=grp: v.tensor_tensor(
                    a2g[:, 256 * grp:256 * (grp + 1)].rearrange("p (c q) -> p c q", c=4),
                    sqscr[:, 0:256].rearrange("p (c q) -> p c q", c=4),
                    g2bc[:, None, :].to_broadcast([128, 4, HD]),
                    ALU.mult),
                    reads=["sqscr", "g2bc"], writes=[f"a2g{grp}"])
            p.emit("gpsimd", lambda g, gt=gt: g.dma_start(
                out=a2d[128 * gt:128 * (gt + 1), :].rearrange("(c l) (h q) -> (l h) c q", l=8, h=NH),
                in_=a2g[:, :].rearrange("p (c q) -> p c q", c=16)),
                reads=[f"a2g{grp}" for grp in range(4)], writes=[f"a2d{gt}"], dma=True)

    if dbg:
        p.emit("gpsimd", lambda g: g.dma_start(out=vtm_o[:, :], in_=vtmd[:, :]),
               reads=[f"vtm{gt}" for gt in range(16)], writes=["vtmo"], dma=True)
        p.emit("gpsimd", lambda g: g.dma_start(out=a2_o[:, :], in_=a2d[:, :]),
               reads=[f"a2d{gt}" for gt in range(16)], writes=["a2o"], dma=True)
        p.emit("gpsimd", lambda g: g.dma_start(out=qil_o[:, :], in_=qil[:, :]),
               reads=[f"qil_{hf}_{tt}_{h}" for hf in range(2) for tt in range(2) for h in range(NH)],
               writes=["qilo"], dma=True)
        p.emit("gpsimd", lambda g: g.dma_start(out=kil_o[:, :], in_=kil[:, :]),
               reads=[f"kil_{hf}_{tt}_{h}" for hf in range(2) for tt in range(2) for h in range(NH)],
               writes=["kilo"], dma=True)

    for gt in range(16):
        p.emit("gpsimd", lambda g, gt=gt: g.dma_start(
            out=xf[0][:, 0:D], in_=a2d[128 * gt:128 * (gt + 1), :]),
            reads=[f"a2d{gt}"], writes=["a2t", "xfA"], dma=True)
        p.emit("scalar", lambda s: s.activation(
            out=rstdbc[:, 0:D], in_=xf[0][:, 0:D], func=AF.Square, accum_out=mu1[:, 0:1]),
            reads=["a2t"], writes=["mu1", "scr", "rstdbc"])
        p.emit("vector", lambda v: v.tensor_scalar_mul(mu1[:, 1:2], mu1[:, 0:1], 1.0 / D),
               reads=["mu1"], writes=["mu1b"])
        p.emit("scalar", lambda s: s.activation(
            out=mu1[:, 2:3], in_=mu1[:, 1:2], func=AF.Ln, bias=epst[:, 0:1]),
            reads=["mu1b", "epst"], writes=["mu1c"])
        p.emit("scalar", lambda s: s.activation(
            out=mu1[:, 3:4], in_=mu1[:, 2:3], func=AF.Exp, scale=nhalf[:, 0:1]),
            reads=["mu1c", "nhalf"], writes=["mu1d"])
        p.emit("vector", lambda v: v.tensor_single_scalar(
            rstdbc[:, 0:D], xf[0][:, 0:D], mu1[:, 3:4], ALU.mult),
            reads=["a2t", "mu1d"], writes=["scr"])
        p.emit("vector", lambda v: v.tensor_tensor(
            rstdbc[:, 0:D], rstdbc[:, 0:D], g3bc[:, :], ALU.mult),
               reads=["scr", "g3bc"], writes=["scr"])
        p.emit("vector", lambda v: v.tensor_tensor(
            xf[0][:, D:2 * D], rstdbc[:, 0:D], xf[0][:, 0:D], ALU.add),
               reads=["scr", "a2t"], writes=["a3t", "xfA"])
        p.emit("gpsimd", lambda g, gt=gt: g.dma_start(
            out=a3o[128 * gt:128 * (gt + 1), :], in_=xf[0][:, D:2 * D]),
            reads=["a3t"], writes=[f"a3o{gt}"], dma=True)
    p.finalize()
    return nc



def build_l1_ffn(nc):
    p = Prog(nc)
    a3T = nc.dram_tensor("a3T", [D, T], F32, kind="ExternalInput").ap()
    x2T = nc.dram_tensor("x2T", [D, T], F32, kind="ExternalInput").ap()
    w1T = nc.dram_tensor("w1T", [D, D], F32, kind="ExternalInput").ap()
    w2T = nc.dram_tensor("w2T", [D, D], F32, kind="ExternalInput").ap()
    xo = nc.dram_tensor("x3fm", [D, T], F32, kind="ExternalOutput").ap()

    TT_ = 512
    NTT = T // TT_
    w1s = nc.alloc_sbuf_tensor("w1s", [128, 8 * D], BF16)
    w2s = nc.alloc_sbuf_tensor("w2s", [128, 8 * D], BF16)
    a3b = nc.alloc_sbuf_tensor("a3b", [128, 8 * T], BF16)
    h1b = nc.alloc_sbuf_tensor("h1b", [128, 8 * TT_], BF16)
    resf = nc.alloc_sbuf_tensor("resf", [128, TT_], F32)
    res2 = nc.alloc_sbuf_tensor("res2", [128, TT_], F32)
    zfm = nc.alloc_sbuf_tensor("zfm", [128, 8 * TT_], F32)
    sqs = [nc.alloc_sbuf_tensor(f"sqs{j}", [128, TT_], F32) for j in range(2)]
    row = nc.alloc_sbuf_tensor("row", [1, 6 * TT_], F32)
    ofm = nc.alloc_sbuf_tensor("ofm", [128, TT_], F32)
    onescol = nc.alloc_sbuf_tensor("onescol", [128, 1], F32)
    onesrow = nc.alloc_sbuf_tensor("onesrow", [1, 128], F32)
    epst = nc.alloc_sbuf_tensor("epst", [128, 1], F32)
    nhalf = nc.alloc_sbuf_tensor("nhalf", [128, 1], F32)
    scr = nc.alloc_sbuf_tensor("scr", [128, TT_], F32)

    ps_h = [nc.alloc_psum_tensor(f"ps_h{j}", [128, TT_], F32) for j in range(2)]
    ps_o = [nc.alloc_psum_tensor(f"ps_o{j}", [128, TT_], F32) for j in range(2)]
    ps_st = nc.alloc_psum_tensor("ps_st", [128, 2 * TT_], F32)
    ps_bc = nc.alloc_psum_tensor("ps_bc", [128, 2 * TT_], F32)

    p.emit("vector", lambda v: v.memset(onescol[:, :], 1.0), writes=["onescol"])
    p.emit("vector", lambda v: v.memset(onesrow[:, :], 1.0), writes=["onesrow"])
    p.emit("vector", lambda v: v.memset(epst[:, :], 1e-5), writes=["epst"])
    p.emit("vector", lambda v: v.memset(nhalf[:, :], -0.5), writes=["nhalf"])
    for kc in range(8):
        p.emit("gpsimd", lambda g, kc=kc: g.dma_start(
            out=w1s[:, D * kc:D * (kc + 1)], in_=w1T[128 * kc:128 * (kc + 1), :]),
            writes=[f"w1_{kc}"], dma=True)
        p.emit("gpsimd", lambda g, kc=kc: g.dma_start(
            out=w2s[:, D * kc:D * (kc + 1)], in_=w2T[128 * kc:128 * (kc + 1), :]),
            writes=[f"w2_{kc}"], dma=True)
        p.emit("gpsimd", lambda g, kc=kc: g.dma_start(
            out=a3b[:, T * kc:T * (kc + 1)], in_=a3T[128 * kc:128 * (kc + 1), :]),
            writes=[f"a3b{kc}"], dma=True)

    for tt in range(NTT):
        for fc in range(8):
            j = fc % 2
            pairs = [(w1s[:, D * kc + 128 * fc: D * kc + 128 * (fc + 1)],
                      a3b[:, T * kc + TT_ * tt: T * kc + TT_ * tt + TT_]) for kc in range(8)]
            mm_acc(p, ps_h[j][:, :], pairs,
                   reads=[f"w1_{kc}" for kc in range(8)] + [f"a3b{kc}" for kc in range(8)],
                   writes=[f"ps_h{j}"])
            p.emit("scalar", lambda s, fc=fc, j=j: s.activation(
                out=h1b[:, TT_ * fc:TT_ * (fc + 1)], in_=ps_h[j][:, :], func=AF.Gelu),
                reads=[f"ps_h{j}"], writes=[f"h1b{fc}"])
        for ec in range(8):
            j = ec % 2
            pairs = [(w2s[:, D * fc + 128 * ec: D * fc + 128 * (ec + 1)],
                      h1b[:, TT_ * fc:TT_ * (fc + 1)]) for fc in range(8)]
            mm_acc(p, ps_o[j][:, :], pairs,
                   reads=[f"w2_{fc}" for fc in range(8)] + [f"h1b{fc}" for fc in range(8)],
                   writes=[f"ps_o{j}"])
            p.emit("sync", lambda s, tt=tt, ec=ec: s.dma_start(
                out=resf[:, :], in_=a3T[128 * ec:128 * (ec + 1), TT_ * tt:TT_ * tt + TT_]),
                writes=["resf"], dma=True)
            p.emit("sync", lambda s, tt=tt, ec=ec: s.dma_start(
                out=res2[:, :], in_=x2T[128 * ec:128 * (ec + 1), TT_ * tt:TT_ * tt + TT_]),
                writes=["res2"], dma=True)
            p.emit("vector", lambda v, ec=ec, j=j: v.tensor_tensor(
                scr[:, :], ps_o[j][:, :], resf[:, :], ALU.add),
                reads=[f"ps_o{j}", "resf"], writes=["scr"])
            p.emit("vector", lambda v, ec=ec: v.tensor_tensor(
                zfm[:, TT_ * ec:TT_ * (ec + 1)], scr[:, :], res2[:, :], ALU.add),
                reads=["scr", "res2"], writes=[f"zfm{ec}"])
        def zsum(t):
            last = None
            for ec in range(8):
                last = p.mm(t, ps_st[0:1, 0:TT_], onescol[:, :],
                            zfm[:, TT_ * ec:TT_ * (ec + 1)],
                            start=(ec == 0), stop=(ec == 7))
            return last
        p.emit("tensor", zsum, reads=[f"zfm{ec}" for ec in range(8)] + ["onescol"],
               writes=["ps_st"])
        p.emit("vector", lambda v: v.tensor_copy(row[:, 0:TT_], ps_st[0:1, 0:TT_]),
               reads=["ps_st"], writes=["row_sum"])
        for ec in range(8):
            j = ec % 2
            p.emit("scalar", lambda s, ec=ec, j=j: s.activation(
                out=sqs[j][:, :], in_=zfm[:, TT_ * ec:TT_ * (ec + 1)], func=AF.Square),
                reads=[f"zfm{ec}"], writes=[f"sqs{j}"])
            mm_acc(p, ps_st[0:1, TT_:2 * TT_], [(onescol[:, :], sqs[j][:, :])],
                   reads=[f"sqs{j}", "onescol"], writes=["ps_st"], accum=(ec > 0))
        p.emit("vector", lambda v: v.tensor_copy(row[:, TT_:2 * TT_], ps_st[0:1, TT_:2 * TT_]),
               reads=["ps_st"], writes=["row_sq"])
        p.emit("vector", lambda v: v.tensor_scalar_mul(
            row[:, 2 * TT_:3 * TT_], row[:, 0:TT_], 1.0 / D),
            reads=["row_sum"], writes=["row_mu"])
        p.emit("vector", lambda v: v.tensor_scalar_mul(
            row[:, 3 * TT_:4 * TT_], row[:, TT_:2 * TT_], 1.0 / D),
            reads=["row_sq"], writes=["row_e2"])
        p.emit("vector", lambda v: v.tensor_tensor(
            row[:, 4 * TT_:5 * TT_], row[:, 2 * TT_:3 * TT_], row[:, 2 * TT_:3 * TT_], ALU.mult),
            reads=["row_mu"], writes=["row_m2"])
        p.emit("vector", lambda v: v.tensor_tensor(
            row[:, 4 * TT_:5 * TT_], row[:, 3 * TT_:4 * TT_], row[:, 4 * TT_:5 * TT_], ALU.subtract),
            reads=["row_e2", "row_m2"], writes=["row_var"])
        p.emit("scalar", lambda s: s.activation(
            out=row[:, 4 * TT_:5 * TT_], in_=row[:, 4 * TT_:5 * TT_], func=AF.Sqrt,
            bias=epst[0:1, 0:1]),
            reads=["row_var"], writes=["row_sd"])
        p.emit("vector", lambda v: v.reciprocal(
            row[:, 5 * TT_:6 * TT_], row[:, 4 * TT_:5 * TT_]),
            reads=["row_sd"], writes=["row_rstd"])
        def bcmm2(t):
            p.mm(t, ps_bc[:, 0:TT_], onesrow[:, :], row[0:1, 2 * TT_:3 * TT_],
                 start=True, stop=True)
            return p.mm(t, ps_bc[:, TT_:2 * TT_], onesrow[:, :], row[0:1, 5 * TT_:6 * TT_],
                        start=True, stop=True)
        p.emit("tensor", bcmm2, reads=["row_mu", "row_rstd", "onesrow"], writes=["ps_bc"])
        for ec in range(8):
            p.emit("vector", lambda v, ec=ec: v.tensor_tensor(
                scr[:, :], zfm[:, TT_ * ec:TT_ * (ec + 1)], ps_bc[:, 0:TT_], ALU.subtract),
                reads=[f"zfm{ec}", "ps_bc"], writes=["scr"])
            p.emit("vector", lambda v: v.tensor_tensor(
                ofm[:, :], scr[:, :], ps_bc[:, TT_:2 * TT_], ALU.mult),
                reads=["scr", "ps_bc"], writes=["ofm"])
            p.emit("gpsimd", lambda g, tt=tt, ec=ec: g.dma_start(
                out=xo[128 * ec:128 * (ec + 1), TT_ * tt:TT_ * tt + TT_], in_=ofm[:, :]),
                reads=["ofm"], writes=[f"xo{tt}_{ec}"], dma=True)
    p.finalize()
    return nc



NEXP = 8
CAP = 640
BIGOOB = 1.0e6


def moe_consts():
    P = np.zeros((8, 8), np.float32)
    G = np.zeros((8, 8), np.float32)
    for c in range(8):
        for cp in range(8):
            if cp // 4 == c // 4:
                G[cp, c] = 1.0
                if cp < c:
                    P[cp, c] = 1.0
    return P, G



def build_moe_route(nc):
    p = Prog(nc)
    x3T = nc.dram_tensor("x3T", [D, T], F32, kind="ExternalInput").ap()
    wr = nc.dram_tensor("wr", [D, NEXP], F32, kind="ExternalInput").ap()
    ro = nc.dram_tensor("route", [T, 8], F32, kind="ExternalOutput").ap()
    co = nc.dram_tensor("cnt", [2, NEXP], F32, kind="ExternalOutput").ap()

    xfm = nc.alloc_sbuf_tensor("xfm", [128, 8 * T], F32)
    wrs = nc.alloc_sbuf_tensor("wrs", [128, 8 * NEXP], F32)
    iota8 = nc.alloc_sbuf_tensor("iota8", [128, NEXP], F32)
    revio = nc.alloc_sbuf_tensor("revio", [128, NEXP], F32)
    ident = nc.alloc_sbuf_tensor("ident", [128, 128], F32)
    probs = nc.alloc_sbuf_tensor("probs", [128, NEXP], F32)
    mx8 = nc.alloc_sbuf_tensor("mx8", [128, 8], F32)
    sm = nc.alloc_sbuf_tensor("sm", [128, 8], F32)
    m1a = nc.alloc_sbuf_tensor("m1a", [128, 16 * NEXP], F32)
    m2a = nc.alloc_sbuf_tensor("m2a", [128, 16 * NEXP], F32)
    gi = nc.alloc_sbuf_tensor("gi", [128, 16 * 4], F32)
    em = nc.alloc_sbuf_tensor("em", [NEXP, 2 * T], F32)
    emones = nc.alloc_sbuf_tensor("emones", [NEXP, T], F32)
    lct = nc.alloc_sbuf_tensor("lct", [128, 16], F32)
    rt = nc.alloc_sbuf_tensor("rt", [128, 8], F32)
    cnts = nc.alloc_sbuf_tensor("cnts", [NEXP, 2], F32)

    ps_l = nc.alloc_psum_tensor("ps_l", [128, NEXP], F32)
    ps_t = nc.alloc_psum_tensor("ps_t", [NEXP, T], F32)
    ps_b = nc.alloc_psum_tensor("ps_b", [128, 16], F32)

    p.emit("gpsimd", lambda g: g.iota(iota8[:, :], pattern=[[1, NEXP]], base=0,
                                      channel_multiplier=0,
                                      allow_small_or_imprecise_dtypes=True),
           writes=["iota8"])
    p.emit("vector", lambda v: v.tensor_scalar(revio[:, :], iota8[:, :], -1.0, 99.0,
                                               ALU.mult, ALU.add),
           reads=["iota8"], writes=["revio"])
    p.emit("gpsimd", lambda g: g.iota(ident[:, :], pattern=[[1, 128]], base=0,
                                      channel_multiplier=-1,
                                      allow_small_or_imprecise_dtypes=True),
           writes=["ident"])
    p.emit("vector", lambda v: v.tensor_single_scalar(ident[:, :], ident[:, :], 0.0, ALU.is_equal),
           reads=["ident"], writes=["ident"])
    p.emit("vector", lambda v: v.memset(emones[:, :], 1.0), writes=["emones"])
    for kc in range(8):
        p.emit("sync", lambda s, kc=kc: s.dma_start(
            out=xfm[:, T * kc:T * (kc + 1)], in_=x3T[128 * kc:128 * (kc + 1), :]),
            writes=[f"xfm{kc}"], dma=True)
        p.emit("sync", lambda s, kc=kc: s.dma_start(
            out=wrs[:, NEXP * kc:NEXP * (kc + 1)], in_=wr[128 * kc:128 * (kc + 1), :]),
            writes=[f"wrs{kc}"], dma=True)

    for tt in range(16):
        pairs = [(xfm[:, T * kc + 128 * tt: T * kc + 128 * (tt + 1)],
                  wrs[:, NEXP * kc:NEXP * (kc + 1)]) for kc in range(8)]
        mm_acc(p, ps_l[:, :], pairs,
               reads=[f"xfm{kc}" for kc in range(8)] + [f"wrs{kc}" for kc in range(8)],
               writes=["ps_l"])
        p.emit("vector", lambda v: v.reduce_max(sm[:, 0:1], ps_l[:, :], axis=mybir.AxisListType.X),
               reads=["ps_l"], writes=["smx"])
        p.emit("vector", lambda v: v.tensor_single_scalar(probs[:, :], ps_l[:, :], sm[:, 0:1],
                                                          ALU.subtract),
               reads=["ps_l", "smx"], writes=["probs"])
        p.emit("scalar", lambda s: s.activation(out=probs[:, :], in_=probs[:, :], func=AF.Exp),
               reads=["probs"], writes=["probs"])
        p.emit("vector", lambda v: v.reduce_sum(sm[:, 1:2], probs[:, :], axis=mybir.AxisListType.X),
               reads=["probs"], writes=["sms"])
        p.emit("vector", lambda v: v.reciprocal(sm[:, 2:3], sm[:, 1:2]),
               reads=["sms"], writes=["smr"])
        p.emit("vector", lambda v: v.tensor_single_scalar(probs[:, :], probs[:, :], sm[:, 2:3],
                                                          ALU.mult),
               reads=["probs", "smr"], writes=["probs"])
        p.emit("vector", lambda v: v.max(out=mx8[:, :], in_=probs[:, :]),
               reads=["probs"], writes=["mx8"])
        p.emit("vector", lambda v: v.tensor_single_scalar(rt[:, 0:8], probs[:, :], mx8[:, 0:1],
                                                          ALU.is_equal),
               reads=["probs", "mx8"], writes=["eq1"])
        p.emit("vector", lambda v: v.tensor_tensor(rt[:, 0:8], rt[:, 0:8], revio[:, :], ALU.mult),
               reads=["eq1", "revio"], writes=["eq1r"])
        p.emit("vector", lambda v: v.reduce_max(sm[:, 3:4], rt[:, 0:8], axis=mybir.AxisListType.X),
               reads=["eq1r"], writes=["m1v"])
        p.emit("vector", lambda v: v.tensor_scalar(sm[:, 3:4], sm[:, 3:4], -1.0, 99.0,
                                                   ALU.mult, ALU.add),
               reads=["m1v"], writes=["idx1"])
        p.emit("vector", lambda v, tt=tt: v.tensor_single_scalar(
            m1a[:, NEXP * tt:NEXP * (tt + 1)], iota8[:, :], sm[:, 3:4], ALU.is_equal),
            reads=["iota8", "idx1"], writes=[f"m1a{tt}"])
        p.emit("vector", lambda v: v.tensor_single_scalar(rt[:, 0:8], probs[:, :], mx8[:, 1:2],
                                                          ALU.is_equal),
            reads=["probs", "mx8"], writes=["eq2"])
        p.emit("vector", lambda v, tt=tt: v.tensor_tensor(
            rt[:, 0:8], rt[:, 0:8], m1a[:, NEXP * tt:NEXP * (tt + 1)], ALU.subtract),
            reads=["eq2", f"m1a{tt}"], writes=["eq2b"])
        p.emit("vector", lambda v: v.tensor_scalar_max(rt[:, 0:8], rt[:, 0:8], 0.0),
               reads=["eq2b"], writes=["eq2c"])
        p.emit("vector", lambda v: v.tensor_tensor(rt[:, 0:8], rt[:, 0:8], revio[:, :], ALU.mult),
               reads=["eq2c", "revio"], writes=["eq2r"])
        p.emit("vector", lambda v: v.reduce_max(sm[:, 4:5], rt[:, 0:8], axis=mybir.AxisListType.X),
               reads=["eq2r"], writes=["m2v"])
        p.emit("vector", lambda v: v.tensor_scalar(sm[:, 4:5], sm[:, 4:5], -1.0, 99.0,
                                                   ALU.mult, ALU.add),
               reads=["m2v"], writes=["idx2"])
        p.emit("vector", lambda v, tt=tt: v.tensor_single_scalar(
            m2a[:, NEXP * tt:NEXP * (tt + 1)], iota8[:, :], sm[:, 4:5], ALU.is_equal),
            reads=["iota8", "idx2"], writes=[f"m2a{tt}"])
        p.emit("vector", lambda v, tt=tt: v.tensor_copy(gi[:, 4 * tt:4 * tt + 1], mx8[:, 0:1]),
               reads=["mx8"], writes=[f"gi{tt}"])
        p.emit("vector", lambda v, tt=tt: v.tensor_copy(gi[:, 4 * tt + 1:4 * tt + 2], mx8[:, 1:2]),
               reads=["mx8"], writes=[f"gi{tt}b"])
        p.emit("vector", lambda v, tt=tt: v.tensor_copy(gi[:, 4 * tt + 2:4 * tt + 3], sm[:, 3:4]),
               reads=["idx1"], writes=[f"gi{tt}c"])
        p.emit("vector", lambda v, tt=tt: v.tensor_copy(gi[:, 4 * tt + 3:4 * tt + 4], sm[:, 4:5]),
               reads=["idx2"], writes=[f"gi{tt}d"])
    for tt in range(16):
        def trf(t, tt=tt):
            t.drain()
            p.pe_base = None
            t.transpose(ps_t[0:NEXP, 128 * tt:128 * (tt + 1)],
                        m1a[:, NEXP * tt:NEXP * (tt + 1)], ident[:, :])
            return t.transpose(ps_t[0:NEXP, T + 128 * tt:T + 128 * (tt + 1)] if False else
                               ps_t[0:NEXP, 128 * tt:128 * (tt + 1)],
                               m1a[:, NEXP * tt:NEXP * (tt + 1)], ident[:, :])
        def trf2(t, tt=tt):
            t.drain()
            p.pe_base = None
            return t.transpose(ps_t[0:NEXP, 128 * tt:128 * (tt + 1)],
                               m1a[:, NEXP * tt:NEXP * (tt + 1)], ident[:, :])
        p.emit("tensor", trf2, reads=[f"m1a{tt}", "ident"], writes=["ps_t"])
        p.emit("vector", lambda v, tt=tt: v.tensor_copy(
            em[:, 128 * tt:128 * (tt + 1)], ps_t[0:NEXP, 128 * tt:128 * (tt + 1)]),
            reads=["ps_t"], writes=["em1"])
        def trf3(t, tt=tt):
            t.drain()
            p.pe_base = None
            return t.transpose(ps_t[0:NEXP, 128 * tt:128 * (tt + 1)],
                               m2a[:, NEXP * tt:NEXP * (tt + 1)], ident[:, :])
        p.emit("tensor", trf3, reads=[f"m2a{tt}", "ident"], writes=["ps_t"])
        p.emit("vector", lambda v, tt=tt: v.tensor_copy(
            em[:, T + 128 * tt:T + 128 * (tt + 1)], ps_t[0:NEXP, 128 * tt:128 * (tt + 1)]),
            reads=["ps_t"], writes=["em2"])
    p.emit("vector", lambda v: v.tensor_tensor_scan(
        em[:, 0:T], em[:, 0:T], emones[:, :], 0.0, ALU.add, ALU.mult),
        reads=["em1", "emones"], writes=["em1"])
    p.emit("vector", lambda v: v.tensor_tensor_scan(
        em[:, T:2 * T], em[:, T:2 * T], emones[:, :], 0.0, ALU.add, ALU.mult),
        reads=["em2", "emones"], writes=["em2"])
    p.emit("vector", lambda v: v.tensor_copy(cnts[:, 0:1], em[:, T - 1:T]),
           reads=["em1"], writes=["cnts1"])
    p.emit("vector", lambda v: v.tensor_copy(cnts[:, 1:2], em[:, 2 * T - 1:2 * T]),
           reads=["em2"], writes=["cnts2"])
    p.emit("gpsimd", lambda g: g.dma_start(
        out=co[:, :].rearrange("k e -> e k"), in_=cnts[:, :],
        allow_slow_non_contiguous=True),
        reads=["cnts1", "cnts2"], writes=["co"], dma=True)
    for tt in range(16):
        def trb(t, tt=tt):
            t.drain()
            p.pe_base = None
            t.transpose(ps_b[:, 0:NEXP], em[0:NEXP, 128 * tt:128 * (tt + 1)], ident[0:NEXP, 0:NEXP])
            return t.transpose(ps_b[:, NEXP:2 * NEXP], em[0:NEXP, T + 128 * tt:T + 128 * (tt + 1)],
                               ident[0:NEXP, 0:NEXP])
        p.emit("tensor", trb, reads=["em1", "em2", "ident"], writes=["ps_b"])
        p.emit("vector", lambda v: v.tensor_copy(lct[:, 0:16], ps_b[:, 0:16]),
               reads=["ps_b"], writes=["lct"])
        p.emit("vector", lambda v, tt=tt: v.tensor_tensor(
            lct[:, 0:8], lct[:, 0:8], m1a[:, NEXP * tt:NEXP * (tt + 1)], ALU.mult),
            reads=["lct", f"m1a{tt}"], writes=["lctm1"])
        p.emit("vector", lambda v, tt=tt: v.tensor_tensor(
            lct[:, 8:16], lct[:, 8:16], m2a[:, NEXP * tt:NEXP * (tt + 1)], ALU.mult),
            reads=["lct", f"m2a{tt}"], writes=["lctm2"])
        p.emit("vector", lambda v: v.reduce_sum(rt[:, 4:5], lct[:, 0:8], axis=mybir.AxisListType.X),
               reads=["lctm1"], writes=["lc1"])
        p.emit("vector", lambda v: v.reduce_sum(rt[:, 5:6], lct[:, 8:16], axis=mybir.AxisListType.X),
               reads=["lctm2"], writes=["lc2"])
        p.emit("vector", lambda v, tt=tt: v.tensor_copy(rt[:, 0:4], gi[:, 4 * tt:4 * (tt + 1)]),
               reads=[f"gi{tt}", f"gi{tt}b", f"gi{tt}c", f"gi{tt}d"], writes=["rt03"])
        p.emit("vector", lambda v: v.memset(rt[:, 6:8], 0.0), writes=["rt67"])
        p.emit("gpsimd", lambda g, tt=tt: g.dma_start(
            out=ro[128 * tt:128 * (tt + 1), :], in_=rt[:, :]),
            reads=["rt03", "lc1", "lc2", "rt67"], writes=[f"ro{tt}"], dma=True)
    p.finalize()
    return nc



def build_moe_expert(nc):
    p = Prog(nc)
    TA = 8 * T
    x3a = nc.dram_tensor("x3all", [TA, D], F32, kind="ExternalInput").ap()
    idt = nc.dram_tensor("idt4", [2, 4, CAP], F32, kind="ExternalInput").ap()
    wi = nc.dram_tensor("wi_e", [D, 4 * D], F32, kind="ExternalInput").ap()
    wo = nc.dram_tensor("wo_e", [4 * D, D], F32, kind="ExternalInput").ap()
    eo = nc.dram_tensor("eoutT", [2, D, CAP], F32, kind="ExternalOutput").ap()
    x3b = nc.dram_tensor("x3b", [TA, D], BF16, kind="Internal").ap()
    ebuf = nc.dram_tensor("ebuf", [2 * CAP, D], BF16, kind="Internal").ap()

    wis = nc.alloc_sbuf_tensor("wis", [128, 8 * 4 * D], BF16)
    wos = nc.alloc_sbuf_tensor("wos", [128, 32 * D], BF16)
    hb = nc.alloc_sbuf_tensor("hb", [128, 32 * CAP], BF16)
    xg = nc.alloc_sbuf_tensor("xg", [128, 8 * CAP], BF16)
    xrow = nc.alloc_sbuf_tensor("xrow", [128, D], BF16)
    idtl = [nc.alloc_sbuf_tensor(f"idtl{j}", [128, 5], F32) for j in range(2)]
    idg = nc.alloc_sbuf_tensor("idg", [128, 5], F32)
    idgi = nc.alloc_sbuf_tensor("idgi", [128, 5], I32)
    ot = nc.alloc_sbuf_tensor("ot", [128, CAP], F32)

    ps_a = nc.alloc_psum_tensor("ps_a", [128, CAP], F32)
    ps_b = nc.alloc_psum_tensor("ps_b", [128, CAP], F32)

    for r in range(8):
        p.emit("gpsimd", lambda g, r=r: g.dma_start(
            out=x3b[T * r:T * (r + 1), :], in_=x3a[T * r:T * (r + 1), :]),
            writes=[f"x3b{r}"], dma=True)
    for kc in range(8):
        p.emit("gpsimd", lambda g, kc=kc: g.dma_start(
            out=wis[:, 4 * D * kc:4 * D * (kc + 1)], in_=wi[128 * kc:128 * (kc + 1), :]),
            writes=[f"wi{kc}"], dma=True)
    for fc in range(32):
        p.emit("gpsimd", lambda g, fc=fc: g.dma_start(
            out=wos[:, D * fc:D * (fc + 1)], in_=wo[128 * fc:128 * (fc + 1), :]),
            writes=[f"wo{fc}"], dma=True)

    for g in range(2):
        p.emit("sync", lambda s, g=g: s.dma_start(
            out=idtl[0][:, :], in_=idt[g, 0].rearrange("(p f) -> p f", p=128)),
            writes=["idtl0"], dma=True)
        p.emit("sync", lambda s, g=g: s.dma_start(
            out=idtl[1][:, :], in_=idt[g, 1].rearrange("(p f) -> p f", p=128)),
            writes=["idtl1"], dma=True)
        p.emit("vector", lambda v: v.tensor_tensor(idg[:, :], idtl[0][:, :], idtl[1][:, :],
                                                   ALU.min),
               reads=["idtl0", "idtl1"], writes=["idg"])
        p.emit("sync", lambda s, g=g: s.dma_start(
            out=idtl[0][:, :], in_=idt[g, 2].rearrange("(p f) -> p f", p=128)),
            writes=["idtl0"], dma=True)
        p.emit("sync", lambda s, g=g: s.dma_start(
            out=idtl[1][:, :], in_=idt[g, 3].rearrange("(p f) -> p f", p=128)),
            writes=["idtl1"], dma=True)
        p.emit("vector", lambda v: v.tensor_tensor(idg[:, :], idg[:, :], idtl[0][:, :], ALU.min),
               reads=["idg", "idtl0"], writes=["idg"])
        p.emit("vector", lambda v: v.tensor_tensor(idg[:, :], idg[:, :], idtl[1][:, :], ALU.min),
               reads=["idg", "idtl1"], writes=["idg"])
        p.emit("vector", lambda v: v.tensor_copy(idgi[:, :], idg[:, :]),
               reads=["idg"], writes=["idgi"])
        for b in range(5):
            p.emit("vector", lambda v: v.memset(xrow[:, :], 0.0), writes=["xrow"])
            p.emit("gpsimd", lambda gg, b=b: gg.indirect_dma_start(
                out=xrow[:, :], out_offset=None,
                in_=x3b[:, :], in_offset=bass.IndirectOffsetOnAxis(idgi[:, b:b + 1], 0),
                bounds_check=TA - 1, oob_is_err=False),
                reads=["idgi", "xrow"] + [f"x3b{r}" for r in range(8)],
                writes=["xrow"], dma=True)
            p.emit("gpsimd", lambda gg, g=g, b=b: gg.dma_start(
                out=ebuf.rearrange("(a s) d -> a s d", s=5)[128 * g:128 * (g + 1), b],
                in_=xrow[:, :]),
                reads=["xrow"], writes=[f"eb_{g}_{b}"], dma=True)

    for g in range(2):
        allsc = [f"eb_{g}_{b}" for b in range(5)]
        for fc in range(8):
            p.emit("sync", lambda s, g=g, fc=fc: s.dma_start(
                out=xg[:, CAP * fc:CAP * (fc + 1)],
                in_=ebuf[CAP * g:CAP * (g + 1), 128 * fc:128 * (fc + 1)], transpose=True),
                reads=allsc, writes=[f"xg{fc}"], dma=True)
        for fc in range(32):
            for nn in range(2):
                n0, n1 = (0, 512) if nn == 0 else (512, CAP)
                pairs = [(wis[:, 4 * D * kc + 128 * fc: 4 * D * kc + 128 * (fc + 1)],
                          xg[:, CAP * kc + n0:CAP * kc + n1]) for kc in range(8)]
                mm_acc(p, ps_a[:, n0:n1], pairs,
                       reads=[f"wi{kc}" for kc in range(8)] + [f"xg{kc}" for kc in range(8)],
                       writes=["ps_a"])
            p.emit("scalar", lambda s, fc=fc: s.activation(
                out=hb[:, CAP * fc:CAP * (fc + 1)], in_=ps_a[:, :], func=AF.Relu),
                reads=["ps_a"], writes=[f"hb{fc}"])
        for ec in range(8):
            for nn in range(2):
                n0, n1 = (0, 512) if nn == 0 else (512, CAP)
                pairs = [(wos[:, D * fc + 128 * ec: D * fc + 128 * (ec + 1)],
                          hb[:, CAP * fc + n0:CAP * fc + n1]) for fc in range(32)]
                mm_acc(p, ps_b[:, n0:n1], pairs,
                       reads=[f"wo{fc}" for fc in range(32)] + [f"hb{fc}" for fc in range(32)],
                       writes=["ps_b"])
            p.emit("vector", lambda v: v.tensor_copy(ot[:, :], ps_b[:, :]),
                   reads=["ps_b"], writes=["ot"])
            p.emit("gpsimd", lambda gg, g=g, ec=ec: gg.dma_start(
                out=eo[g, 128 * ec:128 * (ec + 1), :], in_=ot[:, :]),
                reads=["ot"], writes=[f"eo{g}_{ec}"], dma=True)
    p.finalize()
    return nc



def build_moe_combine(nc, dbg=False):
    p = Prog(nc)
    x3 = nc.dram_tensor("x3", [T, D], F32, kind="ExternalInput").ap()
    ro = nc.dram_tensor("route", [T, 8], F32, kind="ExternalInput").ap()
    ca = nc.dram_tensor("cntall", [8, 16], F32, kind="ExternalInput").ap()
    pm = nc.dram_tensor("Pmat", [8, 8], F32, kind="ExternalInput").ap()
    gm = nc.dram_tensor("Gmat", [8, 8], F32, kind="ExternalInput").ap()
    ea = nc.dram_tensor("eall", [NEXP * 2 * CAP, D], F32, kind="ExternalInput").ap()
    cinp = nc.dram_tensor("cid", [2], F32, kind="ExternalInput").ap()
    out = nc.dram_tensor("out", [T, D], F32, kind="ExternalOutput").ap()
    if dbg:
        slo = nc.dram_tensor("slo", [T, 2], F32, kind="ExternalOutput").ap()
        slo2 = nc.dram_tensor("slo2", [T, 2], I32, kind="ExternalOutput").ap()
        yo = nc.dram_tensor("yo", [T, D], F32, kind="ExternalOutput").ap()
        sdbg = nc.alloc_sbuf_tensor("sdbg", [128, 2], F32)

    rts = nc.alloc_sbuf_tensor("rts", [128, 8], F32)
    iota8 = nc.alloc_sbuf_tensor("iota8", [128, NEXP], F32)
    cnts = nc.alloc_sbuf_tensor("cnts", [8, 16], F32)
    pms = nc.alloc_sbuf_tensor("pms", [8, 16], F32)
    offt = nc.alloc_sbuf_tensor("offt", [8, 16], F32)
    orow = nc.alloc_sbuf_tensor("orow", [1, 16], F32)
    offbc = nc.alloc_sbuf_tensor("offbc", [128, 16], F32)
    onesrow = nc.alloc_sbuf_tensor("onesrow", [1, 128], F32)
    cidr = nc.alloc_sbuf_tensor("cidr", [1, 2], F32)
    iotaP = nc.alloc_sbuf_tensor("iotaP", [8, 1], F32)
    gof = nc.alloc_sbuf_tensor("gof", [128, 1], F32)
    wk1 = nc.alloc_sbuf_tensor("wk1", [128, 12], F32)
    slots = [nc.alloc_sbuf_tensor(f"slot{k}", [128, 1], I32) for k in range(2)]
    rows = [nc.alloc_sbuf_tensor(f"rows{k}", [128, D], F32) for k in range(2)]
    xts = nc.alloc_sbuf_tensor("xts", [128, D], F32)
    yt = nc.alloc_sbuf_tensor("yt", [128, D], F32)
    zt = nc.alloc_sbuf_tensor("zt", [128, D], F32)
    sqscr = nc.alloc_sbuf_tensor("sqscr", [128, D], F32)
    mu = nc.alloc_sbuf_tensor("mu", [128, 4], F32)
    epst = nc.alloc_sbuf_tensor("epst", [128, 1], F32)
    nhalf = nc.alloc_sbuf_tensor("nhalf", [128, 1], F32)
    ot = nc.alloc_sbuf_tensor("ot", [128, D], F32)
    ps_c = nc.alloc_psum_tensor("ps_c", [128, 16], F32)

    p.emit("vector", lambda v: v.memset(onesrow[:, :], 1.0), writes=["onesrow"])
    p.emit("vector", lambda v: v.memset(epst[:, :], 1e-5), writes=["epst"])
    p.emit("vector", lambda v: v.memset(nhalf[:, :], -0.5), writes=["nhalf"])
    p.emit("gpsimd", lambda g: g.iota(iota8[:, :], pattern=[[1, NEXP]], base=0,
                                      channel_multiplier=0,
                                      allow_small_or_imprecise_dtypes=True),
           writes=["iota8"])
    p.emit("sync", lambda s: s.dma_start(out=cnts[:, :], in_=ca[:, :]), writes=["cnts"], dma=True)
    p.emit("sync", lambda s: s.dma_start(out=pms[:, 0:8], in_=pm[:, :]), writes=["pms"], dma=True)
    p.emit("sync", lambda s: s.dma_start(out=pms[:, 8:16], in_=gm[:, :]), writes=["gms"], dma=True)
    p.emit("sync", lambda s: s.dma_start(out=cidr[:, :], in_=cinp[None, :]), writes=["cidr"], dma=True)
    p.emit("gpsimd", lambda g: g.iota(iotaP[:, :], pattern=[[0, 1]], base=0,
                                      channel_multiplier=1,
                                      allow_small_or_imprecise_dtypes=True),
           writes=["iotaP"])
    def offmm(t):
        p.mm(t, ps_c[0:8, 0:16], pms[:, 0:8], cnts[:, :], start=True, stop=True)
        return p.mm(t, ps_c[0:8, 8:16], pms[:, 8:16], cnts[:, 0:8], start=False, stop=True)
    p.emit("tensor", offmm, reads=["pms", "gms", "cnts"], writes=["ps_c"])
    p.emit("vector", lambda v: v.tensor_copy(offt[:, :], ps_c[0:8, 0:16]),
           reads=["ps_c"], writes=["offt"])

    for tile in range(16):
        if tile == 0:
            def cbc8(t):
                return p.mm(t, ps_c[0:8, 0:1], onesrow[0:1, 0:8], cidr[0:1, 0:1],
                            start=True, stop=True)
            p.emit("tensor", cbc8, reads=["cidr", "onesrow"], writes=["ps_c"])
            p.emit("vector", lambda v: v.tensor_copy(wk1[0:8, 1:2], ps_c[0:8, 0:1]),
                   reads=["ps_c"], writes=["cidbc8"])
            p.emit("vector", lambda v: v.tensor_tensor(
                wk1[0:8, 0:1], iotaP[:, :], wk1[0:8, 1:2], ALU.is_equal),
                reads=["iotaP", "cidbc8"], writes=["cmask"])
            def pickmm(t):
                return p.mm(t, ps_c[0:1, 0:16], wk1[0:8, 0:1], offt[:, :],
                            start=True, stop=True)
            p.emit("tensor", pickmm, reads=["offt", "cmask"], writes=["ps_c"])
            p.emit("vector", lambda v: v.tensor_copy(orow[0:1, 0:16], ps_c[0:1, 0:16]),
                   reads=["ps_c"], writes=["orow"])
            def obc(t):
                return p.mm(t, ps_c[:, 0:16], onesrow[:, :], orow[0:1, :], start=True, stop=True)
            p.emit("tensor", obc, reads=["orow", "onesrow"], writes=["ps_c"])
            p.emit("vector", lambda v: v.tensor_copy(offbc[:, :], ps_c[:, 0:16]),
                   reads=["ps_c"], writes=["offbc"])
            def gbc(t):
                return p.mm(t, ps_c[:, 0:1], onesrow[:, :], cidr[0:1, 1:2], start=True, stop=True)
            p.emit("tensor", gbc, reads=["cidr", "onesrow"], writes=["ps_c"])
            p.emit("vector", lambda v: v.tensor_copy(gof[:, :], ps_c[:, 0:1]),
                   reads=["ps_c"], writes=["gof"])
        p.emit("sync", lambda s, tile=tile: s.dma_start(
            out=rts[:, :], in_=ro[128 * tile:128 * (tile + 1), :]),
            writes=["rts"], dma=True)
        p.emit("sync", lambda s, tile=tile: s.dma_start(
            out=xts[:, :], in_=x3[128 * tile:128 * (tile + 1), :]),
            writes=["xts"], dma=True)
        for k in range(2):
            p.emit("vector", lambda v, k=k: v.tensor_single_scalar(
                wk1[:, 0:8], iota8[:, :], rts[:, 2 + k:3 + k], ALU.is_equal),
                reads=["iota8", "rts"], writes=["wm"])
            p.emit("vector", lambda v, k=k: v.tensor_tensor(
                wk1[:, 0:8], wk1[:, 0:8], offbc[:, 8 * k:8 * (k + 1)], ALU.mult),
                reads=["wm", "offbc"], writes=["wmo"])
            p.emit("vector", lambda v: v.reduce_sum(wk1[:, 8:9], wk1[:, 0:8],
                                                    axis=mybir.AxisListType.X),
                   reads=["wmo"], writes=["offpick"])
            p.emit("vector", lambda v, k=k: v.scalar_tensor_tensor(
                wk1[:, 9:10], wk1[:, 8:9], -1.0, rts[:, 4 + k:5 + k], ALU.add, ALU.add),
                reads=["offpick", "rts"], writes=["pos"])
            p.emit("vector", lambda v: v.tensor_single_scalar(
                wk1[:, 11:12], wk1[:, 9:10], float(CAP) - 0.5, ALU.is_gt),
                reads=["pos"], writes=["ov"])
            p.emit("vector", lambda v, k=k: v.scalar_tensor_tensor(
                wk1[:, 9:10], rts[:, 2 + k:3 + k], float(2 * CAP), wk1[:, 9:10],
                ALU.mult, ALU.add),
                reads=["pos", "rts"], writes=["pos"])
            p.emit("vector", lambda v: v.tensor_tensor(
                wk1[:, 9:10], wk1[:, 9:10], gof[:, :], ALU.add),
                reads=["pos", "gof"], writes=["pos"])
            p.emit("vector", lambda v: v.scalar_tensor_tensor(
                wk1[:, 9:10], wk1[:, 11:12], BIGOOB, wk1[:, 9:10], ALU.mult, ALU.add),
                reads=["pos", "ov"], writes=["pos"])
            p.emit("vector", lambda v, k=k: v.tensor_copy(slots[k][:, :], wk1[:, 9:10]),
                   reads=["pos"], writes=[f"slotf{k}"])
            if dbg:
                p.emit("vector", lambda v, k=k: v.tensor_copy(sdbg[:, k:k + 1], wk1[:, 9:10]),
                       reads=["pos"], writes=[f"sdbg{k}"])
        for k in range(2):
            p.emit("vector", lambda v, k=k: v.memset(rows[k][:, :], 0.0),
                   writes=[f"rows{k}"])
            p.emit("gpsimd", lambda gg, k=k, tile=tile: gg.indirect_dma_start(
                out=rows[k][:, :], out_offset=None,
                in_=ea[:, :], in_offset=bass.IndirectOffsetOnAxis(slots[k][:, :], 0),
                bounds_check=NEXP * 2 * CAP - 1, oob_is_err=False),
                reads=[f"slotf{k}", f"rows{k}"], writes=[f"rows{k}"], dma=True)
        p.emit("vector", lambda v: v.tensor_single_scalar(
            yt[:, :], rows[0][:, :], rts[:, 0:1], ALU.mult),
            reads=["rows0", "rts"], writes=["yt"])
        p.emit("vector", lambda v: v.tensor_single_scalar(
            sqscr[:, :], rows[1][:, :], rts[:, 1:2], ALU.mult),
            reads=["rows1", "rts"], writes=["y2"])
        p.emit("vector", lambda v: v.tensor_tensor(yt[:, :], yt[:, :], sqscr[:, :], ALU.add),
               reads=["yt", "y2"], writes=["yt"])
        if dbg:
            p.emit("gpsimd", lambda g, tile=tile: g.dma_start(
                out=slo[128 * tile:128 * (tile + 1), :], in_=sdbg[:, :]),
                reads=["sdbg0", "sdbg1"], writes=[f"slo{tile}"], dma=True)
            p.emit("gpsimd", lambda g, tile=tile: g.dma_start(
                out=slo2[128 * tile:128 * (tile + 1), 0:1], in_=slots[0][:, :]),
                reads=["slotf0"], writes=[f"slo2{tile}"], dma=True)
            p.emit("gpsimd", lambda g, tile=tile: g.dma_start(
                out=slo2[128 * tile:128 * (tile + 1), 1:2], in_=slots[1][:, :]),
                reads=["slotf1"], writes=[f"slo2b{tile}"], dma=True)
            p.emit("gpsimd", lambda g, tile=tile: g.dma_start(
                out=yo[128 * tile:128 * (tile + 1), :], in_=yt[:, :]),
                reads=["yt"], writes=[f"yo{tile}"], dma=True)
        p.emit("vector", lambda v: v.tensor_tensor(zt[:, :], yt[:, :], xts[:, :], ALU.add),
               reads=["yt", "xts"], writes=["ztl"])
        p.emit("vector", lambda v: v.reduce_sum(mu[:, 0:1], zt[:, :], axis=mybir.AxisListType.X),
               reads=["ztl"], writes=["mu0"])
        p.emit("vector", lambda v: v.tensor_scalar_mul(mu[:, 0:1], mu[:, 0:1], 1.0 / D),
               reads=["mu0"], writes=["mu0"])
        p.emit("scalar", lambda s: s.activation(out=sqscr[:, :], in_=zt[:, :], func=AF.Square,
                                                accum_out=mu[:, 1:2]),
               reads=["ztl"], writes=["mu1", "sqscr"])
        p.emit("vector", lambda v: v.tensor_scalar_mul(mu[:, 1:2], mu[:, 1:2], 1.0 / D),
               reads=["mu1"], writes=["mu1"])
        p.emit("vector", lambda v: v.tensor_tensor(mu[:, 2:3], mu[:, 0:1], mu[:, 0:1], ALU.mult),
               reads=["mu0"], writes=["mu2"])
        p.emit("vector", lambda v: v.tensor_tensor(mu[:, 2:3], mu[:, 1:2], mu[:, 2:3], ALU.subtract),
               reads=["mu1", "mu2"], writes=["mu2"])
        p.emit("scalar", lambda s: s.activation(out=mu[:, 2:3], in_=mu[:, 2:3], func=AF.Ln,
                                                bias=epst[:, 0:1]),
               reads=["mu2"], writes=["mu2"])
        p.emit("scalar", lambda s: s.activation(out=mu[:, 3:4], in_=mu[:, 2:3], func=AF.Exp,
                                                scale=nhalf[:, 0:1]),
               reads=["mu2", "nhalf"], writes=["mu3"])
        p.emit("vector", lambda v: v.tensor_scalar(
            ot[:, :], zt[:, :], mu[:, 0:1], mu[:, 3:4], ALU.subtract, ALU.mult),
            reads=["ztl", "mu0", "mu3"], writes=["otw"])
        p.emit("gpsimd", lambda g, tile=tile: g.dma_start(
            out=out[128 * tile:128 * (tile + 1), :], in_=ot[:, :]),
            reads=["otw"], writes=[f"out{tile}"], dma=True)
    p.finalize()
    return nc



BIGID = 1 << 28


def build_moe_idtab(nc):
    p = Prog(nc)
    ro = nc.dram_tensor("route", [T, 8], F32, kind="ExternalInput").ap()
    ca = nc.dram_tensor("cntall", [8, 16], F32, kind="ExternalInput").ap()
    pm = nc.dram_tensor("Pmat", [8, 8], F32, kind="ExternalInput").ap()
    gm = nc.dram_tensor("Gmat", [8, 8], F32, kind="ExternalInput").ap()
    cinp = nc.dram_tensor("cid", [2], F32, kind="ExternalInput").ap()
    it = nc.dram_tensor("idtab", [NEXP * CAP, 1], F32, kind="ExternalOutput").ap()

    rts = nc.alloc_sbuf_tensor("rts", [128, 8], F32)
    iota8 = nc.alloc_sbuf_tensor("iota8", [128, NEXP], F32)
    iotaP = nc.alloc_sbuf_tensor("iotaP", [8, 1], F32)
    cnts = nc.alloc_sbuf_tensor("cnts", [8, 16], F32)
    pms = nc.alloc_sbuf_tensor("pms", [8, 16], F32)
    offt = nc.alloc_sbuf_tensor("offt", [8, 16], F32)
    orow = nc.alloc_sbuf_tensor("orow", [1, 16], F32)
    offbc = nc.alloc_sbuf_tensor("offbc", [128, 16], F32)
    onesrow = nc.alloc_sbuf_tensor("onesrow", [1, 128], F32)
    cidr = nc.alloc_sbuf_tensor("cidr", [1, 2], F32)
    wk1 = nc.alloc_sbuf_tensor("wk1", [128, 12], F32)
    slot = nc.alloc_sbuf_tensor("slot", [128, 2], I32)
    tokid = nc.alloc_sbuf_tensor("tokid", [128, 1], F32)
    cidTbc = nc.alloc_sbuf_tensor("cidTbc", [128, 1], F32)
    bigt = nc.alloc_sbuf_tensor("bigt", [128, 40], F32)
    ps_c = nc.alloc_psum_tensor("ps_c", [128, 16], F32)

    p.emit("vector", lambda v: v.memset(onesrow[:, :], 1.0), writes=["onesrow"])
    p.emit("gpsimd", lambda g: g.iota(iota8[:, :], pattern=[[1, NEXP]], base=0,
                                      channel_multiplier=0,
                                      allow_small_or_imprecise_dtypes=True),
           writes=["iota8"])
    p.emit("gpsimd", lambda g: g.iota(iotaP[:, :], pattern=[[0, 1]], base=0,
                                      channel_multiplier=1,
                                      allow_small_or_imprecise_dtypes=True),
           writes=["iotaP"])
    p.emit("vector", lambda v: v.memset(bigt[:, :], BIGID), writes=["bigt"])
    p.emit("gpsimd", lambda g: g.dma_start(
        out=it[:, :].rearrange("(p f) o -> p (f o)", p=128), in_=bigt[:, :]),
        reads=["bigt"], writes=["it_init"], dma=True)
    p.emit("sync", lambda s: s.dma_start(out=cnts[:, :], in_=ca[:, :]), writes=["cnts"], dma=True)
    p.emit("sync", lambda s: s.dma_start(out=pms[:, 0:8], in_=pm[:, :]), writes=["pms"], dma=True)
    p.emit("sync", lambda s: s.dma_start(out=pms[:, 8:16], in_=gm[:, :]), writes=["gms"], dma=True)
    p.emit("sync", lambda s: s.dma_start(out=cidr[:, :], in_=cinp[None, :]), writes=["cidr"], dma=True)
    def offmm(t):
        p.mm(t, ps_c[0:8, 0:16], pms[:, 0:8], cnts[:, :], start=True, stop=True)
        return p.mm(t, ps_c[0:8, 8:16], pms[:, 8:16], cnts[:, 0:8], start=False, stop=True)
    p.emit("tensor", offmm, reads=["pms", "gms", "cnts"], writes=["ps_c"])
    p.emit("vector", lambda v: v.tensor_copy(offt[:, :], ps_c[0:8, 0:16]),
           reads=["ps_c"], writes=["offt"])
    def cbc8(t):
        return p.mm(t, ps_c[0:8, 0:1], onesrow[0:1, 0:8], cidr[0:1, 0:1], start=True, stop=True)
    p.emit("tensor", cbc8, reads=["cidr", "onesrow"], writes=["ps_c"])
    p.emit("vector", lambda v: v.tensor_copy(wk1[0:8, 1:2], ps_c[0:8, 0:1]),
           reads=["ps_c"], writes=["cidbc8"])
    p.emit("vector", lambda v: v.tensor_tensor(
        wk1[0:8, 0:1], iotaP[:, :], wk1[0:8, 1:2], ALU.is_equal),
        reads=["iotaP", "cidbc8"], writes=["cmask"])
    def pickmm(t):
        return p.mm(t, ps_c[0:1, 0:16], wk1[0:8, 0:1], offt[:, :], start=True, stop=True)
    p.emit("tensor", pickmm, reads=["offt", "cmask"], writes=["ps_c"])
    p.emit("vector", lambda v: v.tensor_copy(orow[0:1, 0:16], ps_c[0:1, 0:16]),
           reads=["ps_c"], writes=["orow"])
    def obc(t):
        return p.mm(t, ps_c[:, 0:16], onesrow[:, :], orow[0:1, :], start=True, stop=True)
    p.emit("tensor", obc, reads=["orow", "onesrow"], writes=["ps_c"])
    p.emit("vector", lambda v: v.tensor_copy(offbc[:, :], ps_c[:, 0:16]),
           reads=["ps_c"], writes=["offbc"])
    def cbc(t):
        return p.mm(t, ps_c[:, 0:1], onesrow[:, :], cidr[0:1, 1:2], start=True, stop=True)
    p.emit("tensor", cbc, reads=["cidr", "onesrow"], writes=["ps_c"])
    p.emit("vector", lambda v: v.tensor_copy(cidTbc[:, :], ps_c[:, 0:1]),
           reads=["ps_c"], writes=["cidTbc"])

    for tile in range(16):
        p.emit("sync", lambda s, tile=tile: s.dma_start(
            out=rts[:, :], in_=ro[128 * tile:128 * (tile + 1), :]),
            writes=["rts"], dma=True)
        p.emit("gpsimd", lambda g, tile=tile: g.iota(
            tokid[:, :], pattern=[[0, 1]], base=128 * tile, channel_multiplier=1,
            allow_small_or_imprecise_dtypes=True),
            writes=["tokid_i"])
        p.emit("vector", lambda v: v.tensor_tensor(
            tokid[:, :], tokid[:, :], cidTbc[:, :], ALU.add),
            reads=["tokid_i", "cidTbc"], writes=["tokid"])
        for k in range(2):
            p.emit("vector", lambda v, k=k: v.tensor_single_scalar(
                wk1[:, 0:8], iota8[:, :], rts[:, 2 + k:3 + k], ALU.is_equal),
                reads=["iota8", "rts"], writes=["wm"])
            p.emit("vector", lambda v, k=k: v.tensor_tensor(
                wk1[:, 0:8], wk1[:, 0:8], offbc[:, 8 * k:8 * (k + 1)], ALU.mult),
                reads=["wm", "offbc"], writes=["wmo"])
            p.emit("vector", lambda v: v.reduce_sum(wk1[:, 8:9], wk1[:, 0:8],
                                                    axis=mybir.AxisListType.X),
                   reads=["wmo"], writes=["offpick"])
            p.emit("vector", lambda v, k=k: v.scalar_tensor_tensor(
                wk1[:, 9:10], wk1[:, 8:9], -1.0, rts[:, 4 + k:5 + k], ALU.add, ALU.add),
                reads=["offpick", "rts"], writes=["pos"])
            p.emit("vector", lambda v: v.tensor_single_scalar(
                wk1[:, 11:12], wk1[:, 9:10], float(CAP) - 0.5, ALU.is_gt),
                reads=["pos"], writes=["ov"])
            p.emit("vector", lambda v, k=k: v.scalar_tensor_tensor(
                wk1[:, 9:10], rts[:, 2 + k:3 + k], float(CAP), wk1[:, 9:10],
                ALU.mult, ALU.add),
                reads=["pos", "rts"], writes=["pos"])
            p.emit("vector", lambda v: v.scalar_tensor_tensor(
                wk1[:, 9:10], wk1[:, 11:12], BIGOOB, wk1[:, 9:10], ALU.mult, ALU.add),
                reads=["pos", "ov"], writes=["pos"])
            p.emit("vector", lambda v, k=k: v.tensor_copy(slot[:, k:k + 1], wk1[:, 9:10]),
                   reads=["pos"], writes=[f"slotk{k}"])
        for k in range(2):
            p.emit("gpsimd", lambda gg, k=k, tile=tile: gg.indirect_dma_start(
                out=it[:, :], out_offset=bass.IndirectOffsetOnAxis(slot[:, k:k + 1], 0),
                in_=tokid[:, 0:1], in_offset=None,
                bounds_check=NEXP * CAP - 1, oob_is_err=False),
                reads=[f"slotk{k}", "tokid", "it_init"], writes=[f"sc_{tile}_{k}"], dma=True)
    p.finalize()
    return nc



_BUILD_CACHE = {}


def _get(name, builder, *args):
    if name not in _BUILD_CACHE:
        _BUILD_CACHE[name] = builder(
            bass.Bass("TRN2", target_bir_lowering=False, debug=False), *args)
    return _BUILD_CACHE[name]


def _ct(a):
    return np.ascontiguousarray(a)


def kernel(L0_attn_x, L0_attn_past_kv, L0_attn_W_qkv, L0_attn_gn_weight,
           L0_attn_gn_bias, L0_ffn_W_fc, L0_ffn_ln_w, L0_ffn_ln_b,
           L0_ffn_W_proj, L1_attn_Wq, L1_attn_Wk, L1_attn_Wv,
           L1_attn_lambda_params, L1_attn_g1, L1_attn_g2, L1_attn_g3,
           L1_attn_W1, L1_attn_W2, L1_ffn_w_router, L1_ffn_wi, L1_ffn_wo):
    cores = list(range(8))
    B = 8
    x = np.asarray(L0_attn_x, dtype=np.float32)
    pkv = np.asarray(L0_attn_past_kv, dtype=np.float32)
    Wqkv = np.asarray(L0_attn_W_qkv, dtype=np.float32)
    gnw = np.asarray(L0_attn_gn_weight, dtype=np.float32)
    gnb = np.asarray(L0_attn_gn_bias, dtype=np.float32)
    Wfc = np.asarray(L0_ffn_W_fc, dtype=np.float32)
    lnw = np.asarray(L0_ffn_ln_w, dtype=np.float32)
    lnb = np.asarray(L0_ffn_ln_b, dtype=np.float32)
    Wpr = np.asarray(L0_ffn_W_proj, dtype=np.float32)
    Wq1 = np.asarray(L1_attn_Wq, dtype=np.float32)
    Wk1 = np.asarray(L1_attn_Wk, dtype=np.float32)
    Wv1 = np.asarray(L1_attn_Wv, dtype=np.float32)
    lam = np.asarray(L1_attn_lambda_params, dtype=np.float32)
    g1 = np.asarray(L1_attn_g1, dtype=np.float32)
    g2 = np.asarray(L1_attn_g2, dtype=np.float32)
    g3 = np.asarray(L1_attn_g3, dtype=np.float32)
    W1 = np.asarray(L1_attn_W1, dtype=np.float32)
    W2 = np.asarray(L1_attn_W2, dtype=np.float32)
    wr = np.asarray(L1_ffn_w_router, dtype=np.float32)
    wi = np.asarray(L1_ffn_wi, dtype=np.float32)
    wo = np.asarray(L1_ffn_wo, dtype=np.float32)

    nc = _get("l0qkv", build_l0_qkv)
    r = run_bass_kernel_spmd(nc, [{"xT": _ct(x[b].T), "w": Wqkv} for b in range(B)],
                             core_ids=cores).results
    M, E, G, Dg = l0_consts()
    nc = _get("l0ret", build_l0_ret)
    in2 = [{"q_fm": r[b]["q_fm"], "k_fm": r[b]["k_fm"],
            "v_tm": _ct(r[b]["v_fm"].T), "k_tm": _ct(r[b]["k_fm"].T),
            "pkv": pkv, "cM": M, "cE": E, "cG": G, "cDg": Dg} for b in range(B)]
    r2 = run_bass_kernel_spmd(nc, in2, core_ids=cores).results
    nc = _get("l0gn", build_l0_gn)
    in3 = [{"x": x[b], "attnraw": r2[b]["attnraw"], "gnw": gnw, "gnb": gnb}
           for b in range(B)]
    r3 = run_bass_kernel_spmd(nc, in3, core_ids=cores).results

    nc = _get("qfc", build_quant, "wT", D, 4 * D)
    rq1 = run_bass_kernel_spmd(nc, [{"wT": _ct(Wfc.T)}] * B, core_ids=cores).results
    nc = _get("qpr", build_quant, "wT", 4 * D, D)
    rq2 = run_bass_kernel_spmd(nc, [{"wT": _ct(Wpr.T)}] * B, core_ids=cores).results
    nc = _get("l0ffn", build_l0_ffn)
    in4 = [{"x1T": _ct(r3[b]["x1"].T), "wfcT_q": rq1[b]["wT_q"],
            "wprT_q": rq2[b]["wT_q"], "lnw": lnw, "lnb": lnb} for b in range(B)]
    r4 = run_bass_kernel_spmd(nc, in4, core_ids=cores).results

    cMA = l1_consts()
    nc = _get("l1attn", build_l1_attn)
    in5 = [{"x2T": r4[b]["x2fm"], "wqT": _ct(Wq1.T), "wkT": _ct(Wk1.T),
            "wvT": _ct(Wv1.T), "lam": lam, "g1": g1, "g2": g2, "g3": g3,
            "cMA": cMA} for b in range(B)]
    r5 = run_bass_kernel_spmd(nc, in5, core_ids=cores).results
    nc = _get("l1ffn", build_l1_ffn)
    in6 = [{"a3T": _ct(r5[b]["attn3"].T), "x2T": r4[b]["x2fm"],
            "w1T": _ct(W1.T), "w2T": _ct(W2.T)} for b in range(B)]
    r6 = run_bass_kernel_spmd(nc, in6, core_ids=cores).results

    Pm, Gm = moe_consts()
    nc = _get("moer", build_moe_route)
    in7 = [{"x3T": r6[b]["x3fm"], "wr": wr} for b in range(B)]
    r7 = run_bass_kernel_spmd(nc, in7, core_ids=cores).results
    cnt_all = np.stack([r7[b]["cnt"] for b in range(B)]).reshape(8, 16)
    nc = _get("moei", build_moe_idtab)
    in8 = [{"route": r7[b]["route"], "cntall": cnt_all, "Pmat": Pm, "Gmat": Gm,
            "cid": np.array([b, b * T], np.float32)} for b in range(B)]
    r8 = run_bass_kernel_spmd(nc, in8, core_ids=cores).results
    idtabs = np.stack([r8[b]["idtab"][:, 0] for b in range(B)])
    x3_tm = np.stack([r6[b]["x3fm"].T for b in range(B)])
    x3all = _ct(x3_tm.reshape(8 * T, D))
    nc = _get("moee", build_moe_expert)
    in9 = []
    for e in range(8):
        idt4 = np.stack([
            np.stack([idtabs[4 * g + c4, e * CAP:(e + 1) * CAP] for c4 in range(4)])
            for g in range(2)])
        in9.append({"x3all": x3all, "idt4": _ct(idt4.astype(np.float32)),
                    "wi_e": _ct(wi[e]), "wo_e": _ct(wo[e])})
    r9 = run_bass_kernel_spmd(nc, in9, core_ids=cores).results
    eall = np.stack([np.transpose(r9[e]["eoutT"], (0, 2, 1)) for e in range(8)])
    eall = _ct(eall.reshape(NEXP * 2 * CAP, D))
    nc = _get("moec", build_moe_combine)
    in10 = [{"x3": _ct(x3_tm[b]), "route": r7[b]["route"], "cntall": cnt_all,
             "Pmat": Pm, "Gmat": Gm, "eall": eall,
             "cid": np.array([b, CAP * (b // 4)], np.float32)} for b in range(B)]
    r10 = run_bass_kernel_spmd(nc, in10, core_ids=cores).results

    return np.stack([r10[b]["out"] for b in range(B)])


# version 03
# speedup vs baseline: 1.4038x; 1.4038x over previous
import math
import numpy as np
import ml_dtypes
import concourse.bass as bass
import concourse.mybir as mybir
from concourse.bass_utils import run_bass_kernel_spmd

F32 = mybir.dt.float32
BF16 = mybir.dt.bfloat16
I32 = mybir.dt.int32
AF = mybir.ActivationFunctionType
ALU = mybir.AluOpType

ENGINES = ("sync", "tensor", "vector", "scalar", "gpsimd")


DMA_POOL = 16


class Prog:

    def __init__(self, nc):
        self.nc = nc
        self.count = {e: 0 for e in ENGINES}
        self.dma_idx = {e: 0 for e in ENGINES}
        self.ops = {e: [] for e in ENGINES}
        self.writers = {}
        self.readers = {}
        self.hw = {e: {} for e in ENGINES}
        self.sems = {}
        self.sem_ids = set(ENGINES)
        self.pe_base = None

    def mm(self, t, out, lhsT, rhs, **kw):
        key = (lhsT.base_partition(), rhs.base_partition())
        if self.pe_base != key:
            t.drain()
        self.pe_base = key
        return t.matmul(out, lhsT, rhs, **kw)

    def emit(self, eng, fn, reads=(), writes=(), dma=False):
        deps = []
        for k in reads:
            w = self.writers.get(k)
            if w:
                deps.append(w)
        for k in writes:
            w = self.writers.get(k)
            if w:
                deps.append(w)
            for de, dv in self.readers.get(k, {}).items():
                deps.append((de, dv))
        if dma:
            idx = self.dma_idx[eng]
            self.dma_idx[eng] = idx + 1
            slot = idx % DMA_POOL
            cycle = idx // DMA_POOL
            sem_id = f"{eng}dma{slot}"
            self.sem_ids.add(sem_id)
            if cycle > 0:
                deps.append((sem_id, 16 * cycle))
            val = 16 * (cycle + 1)
            my_id = sem_id
        else:
            self.count[eng] += 1
            val = self.count[eng]
            my_id = eng
        waits = {}
        for de, dv in deps:
            if de == "tensor" and eng == "tensor":
                continue
            if dv > self.hw[eng].get(de, 0):
                waits[de] = max(waits.get(de, 0), dv)
        for de, dv in waits.items():
            self.hw[eng][de] = dv
        self.ops[eng].append((fn, waits, my_id, 16 if dma else 1))
        for k in reads:
            cur = self.readers.setdefault(k, {})
            cur[my_id] = max(cur.get(my_id, 0), val)
        for k in writes:
            self.writers[k] = (my_id, val)
            self.readers[k] = {}
        return my_id, val

    def barrier_key(self, key, eng="gpsimd"):
        w = self.writers.get(key)
        if w:
            de, dv = w
            if dv > self.hw[eng].get(de, 0):
                self.hw[eng][de] = dv
                self.ops[eng].append((None, {de: dv}, eng, 0))

    def finalize(self, join_eng="sync"):
        nc = self.nc
        final = dict(self.count)
        for e in ENGINES:
            n = self.dma_idx[e]
            for s in range(min(n, DMA_POOL)):
                ncyc = (n - s + DMA_POOL - 1) // DMA_POOL
                final[f"{e}dma{s}"] = 16 * ncyc
        ctxs = []
        opened = {}
        for sid in sorted(self.sem_ids):
            c = nc.semaphore(f"sem_{sid}")
            ctxs.append(c)
            opened[sid] = c.__enter__()
        self.sems = opened
        blk = nc.Block()
        block = blk.__enter__()
        for e in ENGINES:
            ops = self.ops[e]
            need_join = (e == join_eng)
            if not ops and not need_join:
                continue

            def body(engobj, e=e, ops=ops, need_join=need_join):
                for fn, waits, my_id, per in ops:
                    for de, dv in waits.items():
                        engobj.wait_ge(self.sems[de], dv)
                    if fn is not None:
                        instr = fn(engobj)
                        instr.then_inc(self.sems[my_id], per)
                if need_join:
                    for sid, fv in final.items():
                        if sid != e and fv > self.hw[e].get(sid, 0):
                            engobj.wait_ge(self.sems[sid], fv)

            getattr(block, e)(body)
        blk.__exit__(None, None, None)
        for c in ctxs:
            c.__exit__(None, None, None)




def ceil_div(a, b):
    return (a + b - 1) // b


def mm_acc(p, psum_ap, pairs, writes, reads=(), accum=False):
    def fn(t):
        last = None
        npairs = len(pairs)
        for i, (l, r) in enumerate(pairs):
            last = p.mm(t, psum_ap, l, r,
                        start=(i == 0 and not accum),
                        stop=(i == npairs - 1))
        return last
    return p.emit("tensor", fn, reads=reads, writes=writes)


def bcast_row(p, psum_ap, ones_col, row_ap, reads, writes, accum=False):
    return mm_acc(p, psum_ap, [(ones_col, row_ap)], writes=writes, reads=reads,
                  accum=accum)


def np_bf16(x):
    return np.asarray(x).astype(ml_dtypes.bfloat16)


"""Per-launch builders for the mk_seq_postln_d2 module (v0 multi-launch)."""

T = 2048
D = 1024
NH = 16
HD = 64
CH = 128
NCH = T // CH



def l0_consts():
    g = 1.0 - 2.0 ** (-5.0 - np.arange(NH, dtype=np.float64))
    tp = np.arange(CH)
    M = np.zeros((CH, NH * CH), np.float32)
    for h in range(NH):
        diff = tp[None, :] - tp[:, None]
        blk = np.where(diff >= 0, g[h] ** np.maximum(diff, 0), 0.0)
        M[:, h * CH:(h + 1) * CH] = blk.astype(np.float32)
    E = np.zeros((CH, D), np.float32)
    for h in range(NH):
        E[:, h * HD:(h + 1) * HD] = (g[h] ** (CH - 1 - tp))[:, None]
    G = np.zeros((128, 512), np.float32)
    for h in range(NH):
        par, hp = h % 2, h // 2
        G[64 * par:64 * (par + 1), HD * hp:HD * (hp + 1)] = g[h] ** CH
    Dg = np.zeros((CH, D), np.float32)
    for h in range(NH):
        Dg[:, h * HD:(h + 1) * HD] = (g[h] ** (tp + 1))[:, None]
    return M, E, G, Dg



def build_l0_qkv(nc):
    p = Prog(nc)
    xT = nc.dram_tensor("xT", [D, T], F32, kind="ExternalInput").ap()
    w = nc.dram_tensor("w", [D, 3 * D], F32, kind="ExternalInput").ap()
    outs = [nc.dram_tensor(nm, [D, T], BF16, kind="ExternalOutput").ap()
            for nm in ("q_fm", "k_fm", "v_fm")]

    xfm = nc.alloc_sbuf_tensor("xfm", [128, 8 * T], BF16)
    ws = nc.alloc_sbuf_tensor("ws", [128, 8 * 3 * D], BF16)
    osb = [nc.alloc_sbuf_tensor(f"osb{j}", [128, T], BF16) for j in range(2)]
    ps = [nc.alloc_psum_tensor(f"ps{j}", [128, T], F32) for j in range(2)]

    for kc in range(8):
        p.emit("gpsimd", lambda g, kc=kc: g.dma_start(
            out=xfm[:, T * kc:T * (kc + 1)], in_=xT[128 * kc:128 * (kc + 1), :]),
            writes=[f"xfm{kc}"], dma=True)
        p.emit("gpsimd", lambda g, kc=kc: g.dma_start(
            out=ws[:, 3 * D * kc:3 * D * (kc + 1)], in_=w[128 * kc:128 * (kc + 1), :]),
            writes=[f"ws{kc}"], dma=True)

    NT = T // 512
    for ec in range(24):
        j = ec % 2
        for tt in range(NT):
            pairs = [(ws[:, 3 * D * kc + 128 * ec: 3 * D * kc + 128 * (ec + 1)],
                      xfm[:, T * kc + 512 * tt: T * kc + 512 * (tt + 1)])
                     for kc in range(8)]
            mm_acc(p, ps[j][:, 512 * tt:512 * (tt + 1)], pairs,
                   reads=[f"ws{kc}" for kc in range(8)] + [f"xfm{kc}" for kc in range(8)],
                   writes=[f"ps{j}"])
        eng = "vector" if j == 0 else "scalar"
        if eng == "vector":
            p.emit(eng, lambda v, j=j: v.tensor_copy(osb[j][:, :], ps[j][:, :]),
                   reads=[f"ps{j}"], writes=[f"osb{j}"])
        else:
            p.emit(eng, lambda s, j=j: s.copy(osb[j][:, :], ps[j][:, :]),
                   reads=[f"ps{j}"], writes=[f"osb{j}"])
        o = outs[ec // 8]
        r0 = 128 * (ec % 8)
        p.emit("gpsimd", lambda g, j=j, o=o, r0=r0: g.dma_start(
            out=o[r0:r0 + 128, :], in_=osb[j][:, :]),
            reads=[f"osb{j}"], writes=[f"out{ec}"], dma=True)
    p.finalize()
    return nc



def build_l0_ret(nc):
    p = Prog(nc)
    qf = nc.dram_tensor("q_fm", [D, T], BF16, kind="ExternalInput").ap()
    kf = nc.dram_tensor("k_fm", [D, T], BF16, kind="ExternalInput").ap()
    vt = nc.dram_tensor("v_tm", [T, D], BF16, kind="ExternalInput").ap()
    kt = nc.dram_tensor("k_tm", [T, D], BF16, kind="ExternalInput").ap()
    pkv = nc.dram_tensor("pkv", [NH, HD, HD], F32, kind="ExternalInput").ap()
    cM = nc.dram_tensor("cM", [CH, NH * CH], F32, kind="ExternalInput").ap()
    cE = nc.dram_tensor("cE", [CH, D], F32, kind="ExternalInput").ap()
    cG = nc.dram_tensor("cG", [128, 512], F32, kind="ExternalInput").ap()
    cDg = nc.dram_tensor("cDg", [CH, D], F32, kind="ExternalInput").ap()
    ar = nc.dram_tensor("attnraw", [T, D], F32, kind="ExternalOutput").ap()

    qs = nc.alloc_sbuf_tensor("qs", [128, 8 * T], BF16)
    ks = nc.alloc_sbuf_tensor("ks", [128, 8 * T], BF16)
    vts = nc.alloc_sbuf_tensor("vts", [128, NCH * D], BF16)
    kts = nc.alloc_sbuf_tensor("kts", [128, NCH * D], BF16)
    pkvs = nc.alloc_sbuf_tensor("pkvs", [128, 512], F32)
    pkvb = nc.alloc_sbuf_tensor("pkvb", [128, 512], BF16)
    Ms = nc.alloc_sbuf_tensor("Ms", [CH, NH * CH], F32)
    Es = nc.alloc_sbuf_tensor("Es", [CH, D], F32)
    Gs = nc.alloc_sbuf_tensor("Gs", [128, 512], F32)
    Dgs = nc.alloc_sbuf_tensor("Dgs", [CH, D], F32)
    state = nc.alloc_sbuf_tensor("state", [128, 512], F32)
    stateb = nc.alloc_sbuf_tensor("stateb", [128, 512], BF16)
    kdecb = nc.alloc_sbuf_tensor("kdecb", [128, D], BF16)
    Sm = nc.alloc_sbuf_tensor("Sm", [128, NH * CH], BF16)
    scr1 = nc.alloc_sbuf_tensor("scr1", [128, D], F32)
    scr2 = nc.alloc_sbuf_tensor("scr2", [128, 512], F32)
    artile = nc.alloc_sbuf_tensor("artile", [128, D], F32)

    ps_S = nc.alloc_psum_tensor("ps_S", [128, 1024], F32)
    ps_in = nc.alloc_psum_tensor("ps_in", [128, 1024], F32)
    ps_cr = nc.alloc_psum_tensor("ps_cr", [128, 1024], F32)
    ps_su = nc.alloc_psum_tensor("ps_su", [128, 512], F32)

    for kc in range(8):
        p.emit("sync", lambda s, kc=kc: s.dma_start(
            out=qs[:, T * kc:T * (kc + 1)], in_=qf[128 * kc:128 * (kc + 1), :]),
            writes=[f"qs{kc}"], dma=True)
        p.emit("sync", lambda s, kc=kc: s.dma_start(
            out=ks[:, T * kc:T * (kc + 1)], in_=kf[128 * kc:128 * (kc + 1), :]),
            writes=[f"ks{kc}"], dma=True)
    for c in range(NCH):
        p.emit("sync", lambda s, c=c: s.dma_start(
            out=vts[:, D * c:D * (c + 1)], in_=vt[CH * c:CH * (c + 1), :]),
            writes=[f"vts{c}"], dma=True)
        p.emit("sync", lambda s, c=c: s.dma_start(
            out=kts[:, D * c:D * (c + 1)], in_=kt[CH * c:CH * (c + 1), :]),
            writes=[f"kts{c}"], dma=True)
    for par in range(2):
        p.emit("sync", lambda s, par=par: s.dma_start(
            out=pkvs[64 * par:64 * (par + 1), :].rearrange("p (h e) -> p h e", h=NH // 2),
            in_=pkv.rearrange("(hp par) d e -> par d hp e", par=2)[par]),
            writes=["pkvs"], dma=True)
    p.emit("sync", lambda s: s.dma_start(out=Ms[:, :], in_=cM[:, :]), writes=["Ms"], dma=True)
    p.emit("sync", lambda s: s.dma_start(out=Es[:, :], in_=cE[:, :]), writes=["Es"], dma=True)
    p.emit("sync", lambda s: s.dma_start(out=Gs[:, :], in_=cG[:, :]), writes=["Gs"], dma=True)
    p.emit("sync", lambda s: s.dma_start(out=Dgs[:, :], in_=cDg[:, :]), writes=["Dgs"], dma=True)
    p.emit("vector", lambda v: v.tensor_scalar_mul(pkvb[:, :], pkvs[:, :], float(NH * HD)),
           reads=["pkvs"], writes=["pkvb"])
    p.emit("vector", lambda v: v.memset(state[:, :], 0.0), writes=["state"])

    def hslice(buf, h, c):
        ec = h // 2
        r0 = 64 * (h % 2)
        return buf[r0:r0 + 64, T * ec + CH * c: T * ec + CH * (c + 1)]

    for c in range(NCH):
        p.emit("vector", lambda v, c=c: v.tensor_tensor(
            kdecb[:, :], kts[:, D * c:D * (c + 1)], Es[:, :], ALU.mult),
            reads=[f"kts{c}", "Es"], writes=["kdecb"])
        p.emit("scalar", lambda s: s.copy(stateb[:, :], state[:, :]),
            reads=["state"], writes=["stateb"])

        for half in range(2):
            def smm(t, c=c, half=half):
                last = None
                for hh in range(8):
                    h = 8 * half + hh
                    last = p.mm(t, ps_S[:, CH * hh:CH * (hh + 1)],
                                    hslice(ks, h, c), hslice(qs, h, c),
                                    start=True, stop=True)
                return last
            p.emit("tensor", smm,
                   reads=[f"ks{kc}" for kc in range(8)] + [f"qs{kc}" for kc in range(8)],
                   writes=["ps_S"])
            p.emit("vector", lambda v, c=c, half=half: v.tensor_tensor(
                Sm[:, 1024 * half:1024 * (half + 1)], ps_S[:, :],
                Ms[:, 1024 * half:1024 * (half + 1)], ALU.mult),
                reads=["ps_S", "Ms"], writes=["Sm"])

        def imm(t, c=c):
            last = None
            for h in range(NH):
                last = p.mm(t, ps_in[:, HD * h:HD * (h + 1)],
                                Sm[:, CH * h:CH * (h + 1)],
                                vts[:, D * c + HD * h: D * c + HD * (h + 1)],
                                start=True, stop=False)
                last = p.mm(t, ps_in[:, HD * h:HD * (h + 1)],
                                hslice(qs, h, c),
                                pkvb[64 * (h % 2):64 * (h % 2) + 64, HD * (h // 2):HD * (h // 2) + 64],
                                start=False, stop=True)
            return last
        p.emit("tensor", imm,
               reads=["Sm", f"vts{c}", "pkvb"] + [f"qs{kc}" for kc in range(8)],
               writes=["ps_in"])

        def cmm(t, c=c):
            last = None
            for h in range(NH):
                last = p.mm(t, ps_cr[:, HD * h:HD * (h + 1)],
                                hslice(qs, h, c),
                                stateb[64 * (h % 2):64 * (h % 2) + 64, HD * (h // 2):HD * (h // 2) + 64],
                                start=True, stop=True)
            return last
        p.emit("tensor", cmm,
               reads=["stateb"] + [f"qs{kc}" for kc in range(8)], writes=["ps_cr"])

        def umm(t, c=c):
            last = None
            for h in range(NH):
                last = p.mm(t, ps_su[64 * (h % 2):64 * (h % 2) + 64,
                                      HD * (h // 2):HD * (h // 2) + 64],
                                kdecb[:, HD * h:HD * (h + 1)],
                                vts[:, D * c + HD * h: D * c + HD * (h + 1)],
                                start=True, stop=True,
                                tile_position=(0, 64 * (h % 2)))
            return last
        p.emit("tensor", umm, reads=["kdecb", f"vts{c}"], writes=["ps_su"])

        p.emit("vector", lambda v: v.tensor_tensor(scr1[:, :], ps_cr[:, :], Dgs[:, :], ALU.mult),
               reads=["ps_cr", "Dgs"], writes=["scr1"])
        p.emit("vector", lambda v: v.tensor_tensor(artile[:, :], scr1[:, :], ps_in[:, :], ALU.add),
               reads=["scr1", "ps_in"], writes=["artile"])
        p.emit("scalar", lambda s, c=c: s.dma_start(
            out=ar[CH * c:CH * (c + 1), :], in_=artile[:, :]),
            reads=["artile"], writes=[f"ar{c}"], dma=True)

        p.emit("vector", lambda v: v.tensor_tensor(scr2[:, :], state[:, :], Gs[:, :], ALU.mult),
               reads=["state", "Gs"], writes=["scr2"])
        p.emit("vector", lambda v: v.tensor_tensor(state[:, :], scr2[:, :], ps_su[:, :], ALU.add),
               reads=["scr2", "ps_su"], writes=["state"])
    p.finalize()
    return nc



def build_l0_gn(nc):
    p = Prog(nc)
    x = nc.dram_tensor("x", [T, D], F32, kind="ExternalInput").ap()
    ar = nc.dram_tensor("attnraw", [T, D], F32, kind="ExternalInput").ap()
    gnw = nc.dram_tensor("gnw", [NH], F32, kind="ExternalInput").ap()
    gnb = nc.dram_tensor("gnb", [NH], F32, kind="ExternalInput").ap()
    xo = nc.dram_tensor("x1", [T, D], F32, kind="ExternalOutput").ap()

    xs = nc.alloc_sbuf_tensor("xs", [128, D], F32)
    ars = nc.alloc_sbuf_tensor("ars", [128, NCH * D], F32)
    gnws = nc.alloc_sbuf_tensor("gnws", [1, NH], F32)
    gnbs = nc.alloc_sbuf_tensor("gnbs", [1, NH], F32)
    onescol = nc.alloc_sbuf_tensor("onescol", [128, 1], F32)
    onesrow = nc.alloc_sbuf_tensor("onesrow", [1, 128], BF16)
    colsum = nc.alloc_sbuf_tensor("colsum", [128, NCH], F32)
    colsq = nc.alloc_sbuf_tensor("colsq", [128, NCH], F32)
    sqscr = nc.alloc_sbuf_tensor("sqscr", [128, D], F32)
    redr = nc.alloc_sbuf_tensor("redr", [1, 2 * NCH], F32)
    stat = nc.alloc_sbuf_tensor("stat", [1, 8], F32)
    arow = nc.alloc_sbuf_tensor("arow", [1, D], F32)
    brow = nc.alloc_sbuf_tensor("brow", [1, D], F32)
    rowsb = nc.alloc_sbuf_tensor("rowsb", [1, 2 * D], BF16)
    zt = nc.alloc_sbuf_tensor("zt", [128, D], F32)
    z2 = nc.alloc_sbuf_tensor("z2", [128, D], F32)
    mu = nc.alloc_sbuf_tensor("mu", [128, 1], F32)
    var = nc.alloc_sbuf_tensor("var", [128, 1], F32)
    sq2 = nc.alloc_sbuf_tensor("sq2", [128, 1], F32)
    rstd = nc.alloc_sbuf_tensor("rstd", [128, 1], F32)
    ot = nc.alloc_sbuf_tensor("ot", [128, D], F32)

    epst = nc.alloc_sbuf_tensor("epst", [128, 1], F32)
    nhalf = nc.alloc_sbuf_tensor("nhalf", [128, 1], F32)
    ps_red = nc.alloc_psum_tensor("ps_red", [128, 512], F32)
    ps_ab = nc.alloc_psum_tensor("ps_ab", [128, 2 * D], F32)

    p.emit("sync", lambda s: s.dma_start(out=gnws[:, :], in_=gnw[None, :]), writes=["gnws"], dma=True)
    p.emit("sync", lambda s: s.dma_start(out=gnbs[:, :], in_=gnb[None, :]), writes=["gnbs"], dma=True)
    p.emit("vector", lambda v: v.memset(onescol[:, :], 1.0), writes=["onescol"])
    p.emit("vector", lambda v: v.memset(epst[:, :], 1e-5), writes=["epst"])
    p.emit("vector", lambda v: v.memset(nhalf[:, :], -0.5), writes=["nhalf"])
    p.emit("vector", lambda v: v.memset(onesrow[:, :], 1.0), writes=["onesrow"])
    for c in range(NCH):
        p.emit("sync", lambda s, c=c: s.dma_start(
            out=ars[:, D * c:D * (c + 1)], in_=ar[CH * c:CH * (c + 1), :]),
            writes=[f"ars{c}"], dma=True)
    for c in range(NCH):
        p.emit("vector", lambda v, c=c: v.reduce_sum(
            colsum[:, c:c + 1], ars[:, D * c:D * (c + 1)], axis=mybir.AxisListType.X),
            reads=[f"ars{c}"], writes=["colsum"])
        p.emit("scalar", lambda s, c=c: s.activation(
            out=sqscr[:, :], in_=ars[:, D * c:D * (c + 1)], func=AF.Square,
            accum_out=colsq[:, c:c + 1]),
            reads=[f"ars{c}"], writes=["colsq", "sqscr"])
    def rmm(t):
        p.mm(t, ps_red[0:1, 0:NCH], onescol[:, :], colsum[:, :], start=True, stop=True)
        return p.mm(t, ps_red[0:1, NCH:2 * NCH], onescol[:, :], colsq[:, :], start=True, stop=True)
    p.emit("tensor", rmm, reads=["colsum", "colsq", "onescol"], writes=["ps_red"])
    p.emit("vector", lambda v: v.tensor_copy(redr[:, :], ps_red[0:1, 0:2 * NCH]),
           reads=["ps_red"], writes=["redr"])
    p.emit("vector", lambda v: v.reduce_sum(stat[:, 0:1], redr[:, 0:NCH], axis=mybir.AxisListType.X),
           reads=["redr"], writes=["stat_s"])
    p.emit("vector", lambda v: v.reduce_sum(stat[:, 1:2], redr[:, NCH:2 * NCH], axis=mybir.AxisListType.X),
           reads=["redr"], writes=["stat_q"])
    NTOT = float(T * D)
    p.emit("vector", lambda v: v.tensor_scalar_mul(stat[:, 2:3], stat[:, 0:1], 1.0 / NTOT),
           reads=["stat_s"], writes=["stat_mean"])
    p.emit("vector", lambda v: v.tensor_scalar_mul(stat[:, 3:4], stat[:, 1:2], 1.0 / NTOT),
           reads=["stat_q"], writes=["stat_e2"])
    p.emit("vector", lambda v: v.tensor_tensor(stat[:, 4:5], stat[:, 2:3], stat[:, 2:3], ALU.mult),
           reads=["stat_mean"], writes=["stat_m2"])
    p.emit("vector", lambda v: v.tensor_tensor(stat[:, 5:6], stat[:, 3:4], stat[:, 4:5], ALU.subtract),
           reads=["stat_e2", "stat_m2"], writes=["stat_var"])
    p.emit("scalar", lambda s: s.activation(out=stat[:, 6:7], in_=stat[:, 5:6], func=AF.Ln,
                                            bias=epst[0:1, 0:1]),
           reads=["stat_var", "epst"], writes=["stat_sd"])
    p.emit("scalar", lambda s: s.activation(out=stat[:, 7:8], in_=stat[:, 6:7], func=AF.Exp,
                                            scale=nhalf[0:1, 0:1]),
           reads=["stat_sd", "nhalf"], writes=["stat_rstd"])
    def gn_bcast(buf):
        return buf[:, :, None].to_broadcast([1, NH, HD])
    p.emit("vector", lambda v: v.tensor_single_scalar(
        arow[:, :].rearrange("p (h u) -> p h u", h=NH), gn_bcast(gnws), stat[0:1, 7:8], ALU.mult),
        reads=["gnws", "stat_rstd"], writes=["arow"])
    p.emit("vector", lambda v: v.tensor_scalar_mul(stat[:, 2:3], stat[:, 2:3], -1.0),
           reads=["stat_mean"], writes=["stat_negmu"])
    p.emit("vector", lambda v: v.scalar_tensor_tensor(
        brow[:, :].rearrange("p (h u) -> p h u", h=NH), arow[:, :].rearrange("p (h u) -> p h u", h=NH),
        stat[0:1, 2:3], gn_bcast(gnbs), ALU.mult, ALU.add),
        reads=["arow", "stat_negmu", "gnbs"], writes=["brow"])
    p.emit("vector", lambda v: v.tensor_copy(rowsb[:, 0:D], arow[:, :]),
           reads=["arow"], writes=["rowsb"])
    p.emit("vector", lambda v: v.tensor_copy(rowsb[:, D:2 * D], brow[:, :]),
           reads=["brow"], writes=["rowsb2"])
    def abmm(t):
        last = None
        for i in range(4):
            last = p.mm(t, ps_ab[:, 512 * i:512 * (i + 1)], onesrow[:, :],
                            rowsb[0:1, 512 * i:512 * (i + 1)], start=True, stop=True)
        return last
    p.emit("tensor", abmm, reads=["rowsb", "rowsb2", "onesrow"], writes=["ps_ab"])

    for c in range(NCH):
        p.emit("sync", lambda s, c=c: s.dma_start(
            out=xs[:, :], in_=x[CH * c:CH * (c + 1), :]),
            writes=["xs"], dma=True)
        p.emit("vector", lambda v, c=c: v.tensor_tensor(
            zt[:, :], ars[:, D * c:D * (c + 1)], ps_ab[:, 0:D], ALU.mult),
            reads=[f"ars{c}", "ps_ab"], writes=["zt"])
        p.emit("vector", lambda v: v.tensor_tensor(zt[:, :], zt[:, :], ps_ab[:, D:2 * D], ALU.add),
               reads=["ps_ab"], writes=["zt"])
        p.emit("vector", lambda v: v.tensor_tensor(z2[:, :], zt[:, :], xs[:, :], ALU.add),
               reads=["zt", "xs"], writes=["z2"])
        p.emit("vector", lambda v: v.reduce_sum(mu[:, :], z2[:, :], axis=mybir.AxisListType.X),
               reads=["z2"], writes=["mu"])
        p.emit("vector", lambda v: v.tensor_scalar_mul(mu[:, :], mu[:, :], 1.0 / D),
               writes=["mu"])
        p.emit("scalar", lambda s: s.activation(out=sqscr[:, :], in_=z2[:, :], func=AF.Square,
                                                accum_out=sq2[:, :]),
               reads=["z2"], writes=["sq2", "sqscr"])
        p.emit("vector", lambda v: v.tensor_scalar_mul(sq2[:, :], sq2[:, :], 1.0 / D),
               reads=["sq2"], writes=["sq2b"])
        p.emit("vector", lambda v: v.tensor_tensor(var[:, :], mu[:, :], mu[:, :], ALU.mult),
               reads=["mu"], writes=["var"])
        p.emit("vector", lambda v: v.tensor_tensor(var[:, :], sq2[:, :], var[:, :], ALU.subtract),
               reads=["sq2b"], writes=["var"])
        p.emit("scalar", lambda s: s.activation(out=rstd[:, :], in_=var[:, :], func=AF.Ln,
                                                bias=epst[:, 0:1]),
               reads=["var", "epst"], writes=["rstd"])
        p.emit("scalar", lambda s: s.activation(out=rstd[:, :], in_=rstd[:, :], func=AF.Exp,
                                                scale=nhalf[:, 0:1]),
               reads=["rstd", "nhalf"], writes=["rstd"])
        p.emit("vector", lambda v: v.tensor_scalar(
            ot[:, :], z2[:, :], mu[:, 0:1], rstd[:, 0:1], ALU.subtract, ALU.mult),
            reads=["z2", "mu", "rstd"], writes=["ot"])
        p.emit("gpsimd", lambda g, c=c: g.dma_start(
            out=xo[CH * c:CH * (c + 1), :], in_=ot[:, :]),
            reads=["ot"], writes=[f"xo{c}"], dma=True)
    p.finalize()
    return nc



def build_quant(nc, name, R, C):
    p = Prog(nc)
    wt = nc.dram_tensor(name, [R, C], F32, kind="ExternalInput").ap()
    wq = nc.dram_tensor(name + "_q", [R, C], BF16, kind="ExternalOutput").ap()
    NR = R // 128
    tiles = [nc.alloc_sbuf_tensor(f"wtile{j}", [128, C], F32) for j in range(4)]
    qt = [nc.alloc_sbuf_tensor(f"qtile{j}", [128, C], BF16) for j in range(2)]
    at = [nc.alloc_sbuf_tensor(f"at{j}", [128, C], F32) for j in range(2)]
    bt = [nc.alloc_sbuf_tensor(f"bt{j}", [128, C], F32) for j in range(2)]
    cols = nc.alloc_sbuf_tensor("cols", [128, NR], F32)
    onescol = nc.alloc_sbuf_tensor("onescol", [128, 1], F32)
    onesrow = nc.alloc_sbuf_tensor("onesrow", [1, 128], F32)
    red = nc.alloc_sbuf_tensor("red", [1, NR], F32)
    gam = nc.alloc_sbuf_tensor("gam", [1, 4], F32)
    thr = nc.alloc_sbuf_tensor("thr", [128, 2], F32)
    ps_r = nc.alloc_psum_tensor("ps_r", [128, 512], F32)

    p.emit("vector", lambda v: v.memset(onescol[:, :], 1.0), writes=["onescol"])
    p.emit("vector", lambda v: v.memset(onesrow[:, :], 1.0), writes=["onesrow"])
    for r in range(NR):
        j = r % 4
        p.emit("sync", lambda s, r=r, j=j: s.dma_start(
            out=tiles[j][:, :], in_=wt[128 * r:128 * (r + 1), :]),
            writes=[f"wt{j}"], dma=True)
        p.emit("vector", lambda v, r=r, j=j: v.reduce_sum(
            cols[:, r:r + 1], tiles[j][:, :], axis=mybir.AxisListType.X,
            apply_absolute_value=True),
            reads=[f"wt{j}"], writes=[f"cols{r}"])
    p.emit("tensor", lambda t: p.mm(t, ps_r[0:1, 0:NR], onescol[:, :], cols[:, :],
                                    start=True, stop=True),
           reads=[f"cols{r}" for r in range(NR)] + ["onescol"], writes=["ps_r"])
    p.emit("vector", lambda v: v.tensor_copy(red[:, :], ps_r[0:1, 0:NR]),
           reads=["ps_r"], writes=["red"])
    p.emit("vector", lambda v: v.reduce_sum(gam[:, 0:1], red[:, :], axis=mybir.AxisListType.X),
           reads=["red"], writes=["gam0"])
    p.emit("vector", lambda v: v.tensor_scalar(
        gam[:, 1:2], gam[:, 0:1], 0.5 / float(R * C), 5e-7, ALU.mult, ALU.add),
        reads=["gam0"], writes=["gam1"])
    def thmm(t):
        return p.mm(t, ps_r[:, 0:1], onesrow[:, :], gam[0:1, 1:2], start=True, stop=True)
    p.emit("tensor", thmm, reads=["gam1", "onesrow"], writes=["ps_r"])
    p.emit("vector", lambda v: v.tensor_copy(thr[:, 0:1], ps_r[:, 0:1]),
           reads=["ps_r"], writes=["thr"])
    p.emit("vector", lambda v: v.tensor_scalar_mul(thr[:, 1:2], thr[:, 0:1], -1.0),
           reads=["thr"], writes=["thrn"])
    for r in range(NR):
        j = r % 4
        j2 = r % 2
        p.emit("sync", lambda s, r=r, j=j: s.dma_start(
            out=tiles[j][:, :], in_=wt[128 * r:128 * (r + 1), :]),
            writes=[f"wt{j}"], dma=True)
        p.emit("vector", lambda v, j=j, j2=j2: v.tensor_single_scalar(
            at[j2][:, :], tiles[j][:, :], thr[:, 0:1], ALU.is_gt),
            reads=[f"wt{j}", "thr"], writes=[f"at{j2}"])
        p.emit("gpsimd", lambda g, j=j, j2=j2: g.tensor_single_scalar(
            bt[j2][:, :], tiles[j][:, :], thr[:, 1:2], ALU.is_lt),
            reads=[f"wt{j}", "thrn"], writes=[f"bt{j2}"])
        p.emit("vector", lambda v, j2=j2: v.tensor_tensor(
            qt[j2][:, :], at[j2][:, :], bt[j2][:, :], ALU.subtract),
            reads=[f"at{j2}", f"bt{j2}"], writes=[f"qt{j2}"])
        p.emit("scalar", lambda s, r=r, j2=j2: s.dma_start(
            out=wq[128 * r:128 * (r + 1), :], in_=qt[j2][:, :]),
            reads=[f"qt{j2}"], writes=[f"wq{r}"], dma=True)
    p.finalize()
    return nc



def build_l0_ffn(nc, stage=9):
    p = Prog(nc)
    x1T = nc.dram_tensor("x1T", [D, T], F32, kind="ExternalInput").ap()
    wfc = nc.dram_tensor("wfcT_q", [D, 4 * D], BF16, kind="ExternalInput").ap()
    wpr = nc.dram_tensor("wprT_q", [4 * D, D], BF16, kind="ExternalInput").ap()
    lnw = nc.dram_tensor("lnw", [4 * D], F32, kind="ExternalInput").ap()
    lnb = nc.dram_tensor("lnb", [4 * D], F32, kind="ExternalInput").ap()
    xo = nc.dram_tensor("x2fm", [D, T], F32, kind="ExternalOutput").ap()

    TT_ = 256
    NTT = T // TT_
    wfcs = nc.alloc_sbuf_tensor("wfcs", [128, 8 * 4 * D], BF16)
    wprs = nc.alloc_sbuf_tensor("wprs", [128, 32 * D], BF16)
    x1f = nc.alloc_sbuf_tensor("x1f", [128, 8 * TT_], F32)
    x1b = nc.alloc_sbuf_tensor("x1b", [128, 8 * TT_], BF16)
    hb = nc.alloc_sbuf_tensor("hb", [128, 32 * TT_], BF16)
    sqs = [nc.alloc_sbuf_tensor(f"sqs{j}", [128, TT_], F32) for j in range(2)]
    lnwc = nc.alloc_sbuf_tensor("lnwc", [128, 64], F32)
    row = nc.alloc_sbuf_tensor("row", [1, 6 * TT_], F32)
    zfm = nc.alloc_sbuf_tensor("zfm", [128, 8 * TT_], F32)
    ofm = [nc.alloc_sbuf_tensor(f"ofm_{j}", [128, TT_], F32) for j in range(2)]
    onescol = nc.alloc_sbuf_tensor("onescol", [128, 1], F32)
    onescolb = nc.alloc_sbuf_tensor("onescolb", [128, 1], BF16)
    onesrow = nc.alloc_sbuf_tensor("onesrow", [1, 128], F32)
    ident = nc.alloc_sbuf_tensor("ident", [128, 128], F32)
    lnrows = nc.alloc_sbuf_tensor("lnrows", [32, 256], F32)
    epst = nc.alloc_sbuf_tensor("epst", [128, 1], F32)
    nhalf = nc.alloc_sbuf_tensor("nhalf", [128, 1], F32)
    scr = [nc.alloc_sbuf_tensor(f"scr_{j}", [128, TT_], F32) for j in range(2)]
    scr2 = [nc.alloc_sbuf_tensor(f"scr2_{j}", [128, TT_], F32) for j in range(2)]

    ps_h = [nc.alloc_psum_tensor(f"ps_h{j}", [128, TT_], F32) for j in range(2)]
    ps_o = [nc.alloc_psum_tensor(f"ps_o{j}", [128, TT_], F32) for j in range(2)]
    ps_st = nc.alloc_psum_tensor("ps_st", [128, 2 * TT_], F32)
    ps_tr = nc.alloc_psum_tensor("ps_tr", [128, 64], F32)
    ps_bc = nc.alloc_psum_tensor("ps_bc", [128, 2 * TT_], F32)

    p.emit("vector", lambda v: v.memset(onescol[:, :], 1.0), writes=["onescol"])
    p.emit("vector", lambda v: v.memset(onescolb[:, :], 1.0), writes=["onescolb"])
    p.emit("vector", lambda v: v.memset(onesrow[:, :], 1.0), writes=["onesrow"])
    p.emit("vector", lambda v: v.memset(epst[:, :], 1e-5), writes=["epst"])
    p.emit("vector", lambda v: v.memset(nhalf[:, :], -0.5), writes=["nhalf"])
    p.emit("gpsimd", lambda g: g.iota(ident[:, :], pattern=[[1, 128]], base=0,
                                      channel_multiplier=-1,
                                      allow_small_or_imprecise_dtypes=True),
           writes=["ident"])
    p.emit("vector", lambda v: v.tensor_single_scalar(ident[:, :], ident[:, :], 0.0, ALU.is_equal),
           reads=["ident"], writes=["ident"])
    for kc in range(8):
        p.emit("gpsimd", lambda g, kc=kc: g.dma_start(
            out=wfcs[:, 4 * D * kc:4 * D * (kc + 1)], in_=wfc[128 * kc:128 * (kc + 1), :]),
            writes=[f"wfc{kc}"], dma=True)
    for fc in range(32):
        p.emit("gpsimd", lambda g, fc=fc: g.dma_start(
            out=wprs[:, D * fc:D * (fc + 1)], in_=wpr[128 * fc:128 * (fc + 1), :]),
            writes=[f"wpr{fc}"], dma=True)
    p.emit("sync", lambda s: s.dma_start(
        out=lnrows[:, 0:128], in_=lnw.rearrange("(c p) -> c p", p=128)),
        writes=["lnrow_w"], dma=True)
    p.emit("sync", lambda s: s.dma_start(
        out=lnrows[:, 128:256], in_=lnb.rearrange("(c p) -> c p", p=128)),
        writes=["lnrow_b"], dma=True)
    def lntr(t):
        t.drain()
        p.pe_base = None
        t.transpose(ps_tr[:, 0:32], lnrows[0:32, 0:128], ident[0:32, 0:32])
        return t.transpose(ps_tr[:, 32:64], lnrows[0:32, 128:256], ident[0:32, 0:32])
    p.emit("tensor", lntr, reads=["lnrow_w", "lnrow_b", "ident"], writes=["ps_stt"])
    p.emit("vector", lambda v: v.tensor_copy(lnwc[:, :], ps_tr[:, 0:64]),
           reads=["ps_stt"], writes=["lnwc"])

    for tt in range(NTT):
        for kc in range(8):
            p.emit("sync", lambda s, tt=tt, kc=kc: s.dma_start(
                out=x1f[:, TT_ * kc:TT_ * (kc + 1)],
                in_=x1T[128 * kc:128 * (kc + 1), TT_ * tt:TT_ * (tt + 1)]),
                writes=[f"x1f{kc}"], dma=True)
            p.emit("vector", lambda v, kc=kc: v.tensor_copy(
                x1b[:, TT_ * kc:TT_ * (kc + 1)], x1f[:, TT_ * kc:TT_ * (kc + 1)]),
                reads=[f"x1f{kc}"], writes=[f"x1b{kc}"])
        for fc in range(32):
            j = fc % 2
            pairs = [(wfcs[:, 4 * D * kc + 128 * fc: 4 * D * kc + 128 * (fc + 1)],
                      x1b[:, TT_ * kc:TT_ * (kc + 1)]) for kc in range(8)]
            mm_acc(p, ps_h[j][:, :], pairs,
                   reads=[f"wfc{kc}" for kc in range(8)] + [f"x1b{kc}" for kc in range(8)],
                   writes=[f"ps_h{j}"])
            p.emit("scalar", lambda s, fc=fc, j=j: s.activation(
                out=hb[:, TT_ * fc:TT_ * (fc + 1)], in_=ps_h[j][:, :], func=AF.Gelu),
                reads=[f"ps_h{j}"], writes=[f"hb{fc}"])
        if stage < 2:
            for ec in range(8):
                p.emit("vector", lambda v, ec=ec: v.tensor_copy(
                    ofm[:, :], hb[:, TT_ * ec:TT_ * (ec + 1)]),
                    reads=[f"hb{ec}"], writes=["ofm"])
                p.emit("gpsimd", lambda g, tt=tt, ec=ec: g.dma_start(
                    out=xo[128 * ec:128 * (ec + 1), TT_ * tt:TT_ * (tt + 1)], in_=ofm[:, :]),
                    reads=["ofm"], writes=[f"xo{tt}_{ec}"], dma=True)
            continue
        def susum(t):
            last = None
            for fc in range(32):
                last = p.mm(t, ps_st[0:1, 0:TT_], onescolb[:, :],
                            hb[:, TT_ * fc:TT_ * (fc + 1)],
                            start=(fc == 0), stop=(fc == 31))
            return last
        p.emit("tensor", susum, reads=[f"hb{fc}" for fc in range(32)] + ["onescolb"],
               writes=["ps_st"])
        p.emit("vector", lambda v: v.tensor_copy(row[:, 0:TT_], ps_st[0:1, 0:TT_]),
               reads=["ps_st"], writes=["row_sum"])
        if stage < 2.2:
            p.emit("vector", lambda v: v.memset(row[:, TT_:2 * TT_], 1.0), writes=["row_sq"])
        else:
            for fc in range(32):
                j = fc % 2
                p.emit("scalar", lambda s, fc=fc, j=j: s.activation(
                    out=sqs[j][:, :], in_=hb[:, TT_ * fc:TT_ * (fc + 1)], func=AF.Square),
                    reads=[f"hb{fc}"], writes=[f"sqs{j}"])
                mm_acc(p, ps_st[0:1, TT_:2 * TT_], [(onescol[:, :], sqs[j][:, :])],
                       reads=[f"sqs{j}", "onescol"], writes=["ps_st"], accum=(fc > 0))
            p.emit("vector", lambda v: v.tensor_copy(row[:, TT_:2 * TT_], ps_st[0:1, TT_:2 * TT_]),
                   reads=["ps_st"], writes=["row_sq"])
        p.emit("vector", lambda v: v.tensor_scalar_mul(
            row[:, 2 * TT_:3 * TT_], row[:, 0:TT_], 1.0 / (4 * D)),
            reads=["row_sum"], writes=["row_mu"])
        p.emit("vector", lambda v: v.tensor_scalar_mul(
            row[:, 3 * TT_:4 * TT_], row[:, TT_:2 * TT_], 1.0 / (4 * D)),
            reads=["row_sq"], writes=["row_e2"])
        p.emit("vector", lambda v: v.tensor_tensor(
            row[:, 4 * TT_:5 * TT_], row[:, 2 * TT_:3 * TT_], row[:, 2 * TT_:3 * TT_], ALU.mult),
            reads=["row_mu"], writes=["row_m2"])
        p.emit("vector", lambda v: v.tensor_tensor(
            row[:, 4 * TT_:5 * TT_], row[:, 3 * TT_:4 * TT_], row[:, 4 * TT_:5 * TT_], ALU.subtract),
            reads=["row_e2", "row_m2"], writes=["row_var"])
        p.emit("scalar", lambda s: s.activation(
            out=row[:, 4 * TT_:5 * TT_], in_=row[:, 4 * TT_:5 * TT_], func=AF.Ln,
            bias=epst[0:1, 0:1]),
            reads=["row_var", "epst"], writes=["row_sd"])
        p.emit("scalar", lambda s: s.activation(
            out=row[:, 5 * TT_:6 * TT_], in_=row[:, 4 * TT_:5 * TT_], func=AF.Exp,
            scale=nhalf[0:1, 0:1]),
            reads=["row_sd", "nhalf"], writes=["row_rstd"])
        def bcmm(t):
            p.mm(t, ps_bc[:, 0:TT_], onesrow[:, :], row[0:1, 2 * TT_:3 * TT_],
                 start=True, stop=True)
            return p.mm(t, ps_bc[:, TT_:2 * TT_], onesrow[:, :], row[0:1, 5 * TT_:6 * TT_],
                        start=True, stop=True)
        p.emit("tensor", bcmm, reads=["row_mu", "row_rstd", "onesrow"], writes=["ps_bc"])
        if stage < 3:
            for ec in range(8):
                p.emit("vector", lambda v, ec=ec: v.tensor_copy(
                    ofm[:, :], hb[:, TT_ * ec:TT_ * (ec + 1)]),
                    reads=[f"hb{ec}", "row_rstd"], writes=["ofm"])
                p.emit("gpsimd", lambda g, tt=tt, ec=ec: g.dma_start(
                    out=xo[128 * ec:128 * (ec + 1), TT_ * tt:TT_ * (tt + 1)], in_=ofm[:, :]),
                    reads=["ofm"], writes=[f"xo{tt}_{ec}"], dma=True)
            continue
        for fc in range(32):
            j = fc % 2
            p.emit("vector", lambda v, fc=fc, j=j: v.tensor_tensor(
                scr[j][:, :], hb[:, TT_ * fc:TT_ * (fc + 1)], ps_bc[:, 0:TT_], ALU.subtract),
                reads=[f"hb{fc}", "ps_bc"], writes=[f"scr_{j}"])
            p.emit("vector", lambda v, j=j: v.tensor_tensor(
                scr2[j][:, :], scr[j][:, :], ps_bc[:, TT_:2 * TT_], ALU.mult),
                reads=[f"scr_{j}", "ps_bc"], writes=[f"scr2_{j}"])
            p.emit("vector", lambda v, fc=fc, j=j: v.tensor_scalar(
                hb[:, TT_ * fc:TT_ * (fc + 1)], scr2[j][:, :],
                lnwc[:, fc:fc + 1],
                lnwc[:, 32 + fc:33 + fc], ALU.mult, ALU.add),
                reads=[f"scr2_{j}", "lnwc"], writes=[f"hb{fc}"])
        if stage < 4:
            for ec in range(8):
                p.emit("vector", lambda v, ec=ec: v.tensor_copy(
                    ofm[:, :], hb[:, TT_ * ec:TT_ * (ec + 1)]),
                    reads=[f"hb{ec}"], writes=["ofm"])
                p.emit("gpsimd", lambda g, tt=tt, ec=ec: g.dma_start(
                    out=xo[128 * ec:128 * (ec + 1), TT_ * tt:TT_ * (tt + 1)], in_=ofm[:, :]),
                    reads=["ofm"], writes=[f"xo{tt}_{ec}"], dma=True)
            continue
        for ec in range(8):
            j = ec % 2
            pairs = [(wprs[:, D * fc + 128 * ec: D * fc + 128 * (ec + 1)],
                      hb[:, TT_ * fc:TT_ * (fc + 1)]) for fc in range(32)]
            mm_acc(p, ps_o[j][:, :], pairs,
                   reads=[f"wpr{fc}" for fc in range(32)] + [f"hb{fc}" for fc in range(32)],
                   writes=[f"ps_o{j}"])
            p.emit("vector", lambda v, ec=ec, j=j: v.tensor_tensor(
                zfm[:, TT_ * ec:TT_ * (ec + 1)], ps_o[j][:, :],
                x1f[:, TT_ * ec:TT_ * (ec + 1)], ALU.add),
                reads=[f"ps_o{j}", f"x1f{ec}"], writes=[f"zfm{ec}"])
        if stage < 5:
            for ec in range(8):
                p.emit("vector", lambda v, ec=ec: v.tensor_copy(
                    ofm[:, :], zfm[:, TT_ * ec:TT_ * (ec + 1)]),
                    reads=[f"zfm{ec}"], writes=["ofm"])
                p.emit("gpsimd", lambda g, tt=tt, ec=ec: g.dma_start(
                    out=xo[128 * ec:128 * (ec + 1), TT_ * tt:TT_ * (tt + 1)], in_=ofm[:, :]),
                    reads=["ofm"], writes=[f"xo{tt}_{ec}"], dma=True)
            continue
        def zsum(t):
            last = None
            for ec in range(8):
                last = p.mm(t, ps_st[0:1, 0:TT_], onescol[:, :],
                            zfm[:, TT_ * ec:TT_ * (ec + 1)],
                            start=(ec == 0), stop=(ec == 7))
            return last
        p.emit("tensor", zsum, reads=[f"zfm{ec}" for ec in range(8)] + ["onescol"],
               writes=["ps_st"])
        p.emit("vector", lambda v: v.tensor_copy(row[:, 0:TT_], ps_st[0:1, 0:TT_]),
               reads=["ps_st"], writes=["row_sum"])
        for ec in range(8):
            j = ec % 2
            p.emit("scalar", lambda s, ec=ec, j=j: s.activation(
                out=sqs[j][:, :], in_=zfm[:, TT_ * ec:TT_ * (ec + 1)], func=AF.Square),
                reads=[f"zfm{ec}"], writes=[f"sqs{j}"])
            mm_acc(p, ps_st[0:1, TT_:2 * TT_], [(onescol[:, :], sqs[j][:, :])],
                   reads=[f"sqs{j}", "onescol"], writes=["ps_st"], accum=(ec > 0))
        p.emit("vector", lambda v: v.tensor_copy(row[:, TT_:2 * TT_], ps_st[0:1, TT_:2 * TT_]),
               reads=["ps_st"], writes=["row_sq"])
        p.emit("vector", lambda v: v.tensor_scalar_mul(
            row[:, 2 * TT_:3 * TT_], row[:, 0:TT_], 1.0 / D),
            reads=["row_sum"], writes=["row_mu"])
        p.emit("vector", lambda v: v.tensor_scalar_mul(
            row[:, 3 * TT_:4 * TT_], row[:, TT_:2 * TT_], 1.0 / D),
            reads=["row_sq"], writes=["row_e2"])
        p.emit("vector", lambda v: v.tensor_tensor(
            row[:, 4 * TT_:5 * TT_], row[:, 2 * TT_:3 * TT_], row[:, 2 * TT_:3 * TT_], ALU.mult),
            reads=["row_mu"], writes=["row_m2"])
        p.emit("vector", lambda v: v.tensor_tensor(
            row[:, 4 * TT_:5 * TT_], row[:, 3 * TT_:4 * TT_], row[:, 4 * TT_:5 * TT_], ALU.subtract),
            reads=["row_e2", "row_m2"], writes=["row_var"])
        p.emit("scalar", lambda s: s.activation(
            out=row[:, 4 * TT_:5 * TT_], in_=row[:, 4 * TT_:5 * TT_], func=AF.Ln,
            bias=epst[0:1, 0:1]),
            reads=["row_var", "epst"], writes=["row_sd"])
        p.emit("scalar", lambda s: s.activation(
            out=row[:, 5 * TT_:6 * TT_], in_=row[:, 4 * TT_:5 * TT_], func=AF.Exp,
            scale=nhalf[0:1, 0:1]),
            reads=["row_sd", "nhalf"], writes=["row_rstd"])
        def bcmm2(t):
            p.mm(t, ps_bc[:, 0:TT_], onesrow[:, :], row[0:1, 2 * TT_:3 * TT_],
                 start=True, stop=True)
            return p.mm(t, ps_bc[:, TT_:2 * TT_], onesrow[:, :], row[0:1, 5 * TT_:6 * TT_],
                        start=True, stop=True)
        p.emit("tensor", bcmm2, reads=["row_mu", "row_rstd", "onesrow"], writes=["ps_bc"])
        for ec in range(8):
            j = ec % 2
            p.emit("vector", lambda v, ec=ec, j=j: v.tensor_tensor(
                scr[j][:, :], zfm[:, TT_ * ec:TT_ * (ec + 1)], ps_bc[:, 0:TT_], ALU.subtract),
                reads=[f"zfm{ec}", "ps_bc"], writes=[f"scr_{j}"])
            p.emit("vector", lambda v, j=j: v.tensor_tensor(
                ofm[j][:, :], scr[j][:, :], ps_bc[:, TT_:2 * TT_], ALU.mult),
                reads=[f"scr_{j}", "ps_bc"], writes=[f"ofm{j}"])
            p.emit("gpsimd", lambda g, tt=tt, ec=ec, j=j: g.dma_start(
                out=xo[128 * ec:128 * (ec + 1), TT_ * tt:TT_ * (tt + 1)], in_=ofm[j][:, :]),
                reads=[f"ofm{j}"], writes=[f"xo{tt}_{ec}"], dma=True)
    p.finalize()
    return nc



SCALE1 = 1.0 / float(np.sqrt(512.0))
EPSF = float(np.finfo(np.float32).eps)


def l1_consts():
    m = np.zeros((128, 128), np.float32)
    for p in range(128):
        for f in range(128):
            if p // 16 == f // 16:
                m[p, f] = 1.0
    return m


def build_l1_attn(nc, dbg=False):
    p = Prog(nc)
    x2T = nc.dram_tensor("x2T", [D, T], F32, kind="ExternalInput").ap()
    wqT = nc.dram_tensor("wqT", [D, D], F32, kind="ExternalInput").ap()
    wkT = nc.dram_tensor("wkT", [D, D], F32, kind="ExternalInput").ap()
    wvT = nc.dram_tensor("wvT", [D, D], F32, kind="ExternalInput").ap()
    lam = nc.dram_tensor("lam", [4], F32, kind="ExternalInput").ap()
    g1 = nc.dram_tensor("g1", [D], F32, kind="ExternalInput").ap()
    g2 = nc.dram_tensor("g2", [HD], F32, kind="ExternalInput").ap()
    g3 = nc.dram_tensor("g3", [D], F32, kind="ExternalInput").ap()
    cMA = nc.dram_tensor("cMA", [128, 128], F32, kind="ExternalInput").ap()
    a3o = nc.dram_tensor("attn3", [T, D], F32, kind="ExternalOutput").ap()
    vtmd = nc.dram_tensor("vtmd", [T, D], BF16, kind="Internal").ap()
    a2d = nc.dram_tensor("a2d", [T, D], BF16, kind="Internal").ap()
    if dbg:
        vtm_o = nc.dram_tensor("vtm_o", [T, D], BF16, kind="ExternalOutput").ap()
        a2_o = nc.dram_tensor("a2_o", [T, D], BF16, kind="ExternalOutput").ap()
        qil_o = nc.dram_tensor("qil_o", [64, 16 * T // 2], BF16, kind="ExternalOutput").ap()
        kil_o = nc.dram_tensor("kil_o", [64, 16 * T // 2], BF16, kind="ExternalOutput").ap()

    xf = [nc.alloc_sbuf_tensor("xf0", [128, T], F32)]
    xf = [xf[0], xf[0]]
    xnb = nc.alloc_sbuf_tensor("xnb", [128, 8 * T], BF16)
    wq = nc.alloc_sbuf_tensor("wq", [128, 8 * D], BF16)
    wk = nc.alloc_sbuf_tensor("wk", [128, 8 * D], BF16)
    wv = nc.alloc_sbuf_tensor("wv", [128, 8 * D], BF16)
    qil = nc.alloc_sbuf_tensor("qil", [64, 16 * (T // 2)], BF16)
    kil = nc.alloc_sbuf_tensor("kil", [64, 16 * (T // 2)], BF16)
    vfm = nc.alloc_sbuf_tensor("vfm", [128, D], BF16)
    vil = nc.alloc_sbuf_tensor("vil", [128, 16 * HD], BF16)
    rstdbc = nc.alloc_sbuf_tensor("rstdbc", [128, T], F32)
    sqscr = nc.alloc_sbuf_tensor("sqscr", [128, 512], F32)
    rowa = nc.alloc_sbuf_tensor("rowa", [1, T], F32)
    g3row = nc.alloc_sbuf_tensor("g3row", [1, D], F32)
    g1c = nc.alloc_sbuf_tensor("g1c", [128, 8], F32)
    g3c = nc.alloc_sbuf_tensor("g3c", [128, 8], F32)
    g2bc = nc.alloc_sbuf_tensor("g2bc", [128, HD], F32)
    g3bc = nc.alloc_sbuf_tensor("g3bc", [128, D], F32)
    grow = nc.alloc_sbuf_tensor("grow", [8, 256], F32)
    g2r = nc.alloc_sbuf_tensor("g2r", [1, HD], F32)
    lamt = nc.alloc_sbuf_tensor("lamt", [1, 8], F32)
    lvec = nc.alloc_sbuf_tensor("lvec", [64, 2], F32)
    maskA = nc.alloc_sbuf_tensor("maskA", [128, 128], F32)
    ident = nc.alloc_sbuf_tensor("ident", [128, 128], F32)
    onesrow = nc.alloc_sbuf_tensor("onesrow", [1, 128], F32)
    onescol = nc.alloc_sbuf_tensor("onescol", [128, 1], F32)
    epst = nc.alloc_sbuf_tensor("epst", [128, 1], F32)
    nhalf = nc.alloc_sbuf_tensor("nhalf", [128, 1], F32)
    escr = [nc.alloc_sbuf_tensor(f"escr{j}", [64, 512], F32) for j in range(6)]
    Am = nc.alloc_sbuf_tensor("Am", [128, 512], BF16)
    a2g = nc.alloc_sbuf_tensor("a2g", [128, 1024], BF16)
    r2s = nc.alloc_sbuf_tensor("r2s", [128, 16], F32)
    mu1 = nc.alloc_sbuf_tensor("mu1", [128, 4], F32)

    ps = [nc.alloc_psum_tensor(f"psg{j}", [128, 512], F32) for j in range(8)]

    p.emit("vector", lambda v: v.memset(onesrow[:, :], 1.0), writes=["onesrow"])
    p.emit("vector", lambda v: v.memset(onescol[:, :], 1.0), writes=["onescol"])
    p.emit("vector", lambda v: v.memset(epst[:, :], EPSF), writes=["epst"])
    p.emit("vector", lambda v: v.memset(nhalf[:, :], -0.5), writes=["nhalf"])
    p.emit("gpsimd", lambda g: g.iota(ident[:, :], pattern=[[1, 128]], base=0,
                                      channel_multiplier=-1,
                                      allow_small_or_imprecise_dtypes=True),
           writes=["ident"])
    p.emit("vector", lambda v: v.tensor_single_scalar(ident[:, :], ident[:, :], 0.0, ALU.is_equal),
           reads=["ident"], writes=["ident"])
    p.emit("sync", lambda s: s.dma_start(out=maskA[:, :], in_=cMA[:, :]), writes=["maskA"], dma=True)
    p.emit("sync", lambda s: s.dma_start(out=lamt[:, 0:4], in_=lam[None, :]), writes=["lamt"], dma=True)
    p.emit("vector", lambda v: v.tensor_tensor(lamt[:, 4:5], lamt[:, 0:1], lamt[:, 1:2], ALU.mult),
           reads=["lamt"], writes=["l01"])
    p.emit("vector", lambda v: v.tensor_tensor(lamt[:, 5:6], lamt[:, 2:3], lamt[:, 3:4], ALU.mult),
           reads=["lamt"], writes=["l23"])
    p.emit("scalar", lambda s: s.activation(out=lamt[:, 4:5], in_=lamt[:, 4:5], func=AF.Exp),
           reads=["l01"], writes=["l01"])
    p.emit("scalar", lambda s: s.activation(out=lamt[:, 5:6], in_=lamt[:, 5:6], func=AF.Exp),
           reads=["l23"], writes=["l23"])
    p.emit("vector", lambda v: v.tensor_tensor(lamt[:, 6:7], lamt[:, 4:5], lamt[:, 5:6], ALU.subtract),
           reads=["l01", "l23"], writes=["lmb"])
    p.emit("vector", lambda v: v.tensor_scalar(lamt[:, 6:7], lamt[:, 6:7], 0.2, -1.0,
                                               ALU.add, ALU.mult),
           reads=["lmb"], writes=["neglmb"])
    def lbc(t):
        return p.mm(t, ps[0][0:64, 0:1], onesrow[0:1, 0:64], lamt[0:1, 6:7],
                    start=True, stop=True)
    p.emit("tensor", lbc, reads=["neglmb", "onesrow"], writes=["psg0"])
    p.emit("gpsimd", lambda g: g.iota(lvec[:, 1:2], pattern=[[0, 1]], base=0,
                                      channel_multiplier=1,
                                      allow_small_or_imprecise_dtypes=True),
           writes=["lv_i"])
    p.emit("vector", lambda v: v.tensor_single_scalar(lvec[:, 1:2], lvec[:, 1:2], 31.5, ALU.is_gt),
           reads=["lv_i"], writes=["lv_m"])
    p.emit("vector", lambda v: v.tensor_scalar_add(lvec[:, 0:1], ps[0][0:64, 0:1], -1.0),
           reads=["psg0"], writes=["lv_t"])
    p.emit("vector", lambda v: v.tensor_tensor(lvec[:, 0:1], lvec[:, 1:2], lvec[:, 0:1], ALU.mult),
           reads=["lv_t", "lv_m"], writes=["lv_t2"])
    p.emit("vector", lambda v: v.tensor_scalar_add(lvec[:, 0:1], lvec[:, 0:1], 1.0),
           reads=["lv_t2"], writes=["lvec"])
    p.emit("sync", lambda s: s.dma_start(out=grow[0:8, 0:128], in_=g1.rearrange("(c p) -> c p", p=128)),
           writes=["g1r"], dma=True)
    p.emit("sync", lambda s: s.dma_start(out=grow[0:8, 128:256], in_=g3.rearrange("(c p) -> c p", p=128)),
           writes=["g3r"], dma=True)
    p.emit("sync", lambda s: s.dma_start(out=g2r[0:1, :], in_=g2[None, :]),
           writes=["g2r"], dma=True)
    def gtr2(t):
        t.drain(); p.pe_base = None
        t.transpose(ps[1][:, 0:8], grow[0:8, 0:128], ident[0:8, 0:8])
        return t.transpose(ps[1][:, 8:16], grow[0:8, 128:256], ident[0:8, 0:8])
    p.emit("tensor", gtr2, reads=["g1r", "g3r", "ident"], writes=["psg1"])
    p.emit("vector", lambda v: v.tensor_copy(g1c[:, :], ps[1][:, 0:8]),
           reads=["psg1"], writes=["g1c"])
    p.emit("vector", lambda v: v.tensor_copy(g3c[:, :], ps[1][:, 8:16]),
           reads=["psg1"], writes=["g3c"])
    def g2b(t):
        return p.mm(t, ps[2][:, 0:64], onesrow[:, :], g2r[0:1, :], start=True, stop=True)
    p.emit("tensor", g2b, reads=["g2r", "onesrow"], writes=["psg2"])
    p.emit("vector", lambda v: v.tensor_scalar_mul(g2bc[:, :], ps[2][:, 0:64], 0.8),
           reads=["psg2"], writes=["g2bc"])
    p.emit("sync", lambda s: s.dma_start(out=g3row[0:1, 0:D], in_=g3[None, :]),
           writes=["g3row"], dma=True)
    for i in range(2):
        def g3bi(t, i=i):
            return p.mm(t, ps[3][:, 0:512], onesrow[:, :], g3row[0:1, 512 * i:512 * (i + 1)],
                        start=True, stop=True)
        p.emit("tensor", g3bi, reads=["g3row", "onesrow"], writes=["psg3"])
        p.emit("vector", lambda v, i=i: v.tensor_copy(g3bc[:, 512 * i:512 * (i + 1)], ps[3][:, 0:512]),
               reads=["psg3"], writes=["g3bc"])

    for kc in range(8):
        p.emit("gpsimd", lambda g, kc=kc: g.dma_start(
            out=wq[:, D * kc:D * (kc + 1)], in_=wqT[128 * kc:128 * (kc + 1), :]),
            writes=[f"wq{kc}"], dma=True)
        p.emit("gpsimd", lambda g, kc=kc: g.dma_start(
            out=wk[:, D * kc:D * (kc + 1)], in_=wkT[128 * kc:128 * (kc + 1), :]),
            writes=[f"wk{kc}"], dma=True)
        p.emit("gpsimd", lambda g, kc=kc: g.dma_start(
            out=wv[:, D * kc:D * (kc + 1)], in_=wvT[128 * kc:128 * (kc + 1), :]),
            writes=[f"wv{kc}"], dma=True)

    for kc in range(8):
        j = kc % 2
        p.emit("sync", lambda s, kc=kc, j=j: s.dma_start(
            out=xf[j][:, :], in_=x2T[128 * kc:128 * (kc + 1), :]),
            writes=["xfA"], dma=True)
        for tt in range(4):
            p.emit("scalar", lambda s, j=j, tt=tt: s.activation(
                out=sqscr[:, :], in_=xf[j][:, 512 * tt:512 * (tt + 1)], func=AF.Square),
                reads=["xfA"], writes=["sqscr"])
            mm_acc(p, ps[tt][0:1, 0:512], [(onescol[:, :], sqscr[:, :])],
                   reads=["sqscr", "onescol"], writes=[f"psg{tt}"], accum=(kc > 0))
            if kc == 7:
                p.emit("vector", lambda v, tt=tt: v.tensor_copy(
                    rowa[:, 512 * tt:512 * (tt + 1)], ps[tt][0:1, 0:512]),
                    reads=[f"psg{tt}"], writes=[f"msrow{tt}"])
    p.emit("vector", lambda v: v.tensor_scalar_mul(rowa[:, 0:T], rowa[:, 0:T], 1.0 / D),
           reads=[f"msrow{tt}" for tt in range(4)], writes=["msn"])
    p.emit("scalar", lambda s: s.activation(out=rowa[:, 0:T], in_=rowa[:, 0:T],
                                            func=AF.Ln, bias=epst[0:1, 0:1]),
           reads=["msn"], writes=["msq"])
    p.emit("scalar", lambda s: s.activation(out=rowa[:, 0:T], in_=rowa[:, 0:T],
                                            func=AF.Exp, scale=nhalf[0:1, 0:1]),
           reads=["msq", "nhalf"], writes=["rstdrow"])
    for tt in range(4):
        def rbc(t, tt=tt):
            return p.mm(t, ps[6][:, 0:512], onesrow[:, :],
                        rowa[0:1, 512 * tt:512 * (tt + 1)], start=True, stop=True)
        p.emit("tensor", rbc, reads=["rstdrow", "onesrow"], writes=["psg6"])
        p.emit("vector", lambda v, tt=tt: v.tensor_copy(
            rstdbc[:, 512 * tt:512 * (tt + 1)], ps[6][:, 0:512]),
            reads=["psg6"], writes=["rstdbc"])
    for kc in range(8):
        j = kc % 2
        p.emit("sync", lambda s, kc=kc, j=j: s.dma_start(
            out=xf[j][:, :], in_=x2T[128 * kc:128 * (kc + 1), :]),
            writes=["xfA"], dma=True)
        p.emit("vector", lambda v, j=j: v.tensor_tensor(
            xf[j][:, :], xf[j][:, :], rstdbc[:, :], ALU.mult),
            reads=["xfA", "rstdbc"], writes=["xfA"])
        p.emit("vector", lambda v, kc=kc, j=j: v.tensor_single_scalar(
            xnb[:, T * kc:T * (kc + 1)], xf[j][:, :], g1c[:, kc:kc + 1], ALU.mult),
            reads=["xfA", "g1c"], writes=[f"xnb{kc}"])

    for gt in range(16):
        for half in range(2):
            pj = 6 + half
            pairs = [(xnb[:, T * kc + 128 * gt: T * kc + 128 * (gt + 1)],
                      wv[:, D * kc + 512 * half: D * kc + 512 * (half + 1)])
                     for kc in range(8)]
            mm_acc(p, ps[pj][:, 0:512], pairs,
                   reads=[f"wv{kc}" for kc in range(8)] + [f"xnb{kc}" for kc in range(8)],
                   writes=[f"psg{pj}"])
            p.emit("vector", lambda v, half=half, pj=pj: v.tensor_scalar_mul(
                vfm[:, 512 * half:512 * (half + 1)], ps[pj][:, 0:512], SCALE1),
                reads=[f"psg{pj}"], writes=[f"vfm{half}"])
        p.emit("gpsimd", lambda g, gt=gt: g.dma_start(
            out=vtmd[128 * gt:128 * (gt + 1), :], in_=vfm[:, 0:1024]),
            reads=["vfm0", "vfm1"], writes=[f"vtm{gt}"], dma=True)

    for hf in range(2):
        for tt in range(2):
            for h in range(NH):
                pj = h % 4
                wmat, wkeys = (wq, "wq") if True else None
                pairs_q = [(wq[:, D * kc + HD * h: D * kc + HD * (h + 1)],
                            xnb[:, T * kc + 1024 * hf + 512 * tt: T * kc + 1024 * hf + 512 * tt + 512])
                           for kc in range(8)]
                mm_acc(p, ps[pj][0:64, 0:512], pairs_q,
                       reads=[f"wq{kc}" for kc in range(8)] + [f"xnb{kc}" for kc in range(8)],
                       writes=[f"psg{pj}"])
                eb = 3 * (h % 2)
                p.emit("vector", lambda v, pj=pj, eb=eb: v.tensor_scalar_min(
                    escr[eb][:, :], ps[pj][0:64, 0:512], 0.0),
                    reads=[f"psg{pj}"], writes=[f"escr0_{eb}"])
                p.emit("scalar", lambda s, eb=eb: s.activation(
                    out=escr[eb + 1][:, :], in_=escr[eb][:, :], func=AF.Exp),
                    reads=[f"escr0_{eb}"], writes=[f"escr1_{eb}"])
                p.emit("vector", lambda v, pj=pj, eb=eb: v.tensor_scalar_max(
                    escr[eb + 2][:, :], ps[pj][0:64, 0:512], 0.0),
                    reads=[f"psg{pj}"], writes=[f"escr2_{eb}"])
                p.emit("vector", lambda v, eb=eb: v.scalar_tensor_tensor(
                    escr[eb][:, :], escr[eb + 1][:, :], -1.0, escr[eb + 2][:, :], ALU.add, ALU.add),
                    reads=[f"escr1_{eb}", f"escr2_{eb}"], writes=[f"escr0b_{eb}"])
                p.emit("vector", lambda v, h=h, tt=tt, eb=eb: v.tensor_single_scalar(
                    qil[:, :].rearrange("p (t h) -> p t h", h=NH)[:, 512 * tt:512 * tt + 512, h:h + 1],
                    escr[eb][:, :], lvec[:, 0:1], ALU.mult),
                    reads=[f"escr0b_{eb}", "lvec"], writes=[f"qil_{hf}_{tt}_{h}"])
                pairs_k = [(wk[:, D * kc + HD * h: D * kc + HD * (h + 1)],
                            xnb[:, T * kc + 1024 * hf + 512 * tt: T * kc + 1024 * hf + 512 * tt + 512])
                           for kc in range(8)]
                pj2 = 4 + h % 3
                mm_acc(p, ps[pj2][0:64, 0:512], pairs_k,
                       reads=[f"wk{kc}" for kc in range(8)] + [f"xnb{kc}" for kc in range(8)],
                       writes=[f"psg{pj2}"])
                eb2 = 3 * (h % 2)
                p.emit("vector", lambda v, pj2=pj2, eb2=eb2: v.tensor_scalar_min(
                    escr[eb2][:, :], ps[pj2][0:64, 0:512], 0.0),
                    reads=[f"psg{pj2}"], writes=[f"escr0_{eb2}"])
                p.emit("scalar", lambda s, eb2=eb2: s.activation(
                    out=escr[eb2 + 1][:, :], in_=escr[eb2][:, :], func=AF.Exp),
                    reads=[f"escr0_{eb2}"], writes=[f"escr1_{eb2}"])
                p.emit("vector", lambda v, pj2=pj2, eb2=eb2: v.tensor_scalar_max(
                    escr[eb2 + 2][:, :], ps[pj2][0:64, 0:512], 0.0),
                    reads=[f"psg{pj2}"], writes=[f"escr2_{eb2}"])
                p.emit("vector", lambda v, h=h, hf=hf, tt=tt, eb2=eb2: v.scalar_tensor_tensor(
                    kil[:, :].rearrange("p (t h) -> p t h", h=NH)[:, 512 * tt:512 * tt + 512, h:h + 1],
                    escr[eb2 + 1][:, :], -1.0, escr[eb2 + 2][:, :], ALU.add, ALU.add),
                    reads=[f"escr1_{eb2}", f"escr2_{eb2}"], writes=[f"kil_{hf}_{tt}_{h}"])

        for lt in range(8):
            gt = 8 * hf + lt
            p.emit("sync", lambda s, gt=gt: s.dma_start(
                out=vil[:, :].rearrange("p (c q) -> p c q", c=16),
                in_=vtmd[128 * gt:128 * (gt + 1), :].rearrange("(c l) (h q) -> (l h) c q", l=8, h=NH)),
                reads=[f"vtm{gt}"], writes=["vil"], dma=True)
            for grp in range(4):
                def amms(t, lt=lt, grp=grp):
                    last = None
                    for j in range(4):
                        cc = 16 * lt + 4 * grp + j
                        last = p.mm(t, ps[0][:, 128 * j:128 * (j + 1)],
                                    kil[:, 128 * cc:128 * (cc + 1)],
                                    qil[:, 128 * cc:128 * (cc + 1)],
                                    start=True, stop=True)
                    return last
                rd = []
                for tt in range(2):
                    for h in range(NH):
                        rd += [f"qil_{hf}_{tt}_{h}", f"kil_{hf}_{tt}_{h}"]
                p.emit("tensor", amms, reads=rd, writes=["psg0"])
                p.emit("vector", lambda v: v.tensor_tensor(
                    Am[:, :].rearrange("p (c f) -> p c f", c=4),
                    ps[0][:, 0:512].rearrange("p (c f) -> p c f", c=4),
                    maskA[:, None, :].to_broadcast([128, 4, 128]),
                    ALU.mult),
                    reads=["psg0", "maskA"], writes=["Am"])
                def avms(t, lt=lt, grp=grp):
                    last = None
                    for j in range(4):
                        cc4 = 4 * grp + j
                        last = p.mm(t, ps[1][:, 64 * j:64 * (j + 1)],
                                    Am[:, 128 * j:128 * (j + 1)],
                                    vil[:, 64 * cc4:64 * (cc4 + 1)],
                                    start=True, stop=True)
                    return last
                p.emit("tensor", avms, reads=["Am", "vil"], writes=["psg1"])
                p.emit("scalar", lambda s: s.activation(
                    out=sqscr[:, 0:256], in_=ps[1][:, 0:256], func=AF.Square),
                    reads=["psg1"], writes=["sqscr"])
                p.emit("vector", lambda v, grp=grp: v.tensor_reduce(
                    r2s[:, 4 * grp:4 * (grp + 1)],
                    sqscr[:, 0:256].rearrange("p (c q) -> p c q", c=4),
                    axis=mybir.AxisListType.X, op=ALU.add),
                    reads=["sqscr"], writes=["r2s"])
                p.emit("vector", lambda v, grp=grp: v.tensor_scalar_mul(
                    r2s[:, 4 * grp:4 * (grp + 1)], r2s[:, 4 * grp:4 * (grp + 1)], 1.0 / HD),
                    reads=["r2s"], writes=["r2s"])
                p.emit("scalar", lambda s, grp=grp: s.activation(
                    out=r2s[:, 4 * grp:4 * (grp + 1)], in_=r2s[:, 4 * grp:4 * (grp + 1)],
                    func=AF.Ln, bias=epst[:, 0:1]),
                    reads=["r2s", "epst"], writes=["r2s"])
                p.emit("scalar", lambda s, grp=grp: s.activation(
                    out=r2s[:, 4 * grp:4 * (grp + 1)], in_=r2s[:, 4 * grp:4 * (grp + 1)],
                    func=AF.Exp, scale=nhalf[:, 0:1]),
                    reads=["r2s", "nhalf"], writes=["r2s"])
                p.emit("vector", lambda v, grp=grp: v.tensor_tensor(
                    sqscr[:, 0:256].rearrange("p (c q) -> p c q", c=4),
                    ps[1][:, 0:256].rearrange("p (c q) -> p c q", c=4),
                    r2s[:, 4 * grp:4 * (grp + 1), None].to_broadcast([128, 4, HD]),
                    ALU.mult),
                    reads=["psg1", "r2s"], writes=["sqscr"])
                p.emit("vector", lambda v, grp=grp: v.tensor_tensor(
                    a2g[:, 256 * grp:256 * (grp + 1)].rearrange("p (c q) -> p c q", c=4),
                    sqscr[:, 0:256].rearrange("p (c q) -> p c q", c=4),
                    g2bc[:, None, :].to_broadcast([128, 4, HD]),
                    ALU.mult),
                    reads=["sqscr", "g2bc"], writes=[f"a2g{grp}"])
            p.emit("gpsimd", lambda g, gt=gt: g.dma_start(
                out=a2d[128 * gt:128 * (gt + 1), :].rearrange("(c l) (h q) -> (l h) c q", l=8, h=NH),
                in_=a2g[:, :].rearrange("p (c q) -> p c q", c=16)),
                reads=[f"a2g{grp}" for grp in range(4)], writes=[f"a2d{gt}"], dma=True)

    if dbg:
        p.emit("gpsimd", lambda g: g.dma_start(out=vtm_o[:, :], in_=vtmd[:, :]),
               reads=[f"vtm{gt}" for gt in range(16)], writes=["vtmo"], dma=True)
        p.emit("gpsimd", lambda g: g.dma_start(out=a2_o[:, :], in_=a2d[:, :]),
               reads=[f"a2d{gt}" for gt in range(16)], writes=["a2o"], dma=True)
        p.emit("gpsimd", lambda g: g.dma_start(out=qil_o[:, :], in_=qil[:, :]),
               reads=[f"qil_{hf}_{tt}_{h}" for hf in range(2) for tt in range(2) for h in range(NH)],
               writes=["qilo"], dma=True)
        p.emit("gpsimd", lambda g: g.dma_start(out=kil_o[:, :], in_=kil[:, :]),
               reads=[f"kil_{hf}_{tt}_{h}" for hf in range(2) for tt in range(2) for h in range(NH)],
               writes=["kilo"], dma=True)

    for gt in range(16):
        p.emit("gpsimd", lambda g, gt=gt: g.dma_start(
            out=xf[0][:, 0:D], in_=a2d[128 * gt:128 * (gt + 1), :]),
            reads=[f"a2d{gt}"], writes=["a2t", "xfA"], dma=True)
        p.emit("scalar", lambda s: s.activation(
            out=rstdbc[:, 0:D], in_=xf[0][:, 0:D], func=AF.Square, accum_out=mu1[:, 0:1]),
            reads=["a2t"], writes=["mu1", "scr", "rstdbc"])
        p.emit("vector", lambda v: v.tensor_scalar_mul(mu1[:, 1:2], mu1[:, 0:1], 1.0 / D),
               reads=["mu1"], writes=["mu1b"])
        p.emit("scalar", lambda s: s.activation(
            out=mu1[:, 2:3], in_=mu1[:, 1:2], func=AF.Ln, bias=epst[:, 0:1]),
            reads=["mu1b", "epst"], writes=["mu1c"])
        p.emit("scalar", lambda s: s.activation(
            out=mu1[:, 3:4], in_=mu1[:, 2:3], func=AF.Exp, scale=nhalf[:, 0:1]),
            reads=["mu1c", "nhalf"], writes=["mu1d"])
        p.emit("vector", lambda v: v.tensor_single_scalar(
            rstdbc[:, 0:D], xf[0][:, 0:D], mu1[:, 3:4], ALU.mult),
            reads=["a2t", "mu1d"], writes=["scr"])
        p.emit("vector", lambda v: v.tensor_tensor(
            rstdbc[:, 0:D], rstdbc[:, 0:D], g3bc[:, :], ALU.mult),
               reads=["scr", "g3bc"], writes=["scr"])
        p.emit("vector", lambda v: v.tensor_tensor(
            xf[0][:, D:2 * D], rstdbc[:, 0:D], xf[0][:, 0:D], ALU.add),
               reads=["scr", "a2t"], writes=["a3t", "xfA"])
        p.emit("gpsimd", lambda g, gt=gt: g.dma_start(
            out=a3o[128 * gt:128 * (gt + 1), :], in_=xf[0][:, D:2 * D]),
            reads=["a3t"], writes=[f"a3o{gt}"], dma=True)
    p.finalize()
    return nc



def build_l1_ffn(nc):
    p = Prog(nc)
    a3T = nc.dram_tensor("a3T", [D, T], F32, kind="ExternalInput").ap()
    x2T = nc.dram_tensor("x2T", [D, T], F32, kind="ExternalInput").ap()
    w1T = nc.dram_tensor("w1T", [D, D], F32, kind="ExternalInput").ap()
    w2T = nc.dram_tensor("w2T", [D, D], F32, kind="ExternalInput").ap()
    xo = nc.dram_tensor("x3fm", [D, T], F32, kind="ExternalOutput").ap()

    TT_ = 512
    NTT = T // TT_
    w1s = nc.alloc_sbuf_tensor("w1s", [128, 8 * D], BF16)
    w2s = nc.alloc_sbuf_tensor("w2s", [128, 8 * D], BF16)
    a3b = nc.alloc_sbuf_tensor("a3b", [128, 8 * T], BF16)
    h1b = nc.alloc_sbuf_tensor("h1b", [128, 8 * TT_], BF16)
    resf = nc.alloc_sbuf_tensor("resf", [128, TT_], F32)
    res2 = nc.alloc_sbuf_tensor("res2", [128, TT_], F32)
    zfm = nc.alloc_sbuf_tensor("zfm", [128, 8 * TT_], F32)
    sqs = [nc.alloc_sbuf_tensor(f"sqs{j}", [128, TT_], F32) for j in range(2)]
    row = nc.alloc_sbuf_tensor("row", [1, 6 * TT_], F32)
    ofm = nc.alloc_sbuf_tensor("ofm", [128, TT_], F32)
    onescol = nc.alloc_sbuf_tensor("onescol", [128, 1], F32)
    onesrow = nc.alloc_sbuf_tensor("onesrow", [1, 128], F32)
    epst = nc.alloc_sbuf_tensor("epst", [128, 1], F32)
    nhalf = nc.alloc_sbuf_tensor("nhalf", [128, 1], F32)
    scr = nc.alloc_sbuf_tensor("scr", [128, TT_], F32)

    ps_h = [nc.alloc_psum_tensor(f"ps_h{j}", [128, TT_], F32) for j in range(2)]
    ps_o = [nc.alloc_psum_tensor(f"ps_o{j}", [128, TT_], F32) for j in range(2)]
    ps_st = nc.alloc_psum_tensor("ps_st", [128, 2 * TT_], F32)
    ps_bc = nc.alloc_psum_tensor("ps_bc", [128, 2 * TT_], F32)

    p.emit("vector", lambda v: v.memset(onescol[:, :], 1.0), writes=["onescol"])
    p.emit("vector", lambda v: v.memset(onesrow[:, :], 1.0), writes=["onesrow"])
    p.emit("vector", lambda v: v.memset(epst[:, :], 1e-5), writes=["epst"])
    p.emit("vector", lambda v: v.memset(nhalf[:, :], -0.5), writes=["nhalf"])
    for kc in range(8):
        p.emit("gpsimd", lambda g, kc=kc: g.dma_start(
            out=w1s[:, D * kc:D * (kc + 1)], in_=w1T[128 * kc:128 * (kc + 1), :]),
            writes=[f"w1_{kc}"], dma=True)
        p.emit("gpsimd", lambda g, kc=kc: g.dma_start(
            out=w2s[:, D * kc:D * (kc + 1)], in_=w2T[128 * kc:128 * (kc + 1), :]),
            writes=[f"w2_{kc}"], dma=True)
        p.emit("gpsimd", lambda g, kc=kc: g.dma_start(
            out=a3b[:, T * kc:T * (kc + 1)], in_=a3T[128 * kc:128 * (kc + 1), :]),
            writes=[f"a3b{kc}"], dma=True)

    for tt in range(NTT):
        for fc in range(8):
            j = fc % 2
            pairs = [(w1s[:, D * kc + 128 * fc: D * kc + 128 * (fc + 1)],
                      a3b[:, T * kc + TT_ * tt: T * kc + TT_ * tt + TT_]) for kc in range(8)]
            mm_acc(p, ps_h[j][:, :], pairs,
                   reads=[f"w1_{kc}" for kc in range(8)] + [f"a3b{kc}" for kc in range(8)],
                   writes=[f"ps_h{j}"])
            p.emit("scalar", lambda s, fc=fc, j=j: s.activation(
                out=h1b[:, TT_ * fc:TT_ * (fc + 1)], in_=ps_h[j][:, :], func=AF.Gelu),
                reads=[f"ps_h{j}"], writes=[f"h1b{fc}"])
        for ec in range(8):
            j = ec % 2
            pairs = [(w2s[:, D * fc + 128 * ec: D * fc + 128 * (ec + 1)],
                      h1b[:, TT_ * fc:TT_ * (fc + 1)]) for fc in range(8)]
            mm_acc(p, ps_o[j][:, :], pairs,
                   reads=[f"w2_{fc}" for fc in range(8)] + [f"h1b{fc}" for fc in range(8)],
                   writes=[f"ps_o{j}"])
            p.emit("sync", lambda s, tt=tt, ec=ec: s.dma_start(
                out=resf[:, :], in_=a3T[128 * ec:128 * (ec + 1), TT_ * tt:TT_ * tt + TT_]),
                writes=["resf"], dma=True)
            p.emit("sync", lambda s, tt=tt, ec=ec: s.dma_start(
                out=res2[:, :], in_=x2T[128 * ec:128 * (ec + 1), TT_ * tt:TT_ * tt + TT_]),
                writes=["res2"], dma=True)
            p.emit("vector", lambda v, ec=ec, j=j: v.tensor_tensor(
                scr[:, :], ps_o[j][:, :], resf[:, :], ALU.add),
                reads=[f"ps_o{j}", "resf"], writes=["scr"])
            p.emit("vector", lambda v, ec=ec: v.tensor_tensor(
                zfm[:, TT_ * ec:TT_ * (ec + 1)], scr[:, :], res2[:, :], ALU.add),
                reads=["scr", "res2"], writes=[f"zfm{ec}"])
        def zsum(t):
            last = None
            for ec in range(8):
                last = p.mm(t, ps_st[0:1, 0:TT_], onescol[:, :],
                            zfm[:, TT_ * ec:TT_ * (ec + 1)],
                            start=(ec == 0), stop=(ec == 7))
            return last
        p.emit("tensor", zsum, reads=[f"zfm{ec}" for ec in range(8)] + ["onescol"],
               writes=["ps_st"])
        p.emit("vector", lambda v: v.tensor_copy(row[:, 0:TT_], ps_st[0:1, 0:TT_]),
               reads=["ps_st"], writes=["row_sum"])
        for ec in range(8):
            j = ec % 2
            p.emit("scalar", lambda s, ec=ec, j=j: s.activation(
                out=sqs[j][:, :], in_=zfm[:, TT_ * ec:TT_ * (ec + 1)], func=AF.Square),
                reads=[f"zfm{ec}"], writes=[f"sqs{j}"])
            mm_acc(p, ps_st[0:1, TT_:2 * TT_], [(onescol[:, :], sqs[j][:, :])],
                   reads=[f"sqs{j}", "onescol"], writes=["ps_st"], accum=(ec > 0))
        p.emit("vector", lambda v: v.tensor_copy(row[:, TT_:2 * TT_], ps_st[0:1, TT_:2 * TT_]),
               reads=["ps_st"], writes=["row_sq"])
        p.emit("vector", lambda v: v.tensor_scalar_mul(
            row[:, 2 * TT_:3 * TT_], row[:, 0:TT_], 1.0 / D),
            reads=["row_sum"], writes=["row_mu"])
        p.emit("vector", lambda v: v.tensor_scalar_mul(
            row[:, 3 * TT_:4 * TT_], row[:, TT_:2 * TT_], 1.0 / D),
            reads=["row_sq"], writes=["row_e2"])
        p.emit("vector", lambda v: v.tensor_tensor(
            row[:, 4 * TT_:5 * TT_], row[:, 2 * TT_:3 * TT_], row[:, 2 * TT_:3 * TT_], ALU.mult),
            reads=["row_mu"], writes=["row_m2"])
        p.emit("vector", lambda v: v.tensor_tensor(
            row[:, 4 * TT_:5 * TT_], row[:, 3 * TT_:4 * TT_], row[:, 4 * TT_:5 * TT_], ALU.subtract),
            reads=["row_e2", "row_m2"], writes=["row_var"])
        p.emit("scalar", lambda s: s.activation(
            out=row[:, 4 * TT_:5 * TT_], in_=row[:, 4 * TT_:5 * TT_], func=AF.Sqrt,
            bias=epst[0:1, 0:1]),
            reads=["row_var"], writes=["row_sd"])
        p.emit("vector", lambda v: v.reciprocal(
            row[:, 5 * TT_:6 * TT_], row[:, 4 * TT_:5 * TT_]),
            reads=["row_sd"], writes=["row_rstd"])
        def bcmm2(t):
            p.mm(t, ps_bc[:, 0:TT_], onesrow[:, :], row[0:1, 2 * TT_:3 * TT_],
                 start=True, stop=True)
            return p.mm(t, ps_bc[:, TT_:2 * TT_], onesrow[:, :], row[0:1, 5 * TT_:6 * TT_],
                        start=True, stop=True)
        p.emit("tensor", bcmm2, reads=["row_mu", "row_rstd", "onesrow"], writes=["ps_bc"])
        for ec in range(8):
            p.emit("vector", lambda v, ec=ec: v.tensor_tensor(
                scr[:, :], zfm[:, TT_ * ec:TT_ * (ec + 1)], ps_bc[:, 0:TT_], ALU.subtract),
                reads=[f"zfm{ec}", "ps_bc"], writes=["scr"])
            p.emit("vector", lambda v: v.tensor_tensor(
                ofm[:, :], scr[:, :], ps_bc[:, TT_:2 * TT_], ALU.mult),
                reads=["scr", "ps_bc"], writes=["ofm"])
            p.emit("gpsimd", lambda g, tt=tt, ec=ec: g.dma_start(
                out=xo[128 * ec:128 * (ec + 1), TT_ * tt:TT_ * tt + TT_], in_=ofm[:, :]),
                reads=["ofm"], writes=[f"xo{tt}_{ec}"], dma=True)
    p.finalize()
    return nc



NEXP = 8
CAP = 640
BIGOOB = 1.0e6


def moe_consts():
    P = np.zeros((8, 8), np.float32)
    G = np.zeros((8, 8), np.float32)
    for c in range(8):
        for cp in range(8):
            if cp // 4 == c // 4:
                G[cp, c] = 1.0
                if cp < c:
                    P[cp, c] = 1.0
    return P, G



def build_moe_route(nc):
    p = Prog(nc)
    x3T = nc.dram_tensor("x3T", [D, T], F32, kind="ExternalInput").ap()
    x3tm = nc.dram_tensor("x3tm", [T, D], F32, kind="ExternalInput").ap()
    wr = nc.dram_tensor("wr", [D, NEXP], F32, kind="ExternalInput").ap()
    ro = nc.dram_tensor("route", [T, 8], F32, kind="ExternalOutput").ap()
    co = nc.dram_tensor("cnt", [2, NEXP], F32, kind="ExternalOutput").ap()
    xbo = nc.dram_tensor("x3b_s", [T, D], BF16, kind="ExternalOutput").ap()
    for r in range(4):
        p.emit("gpsimd", lambda g, r=r: g.dma_start(
            out=xbo[512 * r:512 * (r + 1), :], in_=x3tm[512 * r:512 * (r + 1), :]),
            writes=[f"xbo{r}"], dma=True)

    xfm = nc.alloc_sbuf_tensor("xfm", [128, 8 * T], F32)
    wrs = nc.alloc_sbuf_tensor("wrs", [128, 8 * NEXP], F32)
    iota8 = nc.alloc_sbuf_tensor("iota8", [128, NEXP], F32)
    revio = nc.alloc_sbuf_tensor("revio", [128, NEXP], F32)
    ident = nc.alloc_sbuf_tensor("ident", [128, 128], F32)
    probs = nc.alloc_sbuf_tensor("probs", [128, NEXP], F32)
    mx8 = nc.alloc_sbuf_tensor("mx8", [128, 8], F32)
    sm = nc.alloc_sbuf_tensor("sm", [128, 8], F32)
    m1a = nc.alloc_sbuf_tensor("m1a", [128, 16 * NEXP], F32)
    m2a = nc.alloc_sbuf_tensor("m2a", [128, 16 * NEXP], F32)
    gi = nc.alloc_sbuf_tensor("gi", [128, 16 * 4], F32)
    em = nc.alloc_sbuf_tensor("em", [NEXP, 2 * T], F32)
    emones = nc.alloc_sbuf_tensor("emones", [NEXP, T], F32)
    lct = nc.alloc_sbuf_tensor("lct", [128, 16], F32)
    rt = nc.alloc_sbuf_tensor("rt", [128, 8], F32)
    cnts = nc.alloc_sbuf_tensor("cnts", [NEXP, 2], F32)

    ps_l = nc.alloc_psum_tensor("ps_l", [128, NEXP], F32)
    ps_t = nc.alloc_psum_tensor("ps_t", [NEXP, T], F32)
    ps_b = nc.alloc_psum_tensor("ps_b", [128, 16], F32)

    p.emit("gpsimd", lambda g: g.iota(iota8[:, :], pattern=[[1, NEXP]], base=0,
                                      channel_multiplier=0,
                                      allow_small_or_imprecise_dtypes=True),
           writes=["iota8"])
    p.emit("vector", lambda v: v.tensor_scalar(revio[:, :], iota8[:, :], -1.0, 99.0,
                                               ALU.mult, ALU.add),
           reads=["iota8"], writes=["revio"])
    p.emit("gpsimd", lambda g: g.iota(ident[:, :], pattern=[[1, 128]], base=0,
                                      channel_multiplier=-1,
                                      allow_small_or_imprecise_dtypes=True),
           writes=["ident"])
    p.emit("vector", lambda v: v.tensor_single_scalar(ident[:, :], ident[:, :], 0.0, ALU.is_equal),
           reads=["ident"], writes=["ident"])
    p.emit("vector", lambda v: v.memset(emones[:, :], 1.0), writes=["emones"])
    for kc in range(8):
        p.emit("sync", lambda s, kc=kc: s.dma_start(
            out=xfm[:, T * kc:T * (kc + 1)], in_=x3T[128 * kc:128 * (kc + 1), :]),
            writes=[f"xfm{kc}"], dma=True)
        p.emit("sync", lambda s, kc=kc: s.dma_start(
            out=wrs[:, NEXP * kc:NEXP * (kc + 1)], in_=wr[128 * kc:128 * (kc + 1), :]),
            writes=[f"wrs{kc}"], dma=True)

    for tt in range(16):
        pairs = [(xfm[:, T * kc + 128 * tt: T * kc + 128 * (tt + 1)],
                  wrs[:, NEXP * kc:NEXP * (kc + 1)]) for kc in range(8)]
        mm_acc(p, ps_l[:, :], pairs,
               reads=[f"xfm{kc}" for kc in range(8)] + [f"wrs{kc}" for kc in range(8)],
               writes=["ps_l"])
        p.emit("vector", lambda v: v.reduce_max(sm[:, 0:1], ps_l[:, :], axis=mybir.AxisListType.X),
               reads=["ps_l"], writes=["smx"])
        p.emit("vector", lambda v: v.tensor_single_scalar(probs[:, :], ps_l[:, :], sm[:, 0:1],
                                                          ALU.subtract),
               reads=["ps_l", "smx"], writes=["probs"])
        p.emit("scalar", lambda s: s.activation(out=probs[:, :], in_=probs[:, :], func=AF.Exp),
               reads=["probs"], writes=["probs"])
        p.emit("vector", lambda v: v.reduce_sum(sm[:, 1:2], probs[:, :], axis=mybir.AxisListType.X),
               reads=["probs"], writes=["sms"])
        p.emit("vector", lambda v: v.reciprocal(sm[:, 2:3], sm[:, 1:2]),
               reads=["sms"], writes=["smr"])
        p.emit("vector", lambda v: v.tensor_single_scalar(probs[:, :], probs[:, :], sm[:, 2:3],
                                                          ALU.mult),
               reads=["probs", "smr"], writes=["probs"])
        p.emit("vector", lambda v: v.max(out=mx8[:, :], in_=probs[:, :]),
               reads=["probs"], writes=["mx8"])
        p.emit("vector", lambda v: v.tensor_single_scalar(rt[:, 0:8], probs[:, :], mx8[:, 0:1],
                                                          ALU.is_equal),
               reads=["probs", "mx8"], writes=["eq1"])
        p.emit("vector", lambda v: v.tensor_tensor(rt[:, 0:8], rt[:, 0:8], revio[:, :], ALU.mult),
               reads=["eq1", "revio"], writes=["eq1r"])
        p.emit("vector", lambda v: v.reduce_max(sm[:, 3:4], rt[:, 0:8], axis=mybir.AxisListType.X),
               reads=["eq1r"], writes=["m1v"])
        p.emit("vector", lambda v: v.tensor_scalar(sm[:, 3:4], sm[:, 3:4], -1.0, 99.0,
                                                   ALU.mult, ALU.add),
               reads=["m1v"], writes=["idx1"])
        p.emit("vector", lambda v, tt=tt: v.tensor_single_scalar(
            m1a[:, NEXP * tt:NEXP * (tt + 1)], iota8[:, :], sm[:, 3:4], ALU.is_equal),
            reads=["iota8", "idx1"], writes=[f"m1a{tt}"])
        p.emit("vector", lambda v: v.tensor_single_scalar(rt[:, 0:8], probs[:, :], mx8[:, 1:2],
                                                          ALU.is_equal),
            reads=["probs", "mx8"], writes=["eq2"])
        p.emit("vector", lambda v, tt=tt: v.tensor_tensor(
            rt[:, 0:8], rt[:, 0:8], m1a[:, NEXP * tt:NEXP * (tt + 1)], ALU.subtract),
            reads=["eq2", f"m1a{tt}"], writes=["eq2b"])
        p.emit("vector", lambda v: v.tensor_scalar_max(rt[:, 0:8], rt[:, 0:8], 0.0),
               reads=["eq2b"], writes=["eq2c"])
        p.emit("vector", lambda v: v.tensor_tensor(rt[:, 0:8], rt[:, 0:8], revio[:, :], ALU.mult),
               reads=["eq2c", "revio"], writes=["eq2r"])
        p.emit("vector", lambda v: v.reduce_max(sm[:, 4:5], rt[:, 0:8], axis=mybir.AxisListType.X),
               reads=["eq2r"], writes=["m2v"])
        p.emit("vector", lambda v: v.tensor_scalar(sm[:, 4:5], sm[:, 4:5], -1.0, 99.0,
                                                   ALU.mult, ALU.add),
               reads=["m2v"], writes=["idx2"])
        p.emit("vector", lambda v, tt=tt: v.tensor_single_scalar(
            m2a[:, NEXP * tt:NEXP * (tt + 1)], iota8[:, :], sm[:, 4:5], ALU.is_equal),
            reads=["iota8", "idx2"], writes=[f"m2a{tt}"])
        p.emit("vector", lambda v, tt=tt: v.tensor_copy(gi[:, 4 * tt:4 * tt + 1], mx8[:, 0:1]),
               reads=["mx8"], writes=[f"gi{tt}"])
        p.emit("vector", lambda v, tt=tt: v.tensor_copy(gi[:, 4 * tt + 1:4 * tt + 2], mx8[:, 1:2]),
               reads=["mx8"], writes=[f"gi{tt}b"])
        p.emit("vector", lambda v, tt=tt: v.tensor_copy(gi[:, 4 * tt + 2:4 * tt + 3], sm[:, 3:4]),
               reads=["idx1"], writes=[f"gi{tt}c"])
        p.emit("vector", lambda v, tt=tt: v.tensor_copy(gi[:, 4 * tt + 3:4 * tt + 4], sm[:, 4:5]),
               reads=["idx2"], writes=[f"gi{tt}d"])
    for tt in range(16):
        def trf(t, tt=tt):
            t.drain()
            p.pe_base = None
            t.transpose(ps_t[0:NEXP, 128 * tt:128 * (tt + 1)],
                        m1a[:, NEXP * tt:NEXP * (tt + 1)], ident[:, :])
            return t.transpose(ps_t[0:NEXP, T + 128 * tt:T + 128 * (tt + 1)] if False else
                               ps_t[0:NEXP, 128 * tt:128 * (tt + 1)],
                               m1a[:, NEXP * tt:NEXP * (tt + 1)], ident[:, :])
        def trf2(t, tt=tt):
            t.drain()
            p.pe_base = None
            return t.transpose(ps_t[0:NEXP, 128 * tt:128 * (tt + 1)],
                               m1a[:, NEXP * tt:NEXP * (tt + 1)], ident[:, :])
        p.emit("tensor", trf2, reads=[f"m1a{tt}", "ident"], writes=["ps_t"])
        p.emit("vector", lambda v, tt=tt: v.tensor_copy(
            em[:, 128 * tt:128 * (tt + 1)], ps_t[0:NEXP, 128 * tt:128 * (tt + 1)]),
            reads=["ps_t"], writes=["em1"])
        def trf3(t, tt=tt):
            t.drain()
            p.pe_base = None
            return t.transpose(ps_t[0:NEXP, 128 * tt:128 * (tt + 1)],
                               m2a[:, NEXP * tt:NEXP * (tt + 1)], ident[:, :])
        p.emit("tensor", trf3, reads=[f"m2a{tt}", "ident"], writes=["ps_t"])
        p.emit("vector", lambda v, tt=tt: v.tensor_copy(
            em[:, T + 128 * tt:T + 128 * (tt + 1)], ps_t[0:NEXP, 128 * tt:128 * (tt + 1)]),
            reads=["ps_t"], writes=["em2"])
    p.emit("vector", lambda v: v.tensor_tensor_scan(
        em[:, 0:T], em[:, 0:T], emones[:, :], 0.0, ALU.add, ALU.mult),
        reads=["em1", "emones"], writes=["em1"])
    p.emit("vector", lambda v: v.tensor_tensor_scan(
        em[:, T:2 * T], em[:, T:2 * T], emones[:, :], 0.0, ALU.add, ALU.mult),
        reads=["em2", "emones"], writes=["em2"])
    p.emit("vector", lambda v: v.tensor_copy(cnts[:, 0:1], em[:, T - 1:T]),
           reads=["em1"], writes=["cnts1"])
    p.emit("vector", lambda v: v.tensor_copy(cnts[:, 1:2], em[:, 2 * T - 1:2 * T]),
           reads=["em2"], writes=["cnts2"])
    p.emit("gpsimd", lambda g: g.dma_start(
        out=co[:, :].rearrange("k e -> e k"), in_=cnts[:, :],
        allow_slow_non_contiguous=True),
        reads=["cnts1", "cnts2"], writes=["co"], dma=True)
    for tt in range(16):
        def trb(t, tt=tt):
            t.drain()
            p.pe_base = None
            t.transpose(ps_b[:, 0:NEXP], em[0:NEXP, 128 * tt:128 * (tt + 1)], ident[0:NEXP, 0:NEXP])
            return t.transpose(ps_b[:, NEXP:2 * NEXP], em[0:NEXP, T + 128 * tt:T + 128 * (tt + 1)],
                               ident[0:NEXP, 0:NEXP])
        p.emit("tensor", trb, reads=["em1", "em2", "ident"], writes=["ps_b"])
        p.emit("vector", lambda v: v.tensor_copy(lct[:, 0:16], ps_b[:, 0:16]),
               reads=["ps_b"], writes=["lct"])
        p.emit("vector", lambda v, tt=tt: v.tensor_tensor(
            lct[:, 0:8], lct[:, 0:8], m1a[:, NEXP * tt:NEXP * (tt + 1)], ALU.mult),
            reads=["lct", f"m1a{tt}"], writes=["lctm1"])
        p.emit("vector", lambda v, tt=tt: v.tensor_tensor(
            lct[:, 8:16], lct[:, 8:16], m2a[:, NEXP * tt:NEXP * (tt + 1)], ALU.mult),
            reads=["lct", f"m2a{tt}"], writes=["lctm2"])
        p.emit("vector", lambda v: v.reduce_sum(rt[:, 4:5], lct[:, 0:8], axis=mybir.AxisListType.X),
               reads=["lctm1"], writes=["lc1"])
        p.emit("vector", lambda v: v.reduce_sum(rt[:, 5:6], lct[:, 8:16], axis=mybir.AxisListType.X),
               reads=["lctm2"], writes=["lc2"])
        p.emit("vector", lambda v, tt=tt: v.tensor_copy(rt[:, 0:4], gi[:, 4 * tt:4 * (tt + 1)]),
               reads=[f"gi{tt}", f"gi{tt}b", f"gi{tt}c", f"gi{tt}d"], writes=["rt03"])
        p.emit("vector", lambda v: v.memset(rt[:, 6:8], 0.0), writes=["rt67"])
        p.emit("gpsimd", lambda g, tt=tt: g.dma_start(
            out=ro[128 * tt:128 * (tt + 1), :], in_=rt[:, :]),
            reads=["rt03", "lc1", "lc2", "rt67"], writes=[f"ro{tt}"], dma=True)
    p.finalize()
    return nc



def build_moe_expert(nc):
    p = Prog(nc)
    TA = 8 * T
    x3b = nc.dram_tensor("x3ball", [TA, D], BF16, kind="ExternalInput").ap()
    idt = nc.dram_tensor("idt4", [2, 4, CAP], F32, kind="ExternalInput").ap()
    wi = nc.dram_tensor("wi_e", [D, 4 * D], F32, kind="ExternalInput").ap()
    wo = nc.dram_tensor("wo_e", [4 * D, D], F32, kind="ExternalInput").ap()
    eo = nc.dram_tensor("eoutT", [2, D, CAP], F32, kind="ExternalOutput").ap()
    ebuf = nc.dram_tensor("ebuf", [2 * CAP, D], BF16, kind="Internal").ap()

    wis = nc.alloc_sbuf_tensor("wis", [128, 8 * 4 * D], BF16)
    wos = nc.alloc_sbuf_tensor("wos", [128, 32 * D], BF16)
    hb = nc.alloc_sbuf_tensor("hb", [128, 32 * CAP], BF16)
    xg = nc.alloc_sbuf_tensor("xg", [128, 8 * CAP], BF16)
    xrow = nc.alloc_sbuf_tensor("xrow", [128, D], BF16)
    idtl = [nc.alloc_sbuf_tensor(f"idtl{j}", [128, 5], F32) for j in range(2)]
    idg = nc.alloc_sbuf_tensor("idg", [128, 5], F32)
    idgi = nc.alloc_sbuf_tensor("idgi", [128, 5], I32)
    ot = nc.alloc_sbuf_tensor("ot", [128, CAP], F32)

    ps_a = nc.alloc_psum_tensor("ps_a", [128, CAP], F32)
    ps_b = nc.alloc_psum_tensor("ps_b", [128, CAP], F32)

    for kc in range(8):
        p.emit("gpsimd", lambda g, kc=kc: g.dma_start(
            out=wis[:, 4 * D * kc:4 * D * (kc + 1)], in_=wi[128 * kc:128 * (kc + 1), :]),
            writes=[f"wi{kc}"], dma=True)
    for fc in range(32):
        p.emit("gpsimd", lambda g, fc=fc: g.dma_start(
            out=wos[:, D * fc:D * (fc + 1)], in_=wo[128 * fc:128 * (fc + 1), :]),
            writes=[f"wo{fc}"], dma=True)

    for g in range(2):
        p.emit("sync", lambda s, g=g: s.dma_start(
            out=idtl[0][:, :], in_=idt[g, 0].rearrange("(p f) -> p f", p=128)),
            writes=["idtl0"], dma=True)
        p.emit("sync", lambda s, g=g: s.dma_start(
            out=idtl[1][:, :], in_=idt[g, 1].rearrange("(p f) -> p f", p=128)),
            writes=["idtl1"], dma=True)
        p.emit("vector", lambda v: v.tensor_tensor(idg[:, :], idtl[0][:, :], idtl[1][:, :],
                                                   ALU.min),
               reads=["idtl0", "idtl1"], writes=["idg"])
        p.emit("sync", lambda s, g=g: s.dma_start(
            out=idtl[0][:, :], in_=idt[g, 2].rearrange("(p f) -> p f", p=128)),
            writes=["idtl0"], dma=True)
        p.emit("sync", lambda s, g=g: s.dma_start(
            out=idtl[1][:, :], in_=idt[g, 3].rearrange("(p f) -> p f", p=128)),
            writes=["idtl1"], dma=True)
        p.emit("vector", lambda v: v.tensor_tensor(idg[:, :], idg[:, :], idtl[0][:, :], ALU.min),
               reads=["idg", "idtl0"], writes=["idg"])
        p.emit("vector", lambda v: v.tensor_tensor(idg[:, :], idg[:, :], idtl[1][:, :], ALU.min),
               reads=["idg", "idtl1"], writes=["idg"])
        p.emit("vector", lambda v: v.tensor_copy(idgi[:, :], idg[:, :]),
               reads=["idg"], writes=["idgi"])
        for b in range(5):
            p.emit("vector", lambda v: v.memset(xrow[:, :], 0.0), writes=["xrow"])
            p.emit("gpsimd", lambda gg, b=b: gg.indirect_dma_start(
                out=xrow[:, :], out_offset=None,
                in_=x3b[:, :], in_offset=bass.IndirectOffsetOnAxis(idgi[:, b:b + 1], 0),
                bounds_check=TA - 1, oob_is_err=False),
                reads=["idgi", "xrow"], writes=["xrow"], dma=True)
            p.emit("gpsimd", lambda gg, g=g, b=b: gg.dma_start(
                out=ebuf.rearrange("(a s) d -> a s d", s=5)[128 * g:128 * (g + 1), b],
                in_=xrow[:, :]),
                reads=["xrow"], writes=[f"eb_{g}_{b}"], dma=True)

    for g in range(2):
        allsc = [f"eb_{g}_{b}" for b in range(5)]
        for fc in range(8):
            p.emit("sync", lambda s, g=g, fc=fc: s.dma_start(
                out=xg[:, CAP * fc:CAP * (fc + 1)],
                in_=ebuf[CAP * g:CAP * (g + 1), 128 * fc:128 * (fc + 1)], transpose=True),
                reads=allsc, writes=[f"xg{fc}"], dma=True)
        for fc in range(32):
            for nn in range(2):
                n0, n1 = (0, 512) if nn == 0 else (512, CAP)
                pairs = [(wis[:, 4 * D * kc + 128 * fc: 4 * D * kc + 128 * (fc + 1)],
                          xg[:, CAP * kc + n0:CAP * kc + n1]) for kc in range(8)]
                mm_acc(p, ps_a[:, n0:n1], pairs,
                       reads=[f"wi{kc}" for kc in range(8)] + [f"xg{kc}" for kc in range(8)],
                       writes=["ps_a"])
            p.emit("scalar", lambda s, fc=fc: s.activation(
                out=hb[:, CAP * fc:CAP * (fc + 1)], in_=ps_a[:, :], func=AF.Relu),
                reads=["ps_a"], writes=[f"hb{fc}"])
        for ec in range(8):
            for nn in range(2):
                n0, n1 = (0, 512) if nn == 0 else (512, CAP)
                pairs = [(wos[:, D * fc + 128 * ec: D * fc + 128 * (ec + 1)],
                          hb[:, CAP * fc + n0:CAP * fc + n1]) for fc in range(32)]
                mm_acc(p, ps_b[:, n0:n1], pairs,
                       reads=[f"wo{fc}" for fc in range(32)] + [f"hb{fc}" for fc in range(32)],
                       writes=["ps_b"])
            p.emit("vector", lambda v: v.tensor_copy(ot[:, :], ps_b[:, :]),
                   reads=["ps_b"], writes=["ot"])
            p.emit("gpsimd", lambda gg, g=g, ec=ec: gg.dma_start(
                out=eo[g, 128 * ec:128 * (ec + 1), :], in_=ot[:, :]),
                reads=["ot"], writes=[f"eo{g}_{ec}"], dma=True)
    p.finalize()
    return nc



def build_moe_combine(nc, dbg=False):
    p = Prog(nc)
    x3 = nc.dram_tensor("x3", [T, D], F32, kind="ExternalInput").ap()
    ro = nc.dram_tensor("route", [T, 8], F32, kind="ExternalInput").ap()
    ca = nc.dram_tensor("cntall", [8, 16], F32, kind="ExternalInput").ap()
    pm = nc.dram_tensor("Pmat", [8, 8], F32, kind="ExternalInput").ap()
    gm = nc.dram_tensor("Gmat", [8, 8], F32, kind="ExternalInput").ap()
    ea = nc.dram_tensor("eall", [NEXP * 2 * CAP, D], F32, kind="ExternalInput").ap()
    cinp = nc.dram_tensor("cid", [2], F32, kind="ExternalInput").ap()
    out = nc.dram_tensor("out", [T, D], F32, kind="ExternalOutput").ap()
    if dbg:
        slo = nc.dram_tensor("slo", [T, 2], F32, kind="ExternalOutput").ap()
        slo2 = nc.dram_tensor("slo2", [T, 2], I32, kind="ExternalOutput").ap()
        yo = nc.dram_tensor("yo", [T, D], F32, kind="ExternalOutput").ap()
        sdbg = nc.alloc_sbuf_tensor("sdbg", [128, 2], F32)

    rts = nc.alloc_sbuf_tensor("rts", [128, 8], F32)
    iota8 = nc.alloc_sbuf_tensor("iota8", [128, NEXP], F32)
    cnts = nc.alloc_sbuf_tensor("cnts", [8, 16], F32)
    pms = nc.alloc_sbuf_tensor("pms", [8, 16], F32)
    offt = nc.alloc_sbuf_tensor("offt", [8, 16], F32)
    orow = nc.alloc_sbuf_tensor("orow", [1, 16], F32)
    offbc = nc.alloc_sbuf_tensor("offbc", [128, 16], F32)
    onesrow = nc.alloc_sbuf_tensor("onesrow", [1, 128], F32)
    cidr = nc.alloc_sbuf_tensor("cidr", [1, 2], F32)
    iotaP = nc.alloc_sbuf_tensor("iotaP", [8, 1], F32)
    gof = nc.alloc_sbuf_tensor("gof", [128, 1], F32)
    wk1 = nc.alloc_sbuf_tensor("wk1", [128, 12], F32)
    slots = [nc.alloc_sbuf_tensor(f"slot{k}", [128, 1], I32) for k in range(2)]
    rows = [nc.alloc_sbuf_tensor(f"rows{k}", [128, D], F32) for k in range(2)]
    xts = nc.alloc_sbuf_tensor("xts", [128, D], F32)
    yt = nc.alloc_sbuf_tensor("yt", [128, D], F32)
    zt = nc.alloc_sbuf_tensor("zt", [128, D], F32)
    sqscr = nc.alloc_sbuf_tensor("sqscr", [128, D], F32)
    mu = nc.alloc_sbuf_tensor("mu", [128, 4], F32)
    epst = nc.alloc_sbuf_tensor("epst", [128, 1], F32)
    nhalf = nc.alloc_sbuf_tensor("nhalf", [128, 1], F32)
    ot = nc.alloc_sbuf_tensor("ot", [128, D], F32)
    ps_c = nc.alloc_psum_tensor("ps_c", [128, 16], F32)

    p.emit("vector", lambda v: v.memset(onesrow[:, :], 1.0), writes=["onesrow"])
    p.emit("vector", lambda v: v.memset(epst[:, :], 1e-5), writes=["epst"])
    p.emit("vector", lambda v: v.memset(nhalf[:, :], -0.5), writes=["nhalf"])
    p.emit("gpsimd", lambda g: g.iota(iota8[:, :], pattern=[[1, NEXP]], base=0,
                                      channel_multiplier=0,
                                      allow_small_or_imprecise_dtypes=True),
           writes=["iota8"])
    p.emit("sync", lambda s: s.dma_start(out=cnts[:, :], in_=ca[:, :]), writes=["cnts"], dma=True)
    p.emit("sync", lambda s: s.dma_start(out=pms[:, 0:8], in_=pm[:, :]), writes=["pms"], dma=True)
    p.emit("sync", lambda s: s.dma_start(out=pms[:, 8:16], in_=gm[:, :]), writes=["gms"], dma=True)
    p.emit("sync", lambda s: s.dma_start(out=cidr[:, :], in_=cinp[None, :]), writes=["cidr"], dma=True)
    p.emit("gpsimd", lambda g: g.iota(iotaP[:, :], pattern=[[0, 1]], base=0,
                                      channel_multiplier=1,
                                      allow_small_or_imprecise_dtypes=True),
           writes=["iotaP"])
    def offmm(t):
        p.mm(t, ps_c[0:8, 0:16], pms[:, 0:8], cnts[:, :], start=True, stop=True)
        return p.mm(t, ps_c[0:8, 8:16], pms[:, 8:16], cnts[:, 0:8], start=False, stop=True)
    p.emit("tensor", offmm, reads=["pms", "gms", "cnts"], writes=["ps_c"])
    p.emit("vector", lambda v: v.tensor_copy(offt[:, :], ps_c[0:8, 0:16]),
           reads=["ps_c"], writes=["offt"])

    for tile in range(16):
        if tile == 0:
            def cbc8(t):
                return p.mm(t, ps_c[0:8, 0:1], onesrow[0:1, 0:8], cidr[0:1, 0:1],
                            start=True, stop=True)
            p.emit("tensor", cbc8, reads=["cidr", "onesrow"], writes=["ps_c"])
            p.emit("vector", lambda v: v.tensor_copy(wk1[0:8, 1:2], ps_c[0:8, 0:1]),
                   reads=["ps_c"], writes=["cidbc8"])
            p.emit("vector", lambda v: v.tensor_tensor(
                wk1[0:8, 0:1], iotaP[:, :], wk1[0:8, 1:2], ALU.is_equal),
                reads=["iotaP", "cidbc8"], writes=["cmask"])
            def pickmm(t):
                return p.mm(t, ps_c[0:1, 0:16], wk1[0:8, 0:1], offt[:, :],
                            start=True, stop=True)
            p.emit("tensor", pickmm, reads=["offt", "cmask"], writes=["ps_c"])
            p.emit("vector", lambda v: v.tensor_copy(orow[0:1, 0:16], ps_c[0:1, 0:16]),
                   reads=["ps_c"], writes=["orow"])
            def obc(t):
                return p.mm(t, ps_c[:, 0:16], onesrow[:, :], orow[0:1, :], start=True, stop=True)
            p.emit("tensor", obc, reads=["orow", "onesrow"], writes=["ps_c"])
            p.emit("vector", lambda v: v.tensor_copy(offbc[:, :], ps_c[:, 0:16]),
                   reads=["ps_c"], writes=["offbc"])
            def gbc(t):
                return p.mm(t, ps_c[:, 0:1], onesrow[:, :], cidr[0:1, 1:2], start=True, stop=True)
            p.emit("tensor", gbc, reads=["cidr", "onesrow"], writes=["ps_c"])
            p.emit("vector", lambda v: v.tensor_copy(gof[:, :], ps_c[:, 0:1]),
                   reads=["ps_c"], writes=["gof"])
        p.emit("sync", lambda s, tile=tile: s.dma_start(
            out=rts[:, :], in_=ro[128 * tile:128 * (tile + 1), :]),
            writes=["rts"], dma=True)
        p.emit("sync", lambda s, tile=tile: s.dma_start(
            out=xts[:, :], in_=x3[128 * tile:128 * (tile + 1), :]),
            writes=["xts"], dma=True)
        for k in range(2):
            p.emit("vector", lambda v, k=k: v.tensor_single_scalar(
                wk1[:, 0:8], iota8[:, :], rts[:, 2 + k:3 + k], ALU.is_equal),
                reads=["iota8", "rts"], writes=["wm"])
            p.emit("vector", lambda v, k=k: v.tensor_tensor(
                wk1[:, 0:8], wk1[:, 0:8], offbc[:, 8 * k:8 * (k + 1)], ALU.mult),
                reads=["wm", "offbc"], writes=["wmo"])
            p.emit("vector", lambda v: v.reduce_sum(wk1[:, 8:9], wk1[:, 0:8],
                                                    axis=mybir.AxisListType.X),
                   reads=["wmo"], writes=["offpick"])
            p.emit("vector", lambda v, k=k: v.scalar_tensor_tensor(
                wk1[:, 9:10], wk1[:, 8:9], -1.0, rts[:, 4 + k:5 + k], ALU.add, ALU.add),
                reads=["offpick", "rts"], writes=["pos"])
            p.emit("vector", lambda v: v.tensor_single_scalar(
                wk1[:, 11:12], wk1[:, 9:10], float(CAP) - 0.5, ALU.is_gt),
                reads=["pos"], writes=["ov"])
            p.emit("vector", lambda v, k=k: v.scalar_tensor_tensor(
                wk1[:, 9:10], rts[:, 2 + k:3 + k], float(2 * CAP), wk1[:, 9:10],
                ALU.mult, ALU.add),
                reads=["pos", "rts"], writes=["pos"])
            p.emit("vector", lambda v: v.tensor_tensor(
                wk1[:, 9:10], wk1[:, 9:10], gof[:, :], ALU.add),
                reads=["pos", "gof"], writes=["pos"])
            p.emit("vector", lambda v: v.scalar_tensor_tensor(
                wk1[:, 9:10], wk1[:, 11:12], BIGOOB, wk1[:, 9:10], ALU.mult, ALU.add),
                reads=["pos", "ov"], writes=["pos"])
            p.emit("vector", lambda v, k=k: v.tensor_copy(slots[k][:, :], wk1[:, 9:10]),
                   reads=["pos"], writes=[f"slotf{k}"])
            if dbg:
                p.emit("vector", lambda v, k=k: v.tensor_copy(sdbg[:, k:k + 1], wk1[:, 9:10]),
                       reads=["pos"], writes=[f"sdbg{k}"])
        for k in range(2):
            p.emit("vector", lambda v, k=k: v.memset(rows[k][:, :], 0.0),
                   writes=[f"rows{k}"])
            p.emit("gpsimd", lambda gg, k=k, tile=tile: gg.indirect_dma_start(
                out=rows[k][:, :], out_offset=None,
                in_=ea[:, :], in_offset=bass.IndirectOffsetOnAxis(slots[k][:, :], 0),
                bounds_check=NEXP * 2 * CAP - 1, oob_is_err=False),
                reads=[f"slotf{k}", f"rows{k}"], writes=[f"rows{k}"], dma=True)
        p.emit("vector", lambda v: v.tensor_single_scalar(
            yt[:, :], rows[0][:, :], rts[:, 0:1], ALU.mult),
            reads=["rows0", "rts"], writes=["yt"])
        p.emit("vector", lambda v: v.tensor_single_scalar(
            sqscr[:, :], rows[1][:, :], rts[:, 1:2], ALU.mult),
            reads=["rows1", "rts"], writes=["y2"])
        p.emit("vector", lambda v: v.tensor_tensor(yt[:, :], yt[:, :], sqscr[:, :], ALU.add),
               reads=["yt", "y2"], writes=["yt"])
        if dbg:
            p.emit("gpsimd", lambda g, tile=tile: g.dma_start(
                out=slo[128 * tile:128 * (tile + 1), :], in_=sdbg[:, :]),
                reads=["sdbg0", "sdbg1"], writes=[f"slo{tile}"], dma=True)
            p.emit("gpsimd", lambda g, tile=tile: g.dma_start(
                out=slo2[128 * tile:128 * (tile + 1), 0:1], in_=slots[0][:, :]),
                reads=["slotf0"], writes=[f"slo2{tile}"], dma=True)
            p.emit("gpsimd", lambda g, tile=tile: g.dma_start(
                out=slo2[128 * tile:128 * (tile + 1), 1:2], in_=slots[1][:, :]),
                reads=["slotf1"], writes=[f"slo2b{tile}"], dma=True)
            p.emit("gpsimd", lambda g, tile=tile: g.dma_start(
                out=yo[128 * tile:128 * (tile + 1), :], in_=yt[:, :]),
                reads=["yt"], writes=[f"yo{tile}"], dma=True)
        p.emit("vector", lambda v: v.tensor_tensor(zt[:, :], yt[:, :], xts[:, :], ALU.add),
               reads=["yt", "xts"], writes=["ztl"])
        p.emit("vector", lambda v: v.reduce_sum(mu[:, 0:1], zt[:, :], axis=mybir.AxisListType.X),
               reads=["ztl"], writes=["mu0"])
        p.emit("vector", lambda v: v.tensor_scalar_mul(mu[:, 0:1], mu[:, 0:1], 1.0 / D),
               reads=["mu0"], writes=["mu0"])
        p.emit("scalar", lambda s: s.activation(out=sqscr[:, :], in_=zt[:, :], func=AF.Square,
                                                accum_out=mu[:, 1:2]),
               reads=["ztl"], writes=["mu1", "sqscr"])
        p.emit("vector", lambda v: v.tensor_scalar_mul(mu[:, 1:2], mu[:, 1:2], 1.0 / D),
               reads=["mu1"], writes=["mu1"])
        p.emit("vector", lambda v: v.tensor_tensor(mu[:, 2:3], mu[:, 0:1], mu[:, 0:1], ALU.mult),
               reads=["mu0"], writes=["mu2"])
        p.emit("vector", lambda v: v.tensor_tensor(mu[:, 2:3], mu[:, 1:2], mu[:, 2:3], ALU.subtract),
               reads=["mu1", "mu2"], writes=["mu2"])
        p.emit("scalar", lambda s: s.activation(out=mu[:, 2:3], in_=mu[:, 2:3], func=AF.Ln,
                                                bias=epst[:, 0:1]),
               reads=["mu2"], writes=["mu2"])
        p.emit("scalar", lambda s: s.activation(out=mu[:, 3:4], in_=mu[:, 2:3], func=AF.Exp,
                                                scale=nhalf[:, 0:1]),
               reads=["mu2", "nhalf"], writes=["mu3"])
        p.emit("vector", lambda v: v.tensor_scalar(
            ot[:, :], zt[:, :], mu[:, 0:1], mu[:, 3:4], ALU.subtract, ALU.mult),
            reads=["ztl", "mu0", "mu3"], writes=["otw"])
        p.emit("gpsimd", lambda g, tile=tile: g.dma_start(
            out=out[128 * tile:128 * (tile + 1), :], in_=ot[:, :]),
            reads=["otw"], writes=[f"out{tile}"], dma=True)
    p.finalize()
    return nc



BIGID = 1 << 28


def build_moe_idtab(nc):
    p = Prog(nc)
    ro = nc.dram_tensor("route", [T, 8], F32, kind="ExternalInput").ap()
    ca = nc.dram_tensor("cntall", [8, 16], F32, kind="ExternalInput").ap()
    pm = nc.dram_tensor("Pmat", [8, 8], F32, kind="ExternalInput").ap()
    gm = nc.dram_tensor("Gmat", [8, 8], F32, kind="ExternalInput").ap()
    cinp = nc.dram_tensor("cid", [2], F32, kind="ExternalInput").ap()
    it = nc.dram_tensor("idtab", [NEXP * CAP, 1], F32, kind="ExternalOutput").ap()

    rts = nc.alloc_sbuf_tensor("rts", [128, 8], F32)
    iota8 = nc.alloc_sbuf_tensor("iota8", [128, NEXP], F32)
    iotaP = nc.alloc_sbuf_tensor("iotaP", [8, 1], F32)
    cnts = nc.alloc_sbuf_tensor("cnts", [8, 16], F32)
    pms = nc.alloc_sbuf_tensor("pms", [8, 16], F32)
    offt = nc.alloc_sbuf_tensor("offt", [8, 16], F32)
    orow = nc.alloc_sbuf_tensor("orow", [1, 16], F32)
    offbc = nc.alloc_sbuf_tensor("offbc", [128, 16], F32)
    onesrow = nc.alloc_sbuf_tensor("onesrow", [1, 128], F32)
    cidr = nc.alloc_sbuf_tensor("cidr", [1, 2], F32)
    wk1 = nc.alloc_sbuf_tensor("wk1", [128, 12], F32)
    slot = nc.alloc_sbuf_tensor("slot", [128, 2], I32)
    tokid = nc.alloc_sbuf_tensor("tokid", [128, 1], F32)
    cidTbc = nc.alloc_sbuf_tensor("cidTbc", [128, 1], F32)
    bigt = nc.alloc_sbuf_tensor("bigt", [128, 40], F32)
    ps_c = nc.alloc_psum_tensor("ps_c", [128, 16], F32)

    p.emit("vector", lambda v: v.memset(onesrow[:, :], 1.0), writes=["onesrow"])
    p.emit("gpsimd", lambda g: g.iota(iota8[:, :], pattern=[[1, NEXP]], base=0,
                                      channel_multiplier=0,
                                      allow_small_or_imprecise_dtypes=True),
           writes=["iota8"])
    p.emit("gpsimd", lambda g: g.iota(iotaP[:, :], pattern=[[0, 1]], base=0,
                                      channel_multiplier=1,
                                      allow_small_or_imprecise_dtypes=True),
           writes=["iotaP"])
    p.emit("vector", lambda v: v.memset(bigt[:, :], BIGID), writes=["bigt"])
    p.emit("gpsimd", lambda g: g.dma_start(
        out=it[:, :].rearrange("(p f) o -> p (f o)", p=128), in_=bigt[:, :]),
        reads=["bigt"], writes=["it_init"], dma=True)
    p.emit("sync", lambda s: s.dma_start(out=cnts[:, :], in_=ca[:, :]), writes=["cnts"], dma=True)
    p.emit("sync", lambda s: s.dma_start(out=pms[:, 0:8], in_=pm[:, :]), writes=["pms"], dma=True)
    p.emit("sync", lambda s: s.dma_start(out=pms[:, 8:16], in_=gm[:, :]), writes=["gms"], dma=True)
    p.emit("sync", lambda s: s.dma_start(out=cidr[:, :], in_=cinp[None, :]), writes=["cidr"], dma=True)
    def offmm(t):
        p.mm(t, ps_c[0:8, 0:16], pms[:, 0:8], cnts[:, :], start=True, stop=True)
        return p.mm(t, ps_c[0:8, 8:16], pms[:, 8:16], cnts[:, 0:8], start=False, stop=True)
    p.emit("tensor", offmm, reads=["pms", "gms", "cnts"], writes=["ps_c"])
    p.emit("vector", lambda v: v.tensor_copy(offt[:, :], ps_c[0:8, 0:16]),
           reads=["ps_c"], writes=["offt"])
    def cbc8(t):
        return p.mm(t, ps_c[0:8, 0:1], onesrow[0:1, 0:8], cidr[0:1, 0:1], start=True, stop=True)
    p.emit("tensor", cbc8, reads=["cidr", "onesrow"], writes=["ps_c"])
    p.emit("vector", lambda v: v.tensor_copy(wk1[0:8, 1:2], ps_c[0:8, 0:1]),
           reads=["ps_c"], writes=["cidbc8"])
    p.emit("vector", lambda v: v.tensor_tensor(
        wk1[0:8, 0:1], iotaP[:, :], wk1[0:8, 1:2], ALU.is_equal),
        reads=["iotaP", "cidbc8"], writes=["cmask"])
    def pickmm(t):
        return p.mm(t, ps_c[0:1, 0:16], wk1[0:8, 0:1], offt[:, :], start=True, stop=True)
    p.emit("tensor", pickmm, reads=["offt", "cmask"], writes=["ps_c"])
    p.emit("vector", lambda v: v.tensor_copy(orow[0:1, 0:16], ps_c[0:1, 0:16]),
           reads=["ps_c"], writes=["orow"])
    def obc(t):
        return p.mm(t, ps_c[:, 0:16], onesrow[:, :], orow[0:1, :], start=True, stop=True)
    p.emit("tensor", obc, reads=["orow", "onesrow"], writes=["ps_c"])
    p.emit("vector", lambda v: v.tensor_copy(offbc[:, :], ps_c[:, 0:16]),
           reads=["ps_c"], writes=["offbc"])
    def cbc(t):
        return p.mm(t, ps_c[:, 0:1], onesrow[:, :], cidr[0:1, 1:2], start=True, stop=True)
    p.emit("tensor", cbc, reads=["cidr", "onesrow"], writes=["ps_c"])
    p.emit("vector", lambda v: v.tensor_copy(cidTbc[:, :], ps_c[:, 0:1]),
           reads=["ps_c"], writes=["cidTbc"])

    for tile in range(16):
        p.emit("sync", lambda s, tile=tile: s.dma_start(
            out=rts[:, :], in_=ro[128 * tile:128 * (tile + 1), :]),
            writes=["rts"], dma=True)
        p.emit("gpsimd", lambda g, tile=tile: g.iota(
            tokid[:, :], pattern=[[0, 1]], base=128 * tile, channel_multiplier=1,
            allow_small_or_imprecise_dtypes=True),
            writes=["tokid_i"])
        p.emit("vector", lambda v: v.tensor_tensor(
            tokid[:, :], tokid[:, :], cidTbc[:, :], ALU.add),
            reads=["tokid_i", "cidTbc"], writes=["tokid"])
        for k in range(2):
            p.emit("vector", lambda v, k=k: v.tensor_single_scalar(
                wk1[:, 0:8], iota8[:, :], rts[:, 2 + k:3 + k], ALU.is_equal),
                reads=["iota8", "rts"], writes=["wm"])
            p.emit("vector", lambda v, k=k: v.tensor_tensor(
                wk1[:, 0:8], wk1[:, 0:8], offbc[:, 8 * k:8 * (k + 1)], ALU.mult),
                reads=["wm", "offbc"], writes=["wmo"])
            p.emit("vector", lambda v: v.reduce_sum(wk1[:, 8:9], wk1[:, 0:8],
                                                    axis=mybir.AxisListType.X),
                   reads=["wmo"], writes=["offpick"])
            p.emit("vector", lambda v, k=k: v.scalar_tensor_tensor(
                wk1[:, 9:10], wk1[:, 8:9], -1.0, rts[:, 4 + k:5 + k], ALU.add, ALU.add),
                reads=["offpick", "rts"], writes=["pos"])
            p.emit("vector", lambda v: v.tensor_single_scalar(
                wk1[:, 11:12], wk1[:, 9:10], float(CAP) - 0.5, ALU.is_gt),
                reads=["pos"], writes=["ov"])
            p.emit("vector", lambda v, k=k: v.scalar_tensor_tensor(
                wk1[:, 9:10], rts[:, 2 + k:3 + k], float(CAP), wk1[:, 9:10],
                ALU.mult, ALU.add),
                reads=["pos", "rts"], writes=["pos"])
            p.emit("vector", lambda v: v.scalar_tensor_tensor(
                wk1[:, 9:10], wk1[:, 11:12], BIGOOB, wk1[:, 9:10], ALU.mult, ALU.add),
                reads=["pos", "ov"], writes=["pos"])
            p.emit("vector", lambda v, k=k: v.tensor_copy(slot[:, k:k + 1], wk1[:, 9:10]),
                   reads=["pos"], writes=[f"slotk{k}"])
        for k in range(2):
            p.emit("gpsimd", lambda gg, k=k, tile=tile: gg.indirect_dma_start(
                out=it[:, :], out_offset=bass.IndirectOffsetOnAxis(slot[:, k:k + 1], 0),
                in_=tokid[:, 0:1], in_offset=None,
                bounds_check=NEXP * CAP - 1, oob_is_err=False),
                reads=[f"slotk{k}", "tokid", "it_init"], writes=[f"sc_{tile}_{k}"], dma=True)
    p.finalize()
    return nc



def build_quant_sums(nc):
    p = Prog(nc)
    wf = nc.dram_tensor("wfcTs", [128, 4 * D], F32, kind="ExternalInput").ap()
    wp = nc.dram_tensor("wprTs", [512, D], F32, kind="ExternalInput").ap()
    so = nc.dram_tensor("sums", [2], F32, kind="ExternalOutput").ap()
    tiles = [nc.alloc_sbuf_tensor(f"t{j}", [128, 4 * D], F32) for j in range(4)]
    cols = nc.alloc_sbuf_tensor("cols", [128, 8], F32)
    onescol = nc.alloc_sbuf_tensor("onescol", [128, 1], F32)
    red = nc.alloc_sbuf_tensor("red", [1, 8], F32)
    ps = nc.alloc_psum_tensor("ps", [128, 16], F32)
    p.emit("vector", lambda v: v.memset(onescol[:, :], 1.0), writes=["ones"])
    p.emit("sync", lambda s: s.dma_start(out=tiles[0][:, :], in_=wf[:, :]),
           writes=["t0"], dma=True)
    p.emit("vector", lambda v: v.reduce_sum(cols[:, 0:1], tiles[0][:, :],
                                            axis=mybir.AxisListType.X,
                                            apply_absolute_value=True),
           reads=["t0"], writes=["c0"])
    for r in range(4):
        j = r % 3 + 1
        p.emit("sync", lambda s, r=r, j=j: s.dma_start(
            out=tiles[j][:, 0:D], in_=wp[128 * r:128 * (r + 1), :]),
            writes=[f"tp{j}"], dma=True)
        p.emit("vector", lambda v, r=r, j=j: v.reduce_sum(
            cols[:, 1 + r:2 + r], tiles[j][:, 0:D], axis=mybir.AxisListType.X,
            apply_absolute_value=True),
            reads=[f"tp{j}"], writes=[f"c{1 + r}"])
    def rmm(t):
        return p.mm(t, ps[0:1, 0:8], onescol[:, :], cols[:, 0:8], start=True, stop=True)
    p.emit("tensor", rmm, reads=[f"c{i}" for i in range(5)] + ["ones"], writes=["ps"])
    p.emit("vector", lambda v: v.tensor_copy(red[:, :], ps[0:1, 0:8]),
           reads=["ps"], writes=["red"])
    p.emit("vector", lambda v: v.tensor_copy(red[:, 0:1], red[:, 0:1]),
           reads=["red"], writes=["sfc"])
    p.emit("vector", lambda v: v.reduce_sum(red[:, 1:2], red[:, 1:5],
                                            axis=mybir.AxisListType.X),
           reads=["red"], writes=["spr"])
    p.emit("gpsimd", lambda g: g.dma_start(out=so[None, :], in_=red[0:1, 0:2]),
           reads=["sfc", "spr"], writes=["so"], dma=True)
    p.finalize()
    return nc


def build_quant_apply(nc):
    p = Prog(nc)
    wf = nc.dram_tensor("wfcTs", [128, 4 * D], F32, kind="ExternalInput").ap()
    wp = nc.dram_tensor("wprTs", [512, D], F32, kind="ExternalInput").ap()
    sa = nc.dram_tensor("allsums", [8, 2], F32, kind="ExternalInput").ap()
    qf = nc.dram_tensor("wfcq_s", [128, 4 * D], BF16, kind="ExternalOutput").ap()
    qp = nc.dram_tensor("wprq_s", [512, D], BF16, kind="ExternalOutput").ap()
    tiles = [nc.alloc_sbuf_tensor(f"t{j}", [128, 4 * D], F32) for j in range(2)]
    at = [nc.alloc_sbuf_tensor(f"a{j}", [128, 4 * D], F32) for j in range(2)]
    bt = [nc.alloc_sbuf_tensor(f"b{j}", [128, 4 * D], F32) for j in range(2)]
    qt = [nc.alloc_sbuf_tensor(f"q{j}", [128, 4 * D], BF16) for j in range(2)]
    sas = nc.alloc_sbuf_tensor("sas", [8, 2], F32)
    srow = nc.alloc_sbuf_tensor("srow", [1, 4], F32)
    onescol8 = nc.alloc_sbuf_tensor("onescol8", [8, 1], F32)
    onesrow = nc.alloc_sbuf_tensor("onesrow", [1, 128], F32)
    thr = nc.alloc_sbuf_tensor("thr", [128, 4], F32)
    ps = nc.alloc_psum_tensor("ps", [128, 16], F32)
    p.emit("vector", lambda v: v.memset(onescol8[:, :], 1.0), writes=["ones8"])
    p.emit("vector", lambda v: v.memset(onesrow[:, :], 1.0), writes=["onesr"])
    p.emit("sync", lambda s: s.dma_start(out=sas[:, :], in_=sa[:, :]), writes=["sas"], dma=True)
    def smm(t):
        return p.mm(t, ps[0:1, 0:2], onescol8[:, :], sas[:, :], start=True, stop=True)
    p.emit("tensor", smm, reads=["sas", "ones8"], writes=["ps"])
    p.emit("vector", lambda v: v.tensor_copy(srow[:, 0:2], ps[0:1, 0:2]),
           reads=["ps"], writes=["srow"])
    p.emit("vector", lambda v: v.tensor_scalar(
        srow[:, 0:1], srow[:, 0:1], 0.5 / float(D * 4 * D), 5e-7, ALU.mult, ALU.add),
        reads=["srow"], writes=["tfc"])
    p.emit("vector", lambda v: v.tensor_scalar(
        srow[:, 1:2], srow[:, 1:2], 0.5 / float(D * 4 * D), 5e-7, ALU.mult, ALU.add),
        reads=["srow"], writes=["tpr"])
    p.emit("vector", lambda v: v.tensor_scalar_mul(srow[:, 2:3], srow[:, 0:1], -1.0),
           reads=["tfc"], writes=["tfcn"])
    p.emit("vector", lambda v: v.tensor_scalar_mul(srow[:, 3:4], srow[:, 1:2], -1.0),
           reads=["tpr"], writes=["tprn"])
    def bmm(t):
        return p.mm(t, ps[:, 0:4], onesrow[:, :], srow[0:1, 0:4], start=True, stop=True)
    p.emit("tensor", bmm, reads=["tfc", "tpr", "tfcn", "tprn", "onesr"], writes=["ps"])
    p.emit("vector", lambda v: v.tensor_copy(thr[:, :], ps[:, 0:4]),
           reads=["ps"], writes=["thr"])

    def quant_tile(idx, src_ap, dst_ap, ncols, tcol, key):
        j = idx % 2
        p.emit("sync", lambda s: s.dma_start(out=tiles[j][:, 0:ncols], in_=src_ap),
               writes=[f"t{j}"], dma=True)
        p.emit("vector", lambda v: v.tensor_single_scalar(
            at[j][:, 0:ncols], tiles[j][:, 0:ncols], thr[:, tcol:tcol + 1], ALU.is_gt),
            reads=[f"t{j}", "thr"], writes=[f"a{j}"])
        p.emit("gpsimd", lambda g: g.tensor_single_scalar(
            bt[j][:, 0:ncols], tiles[j][:, 0:ncols], thr[:, tcol + 2:tcol + 3], ALU.is_lt),
            reads=[f"t{j}", "thr"], writes=[f"b{j}"])
        p.emit("vector", lambda v: v.tensor_tensor(
            qt[j][:, 0:ncols], at[j][:, 0:ncols], bt[j][:, 0:ncols], ALU.subtract),
            reads=[f"a{j}", f"b{j}"], writes=[f"q{j}"])
        p.emit("scalar", lambda s: s.dma_start(out=dst_ap, in_=qt[j][:, 0:ncols]),
               reads=[f"q{j}"], writes=[key], dma=True)

    quant_tile(0, wf[:, :], qf[:, :], 4 * D, 0, "qf")
    for r in range(4):
        quant_tile(1 + r, wp[128 * r:128 * (r + 1), :], qp[128 * r:128 * (r + 1), :],
                   D, 1, f"qp{r}")
    p.finalize()
    return nc



_BUILD_CACHE = {}


def _get(name, builder, *args):
    if name not in _BUILD_CACHE:
        _BUILD_CACHE[name] = builder(
            bass.Bass("TRN2", target_bir_lowering=False, debug=False), *args)
    return _BUILD_CACHE[name]


def _ct(a):
    return np.ascontiguousarray(a)


def kernel(L0_attn_x, L0_attn_past_kv, L0_attn_W_qkv, L0_attn_gn_weight,
           L0_attn_gn_bias, L0_ffn_W_fc, L0_ffn_ln_w, L0_ffn_ln_b,
           L0_ffn_W_proj, L1_attn_Wq, L1_attn_Wk, L1_attn_Wv,
           L1_attn_lambda_params, L1_attn_g1, L1_attn_g2, L1_attn_g3,
           L1_attn_W1, L1_attn_W2, L1_ffn_w_router, L1_ffn_wi, L1_ffn_wo):
    cores = list(range(8))
    B = 8
    x = np.asarray(L0_attn_x, dtype=np.float32)
    pkv = np.asarray(L0_attn_past_kv, dtype=np.float32)
    Wqkv = np.asarray(L0_attn_W_qkv, dtype=np.float32)
    gnw = np.asarray(L0_attn_gn_weight, dtype=np.float32)
    gnb = np.asarray(L0_attn_gn_bias, dtype=np.float32)
    Wfc = np.asarray(L0_ffn_W_fc, dtype=np.float32)
    lnw = np.asarray(L0_ffn_ln_w, dtype=np.float32)
    lnb = np.asarray(L0_ffn_ln_b, dtype=np.float32)
    Wpr = np.asarray(L0_ffn_W_proj, dtype=np.float32)
    Wq1 = np.asarray(L1_attn_Wq, dtype=np.float32)
    Wk1 = np.asarray(L1_attn_Wk, dtype=np.float32)
    Wv1 = np.asarray(L1_attn_Wv, dtype=np.float32)
    lam = np.asarray(L1_attn_lambda_params, dtype=np.float32)
    g1 = np.asarray(L1_attn_g1, dtype=np.float32)
    g2 = np.asarray(L1_attn_g2, dtype=np.float32)
    g3 = np.asarray(L1_attn_g3, dtype=np.float32)
    W1 = np.asarray(L1_attn_W1, dtype=np.float32)
    W2 = np.asarray(L1_attn_W2, dtype=np.float32)
    wr = np.asarray(L1_ffn_w_router, dtype=np.float32)
    wi = np.asarray(L1_ffn_wi, dtype=np.float32)
    wo = np.asarray(L1_ffn_wo, dtype=np.float32)

    nc = _get("l0qkv", build_l0_qkv)
    r = run_bass_kernel_spmd(nc, [{"xT": _ct(x[b].T), "w": Wqkv} for b in range(B)],
                             core_ids=cores).results
    M, E, G, Dg = l0_consts()
    nc = _get("l0ret", build_l0_ret)
    in2 = [{"q_fm": r[b]["q_fm"], "k_fm": r[b]["k_fm"],
            "v_tm": _ct(r[b]["v_fm"].T), "k_tm": _ct(r[b]["k_fm"].T),
            "pkv": pkv, "cM": M, "cE": E, "cG": G, "cDg": Dg} for b in range(B)]
    r2 = run_bass_kernel_spmd(nc, in2, core_ids=cores).results
    nc = _get("l0gn", build_l0_gn)
    in3 = [{"x": x[b], "attnraw": r2[b]["attnraw"], "gnw": gnw, "gnb": gnb}
           for b in range(B)]
    r3 = run_bass_kernel_spmd(nc, in3, core_ids=cores).results

    WfcT = _ct(Wfc.T)
    WprT = _ct(Wpr.T)
    nc = _get("qsum", build_quant_sums)
    inq = [{"wfcTs": WfcT[128 * b:128 * (b + 1)],
            "wprTs": WprT[512 * b:512 * (b + 1)]} for b in range(B)]
    rqa = run_bass_kernel_spmd(nc, inq, core_ids=cores).results
    allsums = np.stack([rqa[b]["sums"] for b in range(B)])
    nc = _get("qapp", build_quant_apply)
    inq2 = [{"wfcTs": WfcT[128 * b:128 * (b + 1)],
             "wprTs": WprT[512 * b:512 * (b + 1)],
             "allsums": allsums} for b in range(B)]
    rqb = run_bass_kernel_spmd(nc, inq2, core_ids=cores).results
    wfcq = np.concatenate([rqb[b]["wfcq_s"] for b in range(B)], axis=0)
    wprq = np.concatenate([rqb[b]["wprq_s"] for b in range(B)], axis=0)
    nc = _get("l0ffn", build_l0_ffn)
    in4 = [{"x1T": _ct(r3[b]["x1"].T), "wfcT_q": wfcq,
            "wprT_q": wprq, "lnw": lnw, "lnb": lnb} for b in range(B)]
    r4 = run_bass_kernel_spmd(nc, in4, core_ids=cores).results

    cMA = l1_consts()
    nc = _get("l1attn", build_l1_attn)
    in5 = [{"x2T": r4[b]["x2fm"], "wqT": _ct(Wq1.T), "wkT": _ct(Wk1.T),
            "wvT": _ct(Wv1.T), "lam": lam, "g1": g1, "g2": g2, "g3": g3,
            "cMA": cMA} for b in range(B)]
    r5 = run_bass_kernel_spmd(nc, in5, core_ids=cores).results
    nc = _get("l1ffn", build_l1_ffn)
    in6 = [{"a3T": _ct(r5[b]["attn3"].T), "x2T": r4[b]["x2fm"],
            "w1T": _ct(W1.T), "w2T": _ct(W2.T)} for b in range(B)]
    r6 = run_bass_kernel_spmd(nc, in6, core_ids=cores).results

    Pm, Gm = moe_consts()
    x3_tm = np.stack([_ct(r6[b]["x3fm"].T) for b in range(B)])
    nc = _get("moer", build_moe_route)
    in7 = [{"x3T": r6[b]["x3fm"], "x3tm": x3_tm[b], "wr": wr} for b in range(B)]
    r7 = run_bass_kernel_spmd(nc, in7, core_ids=cores).results
    x3ball = np.concatenate([r7[b]["x3b_s"] for b in range(B)], axis=0)
    cnt_all = np.stack([r7[b]["cnt"] for b in range(B)]).reshape(8, 16)
    nc = _get("moei", build_moe_idtab)
    in8 = [{"route": r7[b]["route"], "cntall": cnt_all, "Pmat": Pm, "Gmat": Gm,
            "cid": np.array([b, b * T], np.float32)} for b in range(B)]
    r8 = run_bass_kernel_spmd(nc, in8, core_ids=cores).results
    idtabs = np.stack([r8[b]["idtab"][:, 0] for b in range(B)])
    nc = _get("moee", build_moe_expert)
    in9 = []
    for e in range(8):
        idt4 = np.stack([
            np.stack([idtabs[4 * g + c4, e * CAP:(e + 1) * CAP] for c4 in range(4)])
            for g in range(2)])
        in9.append({"x3ball": x3ball, "idt4": _ct(idt4.astype(np.float32)),
                    "wi_e": _ct(wi[e]), "wo_e": _ct(wo[e])})
    r9 = run_bass_kernel_spmd(nc, in9, core_ids=cores).results
    eall = np.stack([np.transpose(r9[e]["eoutT"], (0, 2, 1)) for e in range(8)])
    eall = _ct(eall.reshape(NEXP * 2 * CAP, D))
    nc = _get("moec", build_moe_combine)
    in10 = [{"x3": _ct(x3_tm[b]), "route": r7[b]["route"], "cntall": cnt_all,
             "Pmat": Pm, "Gmat": Gm, "eall": eall,
             "cid": np.array([b, CAP * (b // 4)], np.float32)} for b in range(B)]
    r10 = run_bass_kernel_spmd(nc, in10, core_ids=cores).results

    return np.stack([r10[b]["out"] for b in range(B)])
